# version 18
# baseline (speedup 1.0000x reference)
import math
import numpy as np

import concourse.bass as bass
import concourse.mybir as mybir
from concourse.bass_utils import run_bass_kernel_spmd

F32 = mybir.dt.float32
BF16 = mybir.dt.bfloat16
AF = mybir.ActivationFunctionType
ALU = mybir.AluOpType

BASE = 10000.0


def make_cfg(D=1024, LPC=2048, HID=4096):
    assert D % 128 == 0 and LPC % 512 == 0 and HID % 1024 == 0
    return dict(
        D=D, LPC=LPC, HID=HID,
        NH=D // 64, NHP=D // 128, NFC=D // 128,
        NTB=LPC // 512, NKC=4 * LPC // 128,
        NHC=HID // 128, NOC=D // 128, L=4 * LPC,
    )


def build_program(cfg, detect_races=True, dbg=False, gelu_copy=False):
    D, LPC, HID = cfg["D"], cfg["LPC"], cfg["HID"]
    NFC, NHP, NTB, NKC = cfg["NFC"], cfg["NHP"], cfg["NTB"], cfg["NKC"]
    NHC, NOC = cfg["NHC"], cfg["NOC"]
    from contextlib import ExitStack

    nc = bass.Bass("TRN2", target_bir_lowering=False,
                   detect_race_conditions=detect_races)

    dt = nc.dram_tensor
    xT = dt("xT", [D, LPC], F32, kind="ExternalInput")
    wqT = dt("wqT", [D, D], F32, kind="ExternalInput")
    wkT = dt("wkT", [D, D], F32, kind="ExternalInput")
    wvT = dt("wvT", [D, D], F32, kind="ExternalInput")
    woT = dt("woT", [D, D], F32, kind="ExternalInput")
    w1T = dt("w1T", [D, HID], F32, kind="ExternalInput")
    w1n = dt("w1n", [HID, D], F32, kind="ExternalInput")
    w2T = dt("w2T", [HID, D], F32, kind="ExternalInput")
    w2n = dt("w2n", [D, HID], F32, kind="ExternalInput")
    b1c = dt("b1c", [128, NHC], F32, kind="ExternalInput")
    b2c = dt("b2c", [128, NOC], F32, kind="ExternalInput")
    CCd = dt("CC", [128, LPC], F32, kind="ExternalInput")
    SSd = dt("SS", [128, LPC], F32, kind="ExternalInput")
    Pmd = dt("Pm", [128, 128], F32, kind="ExternalInput")
    yT = dt("yT", [D, LPC], F32, kind="ExternalOutput")

    dk = dict(kind="ExternalOutput") if dbg else {}
    agk_i = dt("agk_i", [D, LPC], BF16)
    agk_o = dt("agk_o", [4 * D, LPC], BF16)
    agv_i = dt("agv_i", [LPC, D], BF16)
    agv_o = dt("agv_o", [4 * LPC, D], BF16)
    sgn1 = dt("sgn1", [D, HID], BF16, **dk)
    sgn2 = dt("sgn2", [HID, D], BF16, **dk)
    atu_d = dt("atu_d", [D, LPC], BF16, **dk)
    if dbg:
        dbgK = dt("dbgK", [D, LPC], BF16, kind="ExternalOutput")
        dbgV = dt("dbgV", [LPC, D], BF16, kind="ExternalOutput")
        dbgA = dt("dbgA", [D, LPC], F32, kind="ExternalOutput")
        dbgB = dt("dbgB", [D, LPC], F32, kind="ExternalOutput")
        dbgE = dt("dbgE", [D, LPC], F32, kind="ExternalOutput")
        dbgC = dt("dbgC", [D, LPC], BF16, kind="ExternalOutput")
        dbgG = dt("dbgG", [HID, LPC], BF16, kind="ExternalOutput")
        dbgW = dt("dbgW", [128, 2 * (NHC + NOC)], F32, kind="ExternalOutput")
        dbgP = dt("dbgP", [128, 512], BF16, kind="ExternalOutput")
        dbgW2 = dt("dbgW2", [128, 2 * (NHC + NOC)], F32, kind="ExternalOutput")

    ctx = ExitStack()

    sb = nc.alloc_sbuf_tensor
    qrope = sb("qrope", [128, NHP * LPC], BF16)
    lns = sb("lns", [128, NFC * 512], BF16)
    xq = sb("xq", [128, NFC * 512], F32)
    guard1 = sb("guard1", [128, 256], F32)
    a2f = sb("a2f", [128, NFC * 512], F32)
    guard2 = sb("guard2", [128, 256], F32)
    wbuf = [sb("wbuf0", [128, NFC * D], BF16), sb("wbuf1", [128, NFC * D], BF16)]
    pt = sb("pt", [128, 3072], BF16)
    atu_t = sb("atu_t", [128, 512], BF16)
    den_s = sb("den_s", [128, 512], F32)
    recin = sb("recin", [128, 512], F32)
    recb = sb("recb", [128, 512], F32)
    rows = sb("rows", [1, 2048], F32)
    onesD = sb("onesD", [128, 1], F32)
    onesb = sb("onesb", [128, 1], BF16)
    onesr = sb("onesr", [1, 64], F32)
    a1c = sb("a1c", [128, NHC + 2], F32)
    a2c = sb("a2c", [128, NOC + 2], F32)
    b1s = sb("b1s", [128, NHC + 2], F32)
    b2s = sb("b2s", [128, NOC + 2], F32)
    t1s = sb("t1s", [128, 512], F32)
    t2s = sb("t2s", [128, 512], F32)
    phb = sb("phb", [128, max(NHC * 256, NFC * 512)], BF16)
    guard3 = sb("guard3", [128, 256], BF16)
    s1b = [sb("s1b%d" % i, [128, NFC * 128], BF16) for i in range(4)]
    s2b = [sb("s2b%d" % i, [128, D], BF16) for i in range(4)]

    r1 = ExitStack()
    CCs = r1.enter_context(nc.sbuf_tensor("CCs", [128, LPC], F32))
    SSs = r1.enter_context(nc.sbuf_tensor("SSs", [128, LPC], F32))
    Pms = r1.enter_context(nc.sbuf_tensor("Pms", [128, 128], F32))
    wst = [r1.enter_context(nc.sbuf_tensor("wst0", [128, 1024], F32)),
           r1.enter_context(nc.sbuf_tensor("wst1", [128, 1024], F32))]
    qpre = [r1.enter_context(nc.sbuf_tensor("qpre%d" % i, [128, 512], F32))
            for i in range(3)]
    sgs = [r1.enter_context(nc.sbuf_tensor("sgs0", [128, 512], BF16)),
           r1.enter_context(nc.sbuf_tensor("sgs1", [128, 512], BF16))]
    krt = [r1.enter_context(nc.sbuf_tensor("krt0", [128, 512], BF16)),
           r1.enter_context(nc.sbuf_tensor("krt1", [128, 512], BF16))]
    vlc = [r1.enter_context(nc.sbuf_tensor("vlc0", [128, D], BF16)),
           r1.enter_context(nc.sbuf_tensor("vlc1", [128, D], BF16))]

    ps = nc.alloc_psum_tensor
    ps_s = ps("ps_s", [128, 3072], F32)
    ps_o = ps("ps_o", [128, 512], F32)
    ps_d = ps("ps_d", [128, 512], F32)
    ps_g1 = ps_s[:, 0:512]
    ps_g2 = ps_s[:, 512:1024]
    bcA = ps_s[:, 2048:2560]
    bcB = ps_s[:, 2560:3072]
    sw_slots = [bcA, bcB]

    ENGS = ("pe", "act", "dve", "pool", "sync")
    steps = {k: [] for k in ENGS}
    sems, ctr = {}, {}
    for name in ("sc", "sw0", "sw1", "sx", "sao", "skt", "svt",
                 "skt1", "skt2", "skt3", "svt1", "svt2", "svt3",
                 "s10", "s11", "s12", "s13", "s20", "s21", "s22", "s23",
                 "kr0", "kr1", "vl0", "vl1", "sg0", "sg1", "swc0", "swc1",
                 "scc", "pe", "act", "dve", "sat", "sout"):
        sems[name] = ctx.enter_context(nc.semaphore(name))
        ctr[name] = 0

    def wait(eng, sem, val):
        if val is None or val <= 0:
            return
        steps[eng].append(lambda e, s=sems[sem], v=val: e.wait_ge(s, v))

    def op(eng, fn, inc=None, n=1):
        if inc is None:
            steps[eng].append(fn)
            return None
        ctr[inc] += n
        steps[eng].append(lambda e, fn=fn, s=sems[inc], n=n: fn(e).then_inc(s, n))
        return ctr[inc]

    def mm(out, lhsT, rhs, start, stop, tp=None, inc=None, sgc=False):
        return op("pe", lambda e, o=out, l=lhsT, r=rhs, s=start, t=stop, p=tp,
                  g=sgc: e.matmul(o, l, r, start=s, stop=t, tile_position=p,
                                  skip_group_check=g), inc=inc)

    def dma(eng, dst, src, inc):
        return op(eng, lambda e, d=dst, s=src: e.dma_start(out=d, in_=s), inc=inc, n=16)

    dma("sync", CCs[:, :], CCd[:, :], "sc")
    dma("sync", SSs[:, :], SSd[:, :], "sc")
    dma("sync", Pms[:, :], Pmd[:, :], "sc")
    dma("sync", b1s[:, 0:NHC], b1c[:, :], "sc")
    v_consts = dma("sync", b2s[:, 0:NOC], b2c[:, :], "sc")
    for _eng in ("pe", "act", "dve"):
        wait(_eng, "sc", v_consts)

    op("dve", lambda e: e.memset(a1c[:, :], 0.0))
    op("dve", lambda e: e.memset(a2c[:, :], 0.0))
    op("dve", lambda e: e.memset(onesD[:, :], 1.0 / D))
    op("dve", lambda e: e.memset(onesb[:, :], 1.0))
    op("dve", lambda e: e.memset(onesr[:, :], 1.0))
    op("dve", lambda e: e.memset(ps_d[:, :], 0.0))
    op("dve", lambda e: e.memset(ps_s[:, :], 0.0))
    v_init = op("dve", lambda e: e.memset(ps_o[:, :], 0.0), inc="dve")

    def layer_norm_stats(src, bc1, bc2, pre_pe=None, pre_dve=None, pre_act=None,
                         pre_sd=None):
        if pre_sd:
            wait("pe", pre_sd[0], pre_sd[1])
        wait("pe", "pe", pre_pe)
        wait("pe", "act", pre_act)
        wait("pe", "dve", pre_dve)
        for fc in range(NFC):
            mm(ps_d[0:1, :], onesD[:, :], src[:, fc * 512:(fc + 1) * 512],
               fc == 0, fc == NFC - 1, tp=(0, 0), sgc=True)
        msqv = {}
        if pre_sd:
            wait("dve", pre_sd[0], pre_sd[1])
        wait("dve", "act", pre_act)
        wait("dve", "dve", pre_dve)
        for fc in range(NFC):
            tscr = t1s if fc % 2 == 0 else t2s
            if fc >= 2:
                wait("dve", "pe", msqv[fc - 2])
            sqv = op("dve", lambda e, t=tscr, s=src, f=fc:
                     e.tensor_tensor(t[:, :], s[:, f * 512:(f + 1) * 512],
                                     s[:, f * 512:(f + 1) * 512], ALU.mult),
                     inc="dve")
            wait("pe", "dve", sqv)
            msqv[fc] = mm(ps_d[64:65, :], onesD[:, :], tscr[:, :],
                          fc == 0, fc == NFC - 1, tp=(0, 64), inc="pe", sgc=True)
        wait("dve", "pe", msqv[NFC - 1])
        op("dve", lambda e: e.tensor_copy(rows[0:1, 0:512], ps_d[0:1, :]))
        op("dve", lambda e: e.tensor_copy(rows[0:1, 1024:1536], ps_d[64:65, :]))
        op("dve", lambda e: e.tensor_tensor(rows[0:1, 1536:2048], rows[0:1, 0:512],
                                            rows[0:1, 0:512], ALU.mult))
        op("dve", lambda e: e.tensor_tensor(
            rows[0:1, 1024:1536], rows[0:1, 1024:1536], rows[0:1, 1536:2048],
            ALU.subtract))
        dv = op("dve", lambda e: e.tensor_scalar_add(
            rows[0:1, 1024:1536], rows[0:1, 1024:1536], 1e-5), inc="dve")
        wait("act", "dve", dv)
        op("act", lambda e: e.activation(rows[0:1, 1024:1536], rows[0:1, 1024:1536],
                                         AF.Ln))
        av = op("act", lambda e: e.activation(rows[0:1, 512:1024],
                                              rows[0:1, 1024:1536],
                                              AF.Exp, scale=-0.5), inc="act")
        wait("pe", "act", av)
        mm(bc1[0:64, :], onesr[:, :], rows[0:1, 0:512], True, True, tp=(0, 0))
        mm(bc1[64:128, :], onesr[:, :], rows[0:1, 0:512], True, True, tp=(0, 64))
        mm(bc2[0:64, :], onesr[:, :], rows[0:1, 512:1024], True, True, tp=(0, 0))
        return mm(bc2[64:128, :], onesr[:, :], rows[0:1, 512:1024], True, True,
                  tp=(0, 64), inc="pe")

    wst_war = [None, None]
    wbuf_war = [None, None]

    def load_weight(wdram, par):
        wait("pool", "pe", wbuf_war[par])
        return dma("pool",
                   wbuf[par][:, :].rearrange("p (c n) -> p c n", n=D),
                   wdram[:, :].rearrange("(c p) n -> p c n", p=128),
                   "swc%d" % par)

    widx = 0
    kv_stores = 0
    evac_war3 = [None, None, None]
    evac_war4 = [None, None, None, None]
    slot_t2 = [None, None]
    qp_rd_pe = [None, None, None]
    qp_rd_dve = [None, None, None]
    kst = [None, None]
    vst = [None, None]
    acc_war = None

    prev_bcv = None
    for tb in range(NTB):
        if tb > 0:
            wait("sync", "pe", prev_bcv)
            wait("sync", "dve", lns_ready)
        x_sd = dma("sync", xq[:, :].rearrange("p (c t) -> p c t", t=512),
                   xT[:, tb * 512:(tb + 1) * 512]
                   .rearrange("(c p) t -> p c t", p=128), "sx")
        bcv = layer_norm_stats(xq, bcA, bcB, pre_sd=("sx", x_sd),
                               pre_dve=max(v_init if tb == 0 else 0,
                                           acc_war or 0) or None)
        lastap = None
        wait("dve", "pe", bcv)
        for fc in range(NFC):
            op("dve", lambda e, f=fc: e.tensor_tensor(
                t1s[:, :], xq[:, f * 512:(f + 1) * 512], bcA, ALU.subtract))
            lastap = op("dve", lambda e, f=fc: e.tensor_tensor(
                lns[:, f * 512:(f + 1) * 512], t1s[:, :], bcB, ALU.mult),
                inc="dve")
        lns_ready = lastap
        prev_bcv = bcv

        for (wdram, is_q) in ((wqT, True), (wkT, False)):
            par = widx % 2
            widx += 1
            castv = load_weight(wdram, par)
            wbuf_war[par] = None
            prjs = {}

            def emit_swap(oc, tb=tb):
                qp = qpre[oc % 3]
                slot = sw_slots[oc % 2]
                wait("pe", "act", evac_war3[oc % 3])
                wait("pe", "dve", slot_t2[oc % 2])
                swv = mm(slot, Pms[:, :], qp[:, :], True, True, inc="pe")
                qp_rd_pe[oc % 3] = swv
                wait("dve", "act", evac_war3[oc % 3])
                dv1 = op("dve", lambda e, q=qp, t=tb: e.tensor_tensor(
                    t1s[:, :], q[:, :], CCs[:, t * 512:(t + 1) * 512], ALU.mult),
                    inc="dve")
                qp_rd_dve[oc % 3] = dv1
                wait("dve", "pe", swv)
                t2v = op("dve", lambda e, s=slot, t=tb: e.tensor_tensor(
                    t2s[:, :], s, SSs[:, t * 512:(t + 1) * 512], ALU.mult),
                    inc="dve")
                slot_t2[oc % 2] = t2v
                if is_q:
                    dst = qrope[:, oc * LPC + tb * 512: oc * LPC + (tb + 1) * 512]
                    op("dve", lambda e, d=dst: e.tensor_tensor(
                        d, t1s[:, :], t2s[:, :], ALU.add))
                else:
                    kb = krt[oc % 2]
                    wait("dve", "kr%d" % (oc % 2), kst[oc % 2])
                    rv = op("dve", lambda e, d=kb: e.tensor_tensor(
                        d[:, :], t1s[:, :], t2s[:, :], ALU.add), inc="dve")
                    wait("pool", "dve", rv)
                    kst[oc % 2] = dma(
                        "pool",
                        agk_i[oc * 128:(oc + 1) * 128, tb * 512:(tb + 1) * 512],
                        kb[:, :], "kr%d" % (oc % 2))

            for oc in range(NHP):
                acc = ps_s[:, (oc % 4) * 512:(oc % 4) * 512 + 512]
                if oc == 0:
                    wait("pe", "swc%d" % par, castv)
                    wait("pe", "dve", lns_ready)
                wait("pe", "act", evac_war4[oc % 4])
                prj = None
                for fc in range(NFC):
                    prj = mm(acc,
                             wbuf[par][:, fc * D + oc * 128: fc * D + (oc + 1) * 128],
                             lns[:, fc * 512:(fc + 1) * 512],
                             fc == 0, fc == NFC - 1,
                             inc="pe" if fc == NFC - 1 else None)
                prjs[oc] = prj
                wbuf_war[par] = prj
                qp = qpre[oc % 3]
                wait("act", "pe", prj)
                wait("act", "pe", qp_rd_pe[oc % 3])
                wait("act", "dve", qp_rd_dve[oc % 3])
                ev = op("act", lambda e, q=qp, a=acc: e.copy(q[:, :], a[:, :]),
                        inc="act")
                evac_war3[oc % 3] = ev
                evac_war4[oc % 4] = ev
                if oc >= 1:
                    emit_swap(oc - 1)
            emit_swap(NHP - 1)

        par = widx % 2
        widx += 1
        castv = load_weight(wvT, par)
        wbuf_war[par] = None
        for tc in range(4):
            vb = vlc[tc % 2]
            prj = None
            if tc == 0:
                wait("pe", "swc%d" % par, castv)
                wait("pe", "dve", lns_ready)
            wait("pe", "dve", acc_war)
            vw = min(512, D)
            if tc == 0:
                wait("pe", "act", max(v for v in evac_war4 if v) if any(evac_war4) else None)
            for oh in range(max(1, D // 512)):
                accv = ps_g1 if oh % 2 == 0 else ps_g2
                for fc in range(NFC):
                    prj = mm(accv[:, 0:vw],
                             lns[:, fc * 512 + tc * 128: fc * 512 + (tc + 1) * 128],
                             wbuf[par][:, fc * D + oh * vw: fc * D + (oh + 1) * vw],
                             fc == 0, fc == NFC - 1,
                             inc="pe" if fc == NFC - 1 else None)
            wbuf_war[par] = prj
            wait("dve", "pe", prj)
            wait("dve", "vl%d" % (tc % 2), vst[tc % 2])
            for oh in range(max(1, D // 512)):
                accv = ps_g1 if oh % 2 == 0 else ps_g2
                vv = op("dve", lambda e, v=vb, a=accv, o=oh, vw=vw: e.tensor_copy(
                    v[:, o * vw:(o + 1) * vw], a[:, 0:vw]), inc="dve")
            acc_war = vv
            wait("pool", "dve", vv)
            vst[tc % 2] = dma(
                "pool", agv_i[(tb * 4 + tc) * 128:(tb * 4 + tc + 1) * 128, :],
                vb[:, :], "vl%d" % (tc % 2))

    wo_par = widx % 2
    widx += 1
    wo_cast = load_weight(woT, wo_par)
    wbuf_war[wo_par] = None

    r1_marks = dict(ctr)
    sgstage = [xq[:, 0:1024], xq[:, 1024:2048]]
    sgout = [s2b[2][:, 0:512], s2b[3][:, 0:512]]
    wait("sync", "pe", ctr["pe"])
    wait("sync", "dve", ctr["dve"])
    sg_war = [None, None]
    i = 0
    sg_hist = {}
    for (src_d, dst_d, RP, CT) in ((w1T, sgn1, NFC, HID), (w2T, sgn2, NHC, D)):
        CJ, CW = max(1, CT // 512), min(512, CT)
        for rp in range(RP):
            for j in range(CJ):
                k = i % 2
                wait("sync", "act", sg_hist.get(i - 2))
                ld = dma("sync", sgstage[k][:, 0:CW],
                         src_d[rp * 128:(rp + 1) * 128, j * CW:(j + 1) * CW],
                         "sw%d" % k)
                wait("act", "sw%d" % k, ld)
                wait("act", "sg%d" % k, sg_war[k])
                sv = op("act", lambda e, k=k, CW=CW: e.sign(sgout[k][:, 0:CW],
                                                            sgstage[k][:, 0:CW]),
                        inc="act")
                sg_hist[i] = sv
                wait("pool", "act", sv)
                sg_war[k] = dma("pool",
                                dst_d[rp * 128:(rp + 1) * 128, j * CW:(j + 1) * CW],
                                sgout[k][:, 0:CW], "sg%d" % k)
                i += 1
    wait("sync", "act", ctr["act"])
    astage = [xq[:, 2048:3072], xq[:, 3072:4096]]
    ab_hist = {}
    ai = 0
    prev_abs = None
    for (src_d, col_t, NCH, DEN) in ((w1n, a1c, NHC, D), (w2n, a2c, NOC, HID)):
        pieces = max(1, DEN // 1024)
        pw = min(DEN, 1024)
        for ch in range(NCH):
            for pz in range(pieces):
                k = ai % 2
                wait("sync", "act", ab_hist.get(ai - 2))
                ldv = dma("sync", astage[k][:, 0:pw],
                          src_d[ch * 128:(ch + 1) * 128, pz * pw:(pz + 1) * pw],
                          "sw%d" % k)
                wait("act", "sw%d" % k, ldv)
                dst_col = col_t[:, NCH:NCH + 1] if pieces > 1 else col_t[:, ch:ch + 1]
                prev_abs = op("act", lambda e, pw=pw, dc=dst_col, k=k: e.activation(
                    s1b[k][:, 0:pw], astage[k][:, 0:pw], AF.Abs, accum_out=dc),
                    inc="act")
                ab_hist[ai] = prev_abs
                ai += 1
                if pieces > 1:
                    wait("dve", "act", prev_abs)
                    if pz == 0:
                        op("dve", lambda e, ct=col_t, c=ch, n=NCH: e.tensor_copy(
                            ct[:, c:c + 1], ct[:, n:n + 1]))
                    else:
                        op("dve", lambda e, ct=col_t, c=ch, n=NCH: e.tensor_tensor(
                            ct[:, c:c + 1], ct[:, c:c + 1], ct[:, n:n + 1], ALU.add))
                    prev_dve = op("dve", lambda e, ct=col_t, c=ch: e.tensor_copy(
                        t2s[:, 0:1], ct[:, c:c + 1]), inc="dve")
                    wait("act", "dve", prev_dve)
        wait("dve", "act", prev_abs)
        op("dve", lambda e, ct=col_t, n=NCH, d=DEN: e.tensor_scalar_mul(
            ct[:, 0:n], ct[:, 0:n], 1.0 / d))
    alpha_act_mark = ctr["act"]
    alpha_done = op("dve", lambda e: e.tensor_copy(t1s[:, 0:1], a1c[:, 0:1]),
                    inc="dve")

    if dbg:
        for _s in ("kr0", "kr1", "vl0", "vl1"):
            wait("pool", _s, ctr[_s])
        dma("pool", dbgK[:, :], agk_i[:, :], "sg0")
        dma("pool", dbgV[:, :], agv_i[:, :], "sg0")
    rgs = [[0, 1, 2, 3], [4, 5, 6, 7]]
    for _s in ("kr0", "kr1"):
        wait("pool", _s, ctr[_s])
    op("pool", lambda e: e.collective_compute(
        "AllGather", ALU.bypass, replica_groups=rgs,
        ins=[agk_i.ap().opt()], outs=[agk_o.ap().opt()]), inc="scc", n=1)
    for _s in ("vl0", "vl1"):
        wait("pool", _s, ctr[_s])
    op("pool", lambda e: e.collective_compute(
        "AllGather", ALU.bypass, replica_groups=rgs,
        ins=[agv_i.ap().opt()], outs=[agv_o.ap().opt()]), inc="scc", n=1)

    if dbg:
        wait("sync", "dve", alpha_done)
        dma("sync", dbgW[:, 0:NHC], a1c[:, 0:NHC], "sat")
        dma("sync", dbgW[:, NHC:2 * NHC], b1s[:, 0:NHC], "sat")
        dma("sync", dbgW[:, 2 * NHC:2 * NHC + NOC], a2c[:, 0:NOC], "sat")
        dma("sync", dbgW[:, 2 * NHC + NOC:], b2s[:, 0:NOC], "sat")
    r1.close()
    ktb = sb("ktb", [128, 4 * LPC], BF16)
    vtb = sb("vtb", [128, 4 * LPC], BF16)

    exp_free = {0: None, 1: None, 2: None}
    pt_free = {0: None, 1: None, 2: None}
    attn_last_pe = None
    sat_last = None
    po_war = None
    pd_war = None

    RCH = NKC // 4
    ktsem = ["skt", "skt1", "skt2", "skt3"]
    vtsem = ["svt", "svt1", "svt2", "svt3"]
    kt_rv = [None] * 4
    vt_rv = [None] * 4

    def load_kv(hp, gate_scv=None, gate_pvs=None):
        for r in range(4):
            if gate_scv is not None:
                wait("sync", "pe", gate_scv[(r + 1) * RCH - 1])
            kt_rv[r] = dma("sync", ktb[:, r * LPC:(r + 1) * LPC],
                           agk_o[r * D + hp * 128: r * D + (hp + 1) * 128, :],
                           ktsem[r])
            if gate_pvs is not None:
                wait("sync", "pe", gate_pvs[(r + 1) * RCH - 1])
            for q4 in range(LPC // 512):
                base = (r * (LPC // 128) + q4 * 4) * 128
                vt_rv[r] = dma(
                    "sync",
                    vtb[:, base:base + 512].rearrange("p (c d) -> p c d", d=128),
                    agv_o[r * LPC + q4 * 512: r * LPC + (q4 + 1) * 512,
                          hp * 128:(hp + 1) * 128]
                    .rearrange("(c p) d -> p c d", p=128), vtsem[r])

    for hp in range(NHP):
        if hp == 0:
            wait("sync", "scc", 2)
            for _s in ("kr0", "kr1", "vl0", "vl1"):
                wait("sync", _s, r1_marks[_s])
            wait("sync", "dve", r1_marks["dve"])
            wait("sync", "act", r1_marks["act"])
            wait("sync", "pe", r1_marks["pe"])
            load_kv(0)

        for qb in range(NTB):
            qsl = slice(hp * LPC + qb * 512, hp * LPC + (qb + 1) * 512)
            denv = None
            scv = {}
            pvs = {}
            LKA = 2
            for ci in range(NKC + LKA):
                if ci < NKC:
                    c = ci
                    w = c % 3
                    wait("pe", "act", exp_free[w])
                    if qb == 0 and c % RCH == 0:
                        wait("pe", ktsem[c // RCH], kt_rv[c // RCH])
                    mm(ps_s[:, w * 1024: w * 1024 + 512],
                       ktb[0:64, c * 128:(c + 1) * 128], qrope[0:64, qsl],
                       True, True, tp=(0, 0))
                    scv[c] = mm(ps_s[:, w * 1024 + 512: w * 1024 + 1024],
                                ktb[64:128, c * 128:(c + 1) * 128],
                                qrope[64:128, qsl],
                                True, True, tp=(64, 0), inc="pe")
                if ci >= LKA:
                    c = ci - LKA
                    w = c % 3
                    wait("act", "pe", scv[c])
                    wait("act", "pe", pt_free[w])
                    expv = op("act", lambda e, w=w: e.activation(
                        pt[:, w * 1024:(w + 1) * 1024],
                        ps_s[:, w * 1024:(w + 1) * 1024],
                        AF.Exp, scale=0.125), inc="act")
                    exp_free[w] = expv
                    wait("pe", "act", expv)
                    if c == 0:
                        wait("pe", "dve", po_war)
                    if qb == 0 and c % RCH == 0:
                        wait("pe", vtsem[c // RCH], vt_rv[c // RCH])
                    mm(ps_o[0:64, :], vtb[:, c * 128: c * 128 + 64],
                       pt[:, w * 1024: w * 1024 + 512],
                       c == 0, c == NKC - 1, tp=(0, 0), sgc=True)
                    pvv = mm(ps_o[64:128, :], vtb[:, c * 128 + 64: c * 128 + 128],
                             pt[:, w * 1024 + 512: w * 1024 + 1024],
                             c == 0, c == NKC - 1, tp=(0, 64), inc="pe", sgc=True)
                    pt_free[w] = pvv
                    pvs[c] = pvv
                    if c % 2 == 1:
                        g = c // 2
                        wp = (c - 1) % 3
                        if g == 0:
                            wait("pe", "dve", pd_war)
                        st = (g == 0)
                        sp = (g == NKC // 2 - 1)
                        mm(ps_d[0:1, :], onesb[:, :],
                           pt[:, wp * 1024: wp * 1024 + 512], st, sp, tp=(0, 0),
                           sgc=True)
                        mm(ps_d[32:33, :], onesb[:, :],
                           pt[:, wp * 1024 + 512: wp * 1024 + 1024], st, sp,
                           tp=(0, 32), sgc=True)
                        mm(ps_d[64:65, :], onesb[:, :],
                           pt[:, w * 1024: w * 1024 + 512], st, sp,
                           tp=(0, 64), sgc=True)
                        denv = mm(ps_d[96:97, :], onesb[:, :],
                                  pt[:, w * 1024 + 512: w * 1024 + 1024], st, sp,
                                  tp=(0, 96), inc="pe", sgc=True)
                        pt_free[wp] = denv
                        pt_free[w] = denv
            attn_last_pe = ctr["pe"]
            if qb == NTB - 1 and hp + 1 < NHP:
                load_kv(hp + 1, gate_scv=scv, gate_pvs=pvs)
            wait("dve", "pe", denv)
            pd_war = op("dve", lambda e: e.tensor_copy(den_s[:, :], ps_d[:, :]),
                        inc="dve")
            op("dve", lambda e: e.tensor_copy(t1s[0:64, :], den_s[64:128, :]))
            op("dve", lambda e: e.tensor_tensor(
                recin[0:64, :], den_s[0:64, :], t1s[0:64, :], ALU.add))
            dnr = op("dve", lambda e: e.tensor_copy(recb[0:1, :], recin[32:33, :]),
                     inc="dve")
            wait("pe", "dve", dnr)
            mm(ps_d[0:64, :], onesr[:, :], recin[0:1, :], True, True, tp=(0, 0))
            bcv = mm(ps_d[64:128, :], onesr[:, :], recb[0:1, :], True, True,
                     tp=(0, 64), inc="pe")
            wait("dve", "pe", bcv)
            pd_war = op("dve", lambda e: e.tensor_copy(recin[:, :], ps_d[:, :]),
                        inc="dve")
            op("dve", lambda e: e.reciprocal(recb[:, :], recin[:, :]))
            wait("dve", "sat", sat_last)
            po_war = op("dve", lambda e: e.tensor_tensor(
                atu_t[:, :], ps_o[:, :], recb[:, :], ALU.mult), inc="dve")
            wait("sync", "dve", po_war)
            sat_last = dma("sync",
                           atu_d[hp * 128:(hp + 1) * 128, qb * 512:(qb + 1) * 512],
                           atu_t[:, :], "sat")

    awo_b = [vtb[:, 0:NFC * 512], vtb[:, NFC * 512:2 * NFC * 512]]
    xq_b = [xq[:, :], ktb[:, :].bitcast(F32)]
    aw_sds = [None, None]
    x_sds = [None, None]
    s1_war = [None, None, None, None]
    s2_war = [None, None, None, None]
    ffn2_last = None
    y_dve = None
    prev_fev = None
    wo_prj_last = None

    aw_sem = ["sao", "svt"]
    x_sem = ["sx", "skt"]

    def load_qb(j):
        aw_sds[j % 2] = dma("sync", awo_b[j % 2].rearrange("p (c t) -> p c t",
                                                           t=512),
                            atu_d[:, j * 512:(j + 1) * 512]
                            .rearrange("(c p) t -> p c t", p=128), aw_sem[j % 2])
        x_sds[j % 2] = dma("sync", xq_b[j % 2].rearrange("p (c t) -> p c t",
                                                         t=512),
                           xT[:, j * 512:(j + 1) * 512]
                           .rearrange("(c p) t -> p c t", p=128), x_sem[j % 2])

    for qb in range(NTB):
        awo = awo_b[qb % 2]
        xqs = xq_b[qb % 2]
        aw_sd = aw_sds[qb % 2]
        x_sd = x_sds[qb % 2]
        if qb == 0:
            wait("sync", "sat", ctr["sat"])
            load_qb(0)
            aw_sd, x_sd = aw_sds[0], x_sds[0]
            if NTB > 1:
                load_qb(1)
        elif qb + 1 < NTB:
            wait("sync", "pe", wo_prj_last)
            wait("sync", "sout", 16 * qb)
            load_qb(qb + 1)
        if dbg:
            wait("act", "sat", ctr["sat"])
            wait("dve", "sat", ctr["sat"])
        a2vs = []
        prj = None
        for oc in range(NOC):
            acc = ps_s[:, (oc % 4) * 512:(oc % 4) * 512 + 512]
            if oc == 0:
                wait("pe", aw_sem[qb % 2], aw_sd)
                wait("pe", "swc%d" % wo_par, wo_cast)
                wait("pe", "dve", max(po_war or 0, y_dve or 0) or None)
                wait("pe", "act", ctr["act"])
            if oc >= 4:
                wait("pe", "act", a2vs[oc - 4])
            for fc in range(NFC):
                prj = mm(acc[:, :],
                         wbuf[wo_par][:, fc * D + oc * 128: fc * D + (oc + 1) * 128],
                         awo[:, fc * 512:(fc + 1) * 512], fc == 0, fc == NFC - 1,
                         inc="pe" if fc == NFC - 1 else None)
            wo_prj_last = prj
            wait("act", "pe", prj)
            a2vs.append(op("act", lambda e, o=oc, a=acc: e.copy(
                a2f[:, o * 512:(o + 1) * 512], a[:, :]), inc="act"))
        a2v = a2vs[-1]
        if dbg:
            wait("sync", "act", a2v)
            dma("sync", dbgA[:, qb * 512:(qb + 1) * 512]
                .rearrange("(c p) t -> p c t", p=128),
                a2f[:, :].rearrange("p (c t) -> p c t", t=512), "sat")
        bc2m = ps_s[:, 0:512]
        bc2r = ps_s[:, 512:1024]
        bcv2 = layer_norm_stats(a2f, bc2m, bc2r, pre_act=a2v, pre_pe=prj,
                                pre_dve=prev_fev)
        wait("dve", "pe", bcv2)
        wait("dve", x_sem[qb % 2], x_sd)
        rv = None
        for fc in range(NFC):
            op("dve", lambda e, f=fc, b=bc2m: e.tensor_tensor(
                t1s[:, :], a2f[:, f * 512:(f + 1) * 512], b, ALU.subtract))
            op("dve", lambda e, b=bc2r: e.tensor_tensor(t2s[:, :], t1s[:, :], b, ALU.mult))
            rv = op("dve", lambda e, f=fc, x=xqs: e.tensor_tensor(
                x[:, f * 512:(f + 1) * 512], x[:, f * 512:(f + 1) * 512],
                t2s[:, :], ALU.add), inc="dve")
        if dbg:
            wait("sync", "dve", rv)
            dma("sync", dbgB[:, qb * 512:(qb + 1) * 512]
                .rearrange("(c p) t -> p c t", p=128),
                xqs.rearrange("p (c t) -> p c t", t=512), "sat")
        bc3m = ps_s[:, 1024:1536]
        bc3r = ps_s[:, 1536:2048]
        bcv3 = layer_norm_stats(xqs, bc3m, bc3r, pre_dve=rv)
        wait("dve", "pe", bcv3)
        ln3v = None
        lnv = {}
        for fc in range(NFC):
            op("dve", lambda e, f=fc, b=bc3m, x=xqs: e.tensor_tensor(
                t1s[:, :], x[:, f * 512:(f + 1) * 512], b, ALU.subtract))
            ln3v = op("dve", lambda e, f=fc, b=bc3r: e.tensor_tensor(
                lns[:, f * 512:(f + 1) * 512], t1s[:, :], b, ALU.mult),
                inc="dve")
            lnv[fc] = ln3v
        if dbg:
            wait("sync", "dve", ln3v)
            dma("sync", dbgC[:, qb * 512:(qb + 1) * 512]
                .rearrange("(c p) t -> p c t", p=128),
                lns[:, :].rearrange("p (c t) -> p c t", t=512), "sat")
        fev = None
        for half in range(2):
            if dbg:
                wait("act", "sat", ctr["sat"])
            gel = {}
            for idx, hc in enumerate(range(NHC)):
                hci = hc
                k = idx % 4
                wait("sync", "pe", s1_war[k])
                s1_sd = dma("sync",
                            s1b[k][:, :].rearrange("p (c h) -> p c h", h=128),
                            sgn1[:, hc * 128:(hc + 1) * 128]
                            .rearrange("(c p) h -> p c h", p=128), "s1%d" % k)
                acc = ps_s[:, (idx % 4) * 512:(idx % 4) * 512 + 512]
                wait("pe", "s1%d" % k, s1_sd)
                if idx < 4:
                    wait("pe", "dve", max(alpha_done, fev or 0))
                    if half == 0:
                        wait("pe", "act", a2v)
                wait("pe", "act", gel.get(idx - 4))
                prj = None
                for fc in range(NFC):
                    if idx == 0 and half == 0:
                        wait("pe", "dve", lnv[fc])
                    prj = mm(acc[:, 0:256],
                             s1b[k][:, fc * 128:(fc + 1) * 128],
                             lns[:, fc * 512 + half * 256: fc * 512 + half * 256 + 256],
                             fc == 0, fc == NFC - 1,
                             inc="pe" if fc == NFC - 1 else None)
                s1_war[k] = prj
                wait("act", "pe", prj)
                if dbg and qb == 0 and half == 0 and hc in (29, 31):
                    pcp = op("act", lambda e, a=acc, h=hc: e.copy(
                        atu_t[:, (0 if h == 31 else 256):(256 if h == 31 else 512)],
                        a[:, 0:256]), inc="act")
                    if hc == 29 or hc == 31:
                        dbg_pcp = pcp
                dst = phb[:, hc * 256:(hc + 1) * 256]
                gfn = AF.Identity if gelu_copy else AF.Gelu
                gel[hci] = op("act", lambda e, h=hc, a=acc, d=dst, gfn=gfn:
                    e.activation(d, a[:, 0:256], gfn,
                                 bias=b1s[:, h:h + 1], scale=a1c[:, h:h + 1]),
                    inc="act")
            if dbg and qb == 0 and half == 0:
                wait("sync", "act", dbg_pcp)
                dma("sync", dbgP[:, :], atu_t[:, :], "sat")
            if dbg:
                wait("sync", "act", max(gel.values()))
                dma("sync", dbgG[:, qb * 512 + half * 256: qb * 512 + half * 256 + 256]
                    .rearrange("(c p) t -> p c t", p=128),
                    phb[:, 0:NHC * 256].rearrange("p (c t) -> p c t", t=256), "sat")
            f2_tiles = [ps_s[:, 0:256], ps_s[:, 512:768], ps_s[:, 1024:1280],
                        ps_s[:, 1536:1792], ps_s[:, 2048:2304],
                        ps_s[:, 2560:2816], ps_o[:, 0:256], ps_d[:, 0:256]]
            f2 = None
            for hc in range(NHC):
                k = hc % 4
                wait("sync", "pe", s2_war[k])
                s2_sd = dma("sync", s2b[k][:, :],
                            sgn2[hc * 128:(hc + 1) * 128, :], "s2%d" % k)
                wait("pe", "s2%d" % k, s2_sd)
                wait("pe", "act", gel[hc])
                if hc == 0:
                    wait("pe", "act", max(gel.values()))
                    wait("pe", "dve", fev)
                for oc in range(NOC):
                    f2 = mm(f2_tiles[oc],
                            s2b[k][:, oc * 128:(oc + 1) * 128],
                            phb[:, hc * 256:(hc + 1) * 256],
                            hc == 0, hc == NHC - 1, sgc=True,
                            inc="pe" if oc == NOC - 1 else None)
                s2_war[k] = f2
            ffn2_last = f2
            wait("dve", "pe", f2)
            if dbg:
                wait("dve", "sat", ctr["sat"])
            for oc in range(NOC):
                fev = op("dve", lambda e, o=oc, h=half: e.tensor_scalar(
                    a2f[:, o * 512 + h * 256: o * 512 + h * 256 + 256],
                    f2_tiles[o],
                    a2c[:, o:o + 1], b2s[:, o:o + 1], ALU.mult, ALU.add),
                    inc="dve")
        prev_fev = fev
        if dbg:
            wait("sync", "dve", fev)
            dma("sync", dbgE[:, qb * 512:(qb + 1) * 512]
                .rearrange("(c p) t -> p c t", p=128),
                a2f[:, :].rearrange("p (c t) -> p c t", t=512), "sat")
        if dbg:
            wait("dve", "sat", ctr["sat"])
        bcv4 = layer_norm_stats(a2f, ps_g1, ps_g2, pre_dve=fev)
        wait("dve", "pe", bcv4)
        for fc in range(NFC):
            op("dve", lambda e, f=fc: e.tensor_tensor(
                t1s[:, :], a2f[:, f * 512:(f + 1) * 512], ps_g1[:, :], ALU.subtract))
            op("dve", lambda e: e.tensor_tensor(t2s[:, :], t1s[:, :], ps_g2[:, :],
                                                ALU.mult))
            y_dve = op("dve", lambda e, f=fc, x=xqs: e.tensor_tensor(
                x[:, f * 512:(f + 1) * 512], x[:, f * 512:(f + 1) * 512],
                t2s[:, :], ALU.add), inc="dve")
        wait("sync", "dve", y_dve)
        dma("sync", yT[:, qb * 512:(qb + 1) * 512]
            .rearrange("(c p) t -> p c t", p=128),
            xqs.rearrange("p (c t) -> p c t", t=512), "sout")

    if dbg:
        wait("sync", "dve", ctr["dve"])
        wait("sync", "act", ctr["act"])
        dma("sync", dbgW2[:, 0:NHC], a1c[:, 0:NHC], "sat")
        dma("sync", dbgW2[:, NHC:2 * NHC], b1s[:, 0:NHC], "sat")
        dma("sync", dbgW2[:, 2 * NHC:2 * NHC + NOC], a2c[:, 0:NOC], "sat")
        dma("sync", dbgW2[:, 2 * NHC + NOC:], b2s[:, 0:NOC], "sat")
    wait("sync", "sout", 16 * NTB)

    with nc.Block() as block:
        @block.tensor
        def _(e):
            for f in steps["pe"]:
                f(e)

        @block.scalar
        def _(e):
            for f in steps["act"]:
                f(e)

        @block.vector
        def _(e):
            for f in steps["dve"]:
                f(e)

        @block.gpsimd
        def _(e):
            for f in steps["pool"]:
                f(e)

        @block.sync
        def _(e):
            for f in steps["sync"]:
                f(e)

    ctx.close()
    return nc


def _stage_inputs(cfg, x, W_q, W_k, W_v, W_o, w1, b1, w2, b2):
    D, LPC, HID = cfg["D"], cfg["LPC"], cfg["HID"]
    NH, NHC, NOC = cfg["NH"], cfg["NHC"], cfg["NOC"]
    f32 = np.float32
    perm = np.concatenate([
        np.concatenate([np.arange(h * 64, (h + 1) * 64, 2),
                        np.arange(h * 64 + 1, (h + 1) * 64, 2)])
        for h in range(NH)])
    com = {
        "wqT": np.ascontiguousarray(np.asarray(W_q, f32).T[:, perm]),
        "wkT": np.ascontiguousarray(np.asarray(W_k, f32).T[:, perm]),
        "wvT": np.ascontiguousarray(np.asarray(W_v, f32).T),
        "woT": np.ascontiguousarray(np.asarray(W_o, f32).T),
        "w1T": np.ascontiguousarray(np.asarray(w1, f32).T),
        "w1n": np.ascontiguousarray(np.asarray(w1, f32)),
        "w2T": np.ascontiguousarray(np.asarray(w2, f32).T),
        "w2n": np.ascontiguousarray(np.asarray(w2, f32)),
        "b1c": np.ascontiguousarray(np.asarray(b1, f32).reshape(NHC, 128).T),
        "b2c": np.ascontiguousarray(np.asarray(b2, f32).reshape(NOC, 128).T),
    }
    Pm = np.zeros((128, 128), f32)
    for r in range(128):
        Pm[r, r ^ 32] = 1.0
    com["Pm"] = Pm
    inv = (BASE ** (-(np.arange(32, dtype=np.float64)) / 32.0))
    xnp = np.asarray(x, f32)
    in_maps = []
    for c in range(8):
        b, q0 = c // 4, (c % 4) * LPC
        pos = (q0 + np.arange(LPC, dtype=np.float64))
        ang = np.outer(inv, pos)
        cc = np.cos(ang).astype(f32)
        ss = np.sin(ang).astype(f32)
        CC = np.tile(cc, (4, 1))
        SS = np.concatenate([-ss, ss, -ss, ss], axis=0)
        m = dict(com)
        m["CC"] = np.ascontiguousarray(CC)
        m["SS"] = np.ascontiguousarray(SS)
        m["xT"] = np.ascontiguousarray(xnp[b, q0:q0 + LPC, :].T)
        in_maps.append(m)
    return in_maps


def kernel(L0_attn_x, L0_attn_W_q, L0_attn_W_k, L0_attn_W_v, L0_attn_W_o,
           L0_ffn_w1, L0_ffn_b1, L0_ffn_w2, L0_ffn_b2):
    cfg = make_cfg()
    in_maps = _stage_inputs(cfg, L0_attn_x, L0_attn_W_q, L0_attn_W_k, L0_attn_W_v,
                            L0_attn_W_o, L0_ffn_w1, L0_ffn_b1, L0_ffn_w2, L0_ffn_b2)
    nc = build_program(cfg, detect_races=False)
    res = run_bass_kernel_spmd(nc, in_maps, core_ids=list(range(8)))
    LPC = cfg["LPC"]
    out = np.empty((2, 4 * LPC, cfg["D"]), np.float32)
    for c in range(8):
        b, q0 = c // 4, (c % 4) * LPC
        out[b, q0:q0 + LPC, :] = res.results[c]["yT"].T
    return out


# version 19
# speedup vs baseline: 1.0086x; 1.0086x over previous
import math
import numpy as np

import concourse.bass as bass
import concourse.mybir as mybir
from concourse.bass_utils import run_bass_kernel_spmd

F32 = mybir.dt.float32
BF16 = mybir.dt.bfloat16
AF = mybir.ActivationFunctionType
ALU = mybir.AluOpType

BASE = 10000.0


def make_cfg(D=1024, LPC=2048, HID=4096):
    assert D % 128 == 0 and LPC % 512 == 0 and HID % 1024 == 0
    return dict(
        D=D, LPC=LPC, HID=HID,
        NH=D // 64, NHP=D // 128, NFC=D // 128,
        NTB=LPC // 512, NKC=4 * LPC // 128,
        NHC=HID // 128, NOC=D // 128, L=4 * LPC,
    )


def build_program(cfg, detect_races=True, dbg=False, gelu_copy=False):
    D, LPC, HID = cfg["D"], cfg["LPC"], cfg["HID"]
    NFC, NHP, NTB, NKC = cfg["NFC"], cfg["NHP"], cfg["NTB"], cfg["NKC"]
    NHC, NOC = cfg["NHC"], cfg["NOC"]
    from contextlib import ExitStack

    nc = bass.Bass("TRN2", target_bir_lowering=False,
                   detect_race_conditions=detect_races)

    dt = nc.dram_tensor
    xT = dt("xT", [D, LPC], F32, kind="ExternalInput")
    wqT = dt("wqT", [D, D], F32, kind="ExternalInput")
    wkT = dt("wkT", [D, D], F32, kind="ExternalInput")
    wvT = dt("wvT", [D, D], F32, kind="ExternalInput")
    woT = dt("woT", [D, D], F32, kind="ExternalInput")
    w1T = dt("w1T", [D, HID], F32, kind="ExternalInput")
    w1n = dt("w1n", [HID, D], F32, kind="ExternalInput")
    w2T = dt("w2T", [HID, D], F32, kind="ExternalInput")
    w2n = dt("w2n", [D, HID], F32, kind="ExternalInput")
    b1c = dt("b1c", [128, NHC], F32, kind="ExternalInput")
    b2c = dt("b2c", [128, NOC], F32, kind="ExternalInput")
    CCd = dt("CC", [128, LPC], F32, kind="ExternalInput")
    SSd = dt("SS", [128, LPC], F32, kind="ExternalInput")
    Pmd = dt("Pm", [128, 128], F32, kind="ExternalInput")
    yT = dt("yT", [D, LPC], F32, kind="ExternalOutput")

    dk = dict(kind="ExternalOutput") if dbg else {}
    agk_i = dt("agk_i", [D, LPC], BF16)
    agk_o = dt("agk_o", [4 * D, LPC], BF16)
    agv_i = dt("agv_i", [LPC, D], BF16)
    agv_o = dt("agv_o", [4 * LPC, D], BF16)
    sgn1 = dt("sgn1", [D, HID], BF16, **dk)
    sgn2 = dt("sgn2", [HID, D], BF16, **dk)
    atu_d = dt("atu_d", [D, LPC], BF16, **dk)
    if dbg:
        dbgK = dt("dbgK", [D, LPC], BF16, kind="ExternalOutput")
        dbgV = dt("dbgV", [LPC, D], BF16, kind="ExternalOutput")
        dbgA = dt("dbgA", [D, LPC], F32, kind="ExternalOutput")
        dbgB = dt("dbgB", [D, LPC], F32, kind="ExternalOutput")
        dbgE = dt("dbgE", [D, LPC], F32, kind="ExternalOutput")
        dbgC = dt("dbgC", [D, LPC], BF16, kind="ExternalOutput")
        dbgG = dt("dbgG", [HID, LPC], BF16, kind="ExternalOutput")
        dbgW = dt("dbgW", [128, 2 * (NHC + NOC)], F32, kind="ExternalOutput")
        dbgP = dt("dbgP", [128, 512], BF16, kind="ExternalOutput")
        dbgW2 = dt("dbgW2", [128, 2 * (NHC + NOC)], F32, kind="ExternalOutput")

    ctx = ExitStack()

    sb = nc.alloc_sbuf_tensor
    qrope = sb("qrope", [128, NHP * LPC], BF16)
    lns = sb("lns", [128, NFC * 512], BF16)
    xq = sb("xq", [128, NFC * 512], F32)
    guard1 = sb("guard1", [128, 256], F32)
    a2f = sb("a2f", [128, NFC * 512], F32)
    guard2 = sb("guard2", [128, 256], F32)
    wbuf = [sb("wbuf0", [128, NFC * D], BF16), sb("wbuf1", [128, NFC * D], BF16)]
    pt = sb("pt", [128, 3072], BF16)
    atu_t = sb("atu_t", [128, 512], BF16)
    den_s = sb("den_s", [128, 512], F32)
    recin = sb("recin", [128, 512], F32)
    recb = sb("recb", [128, 512], F32)
    rows = sb("rows", [1, 2048], F32)
    onesD = sb("onesD", [128, 1], F32)
    onesb = sb("onesb", [128, 1], BF16)
    onesr = sb("onesr", [1, 64], F32)
    a1c = sb("a1c", [128, NHC + 2], F32)
    a2c = sb("a2c", [128, NOC + 2], F32)
    b1s = sb("b1s", [128, NHC + 2], F32)
    b2s = sb("b2s", [128, NOC + 2], F32)
    t1s = sb("t1s", [128, 512], F32)
    t2s = sb("t2s", [128, 512], F32)
    phb = sb("phb", [128, max(NHC * 256, NFC * 512)], BF16)
    guard3 = sb("guard3", [128, 256], BF16)
    s1b = [sb("s1b%d" % i, [128, NFC * 128], BF16) for i in range(4)]
    s2b = [sb("s2b%d" % i, [128, D], BF16) for i in range(4)]

    r1 = ExitStack()
    CCs = r1.enter_context(nc.sbuf_tensor("CCs", [128, LPC], F32))
    SSs = r1.enter_context(nc.sbuf_tensor("SSs", [128, LPC], F32))
    Pms = r1.enter_context(nc.sbuf_tensor("Pms", [128, 128], F32))
    wst = [r1.enter_context(nc.sbuf_tensor("wst0", [128, 1024], F32)),
           r1.enter_context(nc.sbuf_tensor("wst1", [128, 1024], F32))]
    qpre = [r1.enter_context(nc.sbuf_tensor("qpre%d" % i, [128, 512], F32))
            for i in range(3)]
    sgs = [r1.enter_context(nc.sbuf_tensor("sgs0", [128, 512], BF16)),
           r1.enter_context(nc.sbuf_tensor("sgs1", [128, 512], BF16))]
    krt = [r1.enter_context(nc.sbuf_tensor("krt0", [128, 512], BF16)),
           r1.enter_context(nc.sbuf_tensor("krt1", [128, 512], BF16))]
    vlc = [r1.enter_context(nc.sbuf_tensor("vlc0", [128, D], BF16)),
           r1.enter_context(nc.sbuf_tensor("vlc1", [128, D], BF16))]

    ps = nc.alloc_psum_tensor
    ps_s = ps("ps_s", [128, 3072], F32)
    ps_o = ps("ps_o", [128, 512], F32)
    ps_d = ps("ps_d", [128, 512], F32)
    ps_g1 = ps_s[:, 0:512]
    ps_g2 = ps_s[:, 512:1024]
    bcA = ps_s[:, 2048:2560]
    bcB = ps_s[:, 2560:3072]
    sw_slots = [bcA, bcB]

    ENGS = ("pe", "act", "dve", "pool", "sync")
    steps = {k: [] for k in ENGS}
    sems, ctr = {}, {}
    for name in ("sc", "sw0", "sw1", "sx", "sao", "skt", "svt",
                 "skt1", "skt2", "skt3", "svt1", "svt2", "svt3",
                 "s10", "s11", "s12", "s13", "s20", "s21", "s22", "s23",
                 "kr0", "kr1", "vl0", "vl1", "sg0", "sg1", "swc0", "swc1",
                 "scc", "pe", "act", "dve", "sat", "sout"):
        sems[name] = ctx.enter_context(nc.semaphore(name))
        ctr[name] = 0

    def wait(eng, sem, val):
        if val is None or val <= 0:
            return
        steps[eng].append(lambda e, s=sems[sem], v=val: e.wait_ge(s, v))

    def op(eng, fn, inc=None, n=1):
        if inc is None:
            steps[eng].append(fn)
            return None
        ctr[inc] += n
        steps[eng].append(lambda e, fn=fn, s=sems[inc], n=n: fn(e).then_inc(s, n))
        return ctr[inc]

    def mm(out, lhsT, rhs, start, stop, tp=None, inc=None, sgc=False):
        return op("pe", lambda e, o=out, l=lhsT, r=rhs, s=start, t=stop, p=tp,
                  g=sgc: e.matmul(o, l, r, start=s, stop=t, tile_position=p,
                                  skip_group_check=g), inc=inc)

    def dma(eng, dst, src, inc):
        return op(eng, lambda e, d=dst, s=src: e.dma_start(out=d, in_=s), inc=inc, n=16)

    dma("sync", CCs[:, :], CCd[:, :], "sc")
    dma("sync", SSs[:, :], SSd[:, :], "sc")
    dma("sync", Pms[:, :], Pmd[:, :], "sc")
    dma("sync", b1s[:, 0:NHC], b1c[:, :], "sc")
    v_consts = dma("sync", b2s[:, 0:NOC], b2c[:, :], "sc")
    for _eng in ("pe", "act", "dve"):
        wait(_eng, "sc", v_consts)

    op("dve", lambda e: e.memset(a1c[:, :], 0.0))
    op("dve", lambda e: e.memset(a2c[:, :], 0.0))
    op("dve", lambda e: e.memset(onesD[:, :], 1.0 / D))
    op("dve", lambda e: e.memset(onesb[:, :], 1.0))
    op("dve", lambda e: e.memset(onesr[:, :], 1.0))
    op("dve", lambda e: e.memset(ps_d[:, :], 0.0))
    op("dve", lambda e: e.memset(ps_s[:, :], 0.0))
    v_init = op("dve", lambda e: e.memset(ps_o[:, :], 0.0), inc="dve")

    def layer_norm_stats(src, bc1, bc2, pre_pe=None, pre_dve=None, pre_act=None,
                         pre_sd=None):
        if pre_sd:
            wait("pe", pre_sd[0], pre_sd[1])
        wait("pe", "pe", pre_pe)
        wait("pe", "act", pre_act)
        wait("pe", "dve", pre_dve)
        for fc in range(NFC):
            mm(ps_d[0:1, :], onesD[:, :], src[:, fc * 512:(fc + 1) * 512],
               fc == 0, fc == NFC - 1, tp=(0, 0), sgc=True)
        msqv = {}
        if pre_sd:
            wait("dve", pre_sd[0], pre_sd[1])
        wait("dve", "act", pre_act)
        wait("dve", "dve", pre_dve)
        for fc in range(NFC):
            tscr = t1s if fc % 2 == 0 else t2s
            if fc >= 2:
                wait("dve", "pe", msqv[fc - 2])
            sqv = op("dve", lambda e, t=tscr, s=src, f=fc:
                     e.tensor_tensor(t[:, :], s[:, f * 512:(f + 1) * 512],
                                     s[:, f * 512:(f + 1) * 512], ALU.mult),
                     inc="dve")
            wait("pe", "dve", sqv)
            msqv[fc] = mm(ps_d[64:65, :], onesD[:, :], tscr[:, :],
                          fc == 0, fc == NFC - 1, tp=(0, 64), inc="pe", sgc=True)
        wait("dve", "pe", msqv[NFC - 1])
        op("dve", lambda e: e.tensor_copy(rows[0:1, 0:512], ps_d[0:1, :]))
        op("dve", lambda e: e.tensor_copy(rows[0:1, 1024:1536], ps_d[64:65, :]))
        op("dve", lambda e: e.tensor_tensor(rows[0:1, 1536:2048], rows[0:1, 0:512],
                                            rows[0:1, 0:512], ALU.mult))
        op("dve", lambda e: e.tensor_tensor(
            rows[0:1, 1024:1536], rows[0:1, 1024:1536], rows[0:1, 1536:2048],
            ALU.subtract))
        dv = op("dve", lambda e: e.tensor_scalar_add(
            rows[0:1, 1024:1536], rows[0:1, 1024:1536], 1e-5), inc="dve")
        wait("act", "dve", dv)
        op("act", lambda e: e.activation(rows[0:1, 1024:1536], rows[0:1, 1024:1536],
                                         AF.Ln))
        av = op("act", lambda e: e.activation(rows[0:1, 512:1024],
                                              rows[0:1, 1024:1536],
                                              AF.Exp, scale=-0.5), inc="act")
        wait("pe", "act", av)
        mm(bc1[0:64, :], onesr[:, :], rows[0:1, 0:512], True, True, tp=(0, 0))
        mm(bc1[64:128, :], onesr[:, :], rows[0:1, 0:512], True, True, tp=(0, 64))
        mm(bc2[0:64, :], onesr[:, :], rows[0:1, 512:1024], True, True, tp=(0, 0))
        return mm(bc2[64:128, :], onesr[:, :], rows[0:1, 512:1024], True, True,
                  tp=(0, 64), inc="pe")

    wst_war = [None, None]
    wbuf_war = [None, None]

    def load_weight(wdram, par):
        wait("pool", "pe", wbuf_war[par])
        return dma("pool",
                   wbuf[par][:, :].rearrange("p (c n) -> p c n", n=D),
                   wdram[:, :].rearrange("(c p) n -> p c n", p=128),
                   "swc%d" % par)

    widx = 0
    kv_stores = 0
    evac_war3 = [None, None, None]
    evac_war4 = [None, None, None, None]
    slot_t2 = [None, None]
    qp_rd_pe = [None, None, None]
    qp_rd_dve = [None, None, None]
    kst = [None, None]
    vst = [None, None]
    acc_war = None

    prev_bcv = None
    for tb in range(NTB):
        if tb > 0:
            wait("sync", "pe", prev_bcv)
            wait("sync", "dve", lns_ready)
        x_sd = dma("sync", xq[:, :].rearrange("p (c t) -> p c t", t=512),
                   xT[:, tb * 512:(tb + 1) * 512]
                   .rearrange("(c p) t -> p c t", p=128), "sx")
        bcv = layer_norm_stats(xq, bcA, bcB, pre_sd=("sx", x_sd),
                               pre_dve=max(v_init if tb == 0 else 0,
                                           acc_war or 0) or None)
        lastap = None
        wait("dve", "pe", bcv)
        for fc in range(NFC):
            op("dve", lambda e, f=fc: e.tensor_tensor(
                t1s[:, :], xq[:, f * 512:(f + 1) * 512], bcA, ALU.subtract))
            lastap = op("dve", lambda e, f=fc: e.tensor_tensor(
                lns[:, f * 512:(f + 1) * 512], t1s[:, :], bcB, ALU.mult),
                inc="dve")
        lns_ready = lastap
        prev_bcv = bcv

        for (wdram, is_q) in ((wqT, True), (wkT, False)):
            par = widx % 2
            widx += 1
            castv = load_weight(wdram, par)
            wbuf_war[par] = None
            prjs = {}

            def emit_swap(oc, tb=tb):
                qp = qpre[oc % 3]
                slot = sw_slots[oc % 2]
                wait("pe", "act", evac_war3[oc % 3])
                wait("pe", "dve", slot_t2[oc % 2])
                swv = mm(slot, Pms[:, :], qp[:, :], True, True, inc="pe")
                qp_rd_pe[oc % 3] = swv
                wait("dve", "act", evac_war3[oc % 3])
                dv1 = op("dve", lambda e, q=qp, t=tb: e.tensor_tensor(
                    t1s[:, :], q[:, :], CCs[:, t * 512:(t + 1) * 512], ALU.mult),
                    inc="dve")
                qp_rd_dve[oc % 3] = dv1
                wait("dve", "pe", swv)
                t2v = op("dve", lambda e, s=slot, t=tb: e.tensor_tensor(
                    t2s[:, :], s, SSs[:, t * 512:(t + 1) * 512], ALU.mult),
                    inc="dve")
                slot_t2[oc % 2] = t2v
                if is_q:
                    dst = qrope[:, oc * LPC + tb * 512: oc * LPC + (tb + 1) * 512]
                    op("dve", lambda e, d=dst: e.tensor_tensor(
                        d, t1s[:, :], t2s[:, :], ALU.add))
                else:
                    kb = krt[oc % 2]
                    wait("dve", "kr%d" % (oc % 2), kst[oc % 2])
                    rv = op("dve", lambda e, d=kb: e.tensor_tensor(
                        d[:, :], t1s[:, :], t2s[:, :], ALU.add), inc="dve")
                    wait("pool", "dve", rv)
                    kst[oc % 2] = dma(
                        "pool",
                        agk_i[oc * 128:(oc + 1) * 128, tb * 512:(tb + 1) * 512],
                        kb[:, :], "kr%d" % (oc % 2))

            for oc in range(NHP):
                acc = ps_s[:, (oc % 4) * 512:(oc % 4) * 512 + 512]
                if oc == 0:
                    wait("pe", "swc%d" % par, castv)
                    wait("pe", "dve", lns_ready)
                wait("pe", "act", evac_war4[oc % 4])
                prj = None
                for fc in range(NFC):
                    prj = mm(acc,
                             wbuf[par][:, fc * D + oc * 128: fc * D + (oc + 1) * 128],
                             lns[:, fc * 512:(fc + 1) * 512],
                             fc == 0, fc == NFC - 1,
                             inc="pe" if fc == NFC - 1 else None)
                prjs[oc] = prj
                wbuf_war[par] = prj
                qp = qpre[oc % 3]
                wait("act", "pe", prj)
                wait("act", "pe", qp_rd_pe[oc % 3])
                wait("act", "dve", qp_rd_dve[oc % 3])
                ev = op("act", lambda e, q=qp, a=acc: e.copy(q[:, :], a[:, :]),
                        inc="act")
                evac_war3[oc % 3] = ev
                evac_war4[oc % 4] = ev
                if oc >= 1:
                    emit_swap(oc - 1)
            emit_swap(NHP - 1)

        par = widx % 2
        widx += 1
        castv = load_weight(wvT, par)
        wbuf_war[par] = None
        for tc in range(4):
            vb = vlc[tc % 2]
            prj = None
            if tc == 0:
                wait("pe", "swc%d" % par, castv)
                wait("pe", "dve", lns_ready)
            wait("pe", "dve", acc_war)
            vw = min(512, D)
            if tc == 0:
                wait("pe", "act", max(v for v in evac_war4 if v) if any(evac_war4) else None)
            for oh in range(max(1, D // 512)):
                accv = ps_g1 if oh % 2 == 0 else ps_g2
                for fc in range(NFC):
                    prj = mm(accv[:, 0:vw],
                             lns[:, fc * 512 + tc * 128: fc * 512 + (tc + 1) * 128],
                             wbuf[par][:, fc * D + oh * vw: fc * D + (oh + 1) * vw],
                             fc == 0, fc == NFC - 1,
                             inc="pe" if fc == NFC - 1 else None)
            wbuf_war[par] = prj
            wait("dve", "pe", prj)
            wait("dve", "vl%d" % (tc % 2), vst[tc % 2])
            for oh in range(max(1, D // 512)):
                accv = ps_g1 if oh % 2 == 0 else ps_g2
                vv = op("dve", lambda e, v=vb, a=accv, o=oh, vw=vw: e.tensor_copy(
                    v[:, o * vw:(o + 1) * vw], a[:, 0:vw]), inc="dve")
            acc_war = vv
            wait("pool", "dve", vv)
            vst[tc % 2] = dma(
                "pool", agv_i[(tb * 4 + tc) * 128:(tb * 4 + tc + 1) * 128, :],
                vb[:, :], "vl%d" % (tc % 2))

    wo_par = widx % 2
    widx += 1
    wo_cast = load_weight(woT, wo_par)
    wbuf_war[wo_par] = None

    r1_marks = dict(ctr)
    sgstage = [xq[:, 0:1024], xq[:, 1024:2048]]
    sgout = [s2b[2][:, 0:512], s2b[3][:, 0:512]]
    wait("sync", "pe", ctr["pe"])
    wait("sync", "dve", ctr["dve"])
    sg_war = [None, None]
    i = 0
    sg_hist = {}
    for (src_d, dst_d, RP, CT) in ((w1T, sgn1, NFC, HID), (w2T, sgn2, NHC, D)):
        CJ, CW = max(1, CT // 512), min(512, CT)
        for rp in range(RP):
            for j in range(CJ):
                k = i % 2
                wait("sync", "act", sg_hist.get(i - 2))
                ld = dma("sync", sgstage[k][:, 0:CW],
                         src_d[rp * 128:(rp + 1) * 128, j * CW:(j + 1) * CW],
                         "sw%d" % k)
                wait("act", "sw%d" % k, ld)
                wait("act", "sg%d" % k, sg_war[k])
                sv = op("act", lambda e, k=k, CW=CW: e.sign(sgout[k][:, 0:CW],
                                                            sgstage[k][:, 0:CW]),
                        inc="act")
                sg_hist[i] = sv
                wait("pool", "act", sv)
                sg_war[k] = dma("pool",
                                dst_d[rp * 128:(rp + 1) * 128, j * CW:(j + 1) * CW],
                                sgout[k][:, 0:CW], "sg%d" % k)
                i += 1
    wait("sync", "act", ctr["act"])
    astage = [xq[:, 2048:3072], xq[:, 3072:4096]]
    ab_hist = {}
    ai = 0
    prev_abs = None
    for (src_d, col_t, NCH, DEN) in ((w1n, a1c, NHC, D), (w2n, a2c, NOC, HID)):
        pieces = max(1, DEN // 1024)
        pw = min(DEN, 1024)
        for ch in range(NCH):
            for pz in range(pieces):
                k = ai % 2
                wait("sync", "act", ab_hist.get(ai - 2))
                ldv = dma("sync", astage[k][:, 0:pw],
                          src_d[ch * 128:(ch + 1) * 128, pz * pw:(pz + 1) * pw],
                          "sw%d" % k)
                wait("act", "sw%d" % k, ldv)
                dst_col = col_t[:, NCH:NCH + 1] if pieces > 1 else col_t[:, ch:ch + 1]
                prev_abs = op("act", lambda e, pw=pw, dc=dst_col, k=k: e.activation(
                    s1b[k][:, 0:pw], astage[k][:, 0:pw], AF.Abs, accum_out=dc),
                    inc="act")
                ab_hist[ai] = prev_abs
                ai += 1
                if pieces > 1:
                    wait("dve", "act", prev_abs)
                    if pz == 0:
                        op("dve", lambda e, ct=col_t, c=ch, n=NCH: e.tensor_copy(
                            ct[:, c:c + 1], ct[:, n:n + 1]))
                    else:
                        op("dve", lambda e, ct=col_t, c=ch, n=NCH: e.tensor_tensor(
                            ct[:, c:c + 1], ct[:, c:c + 1], ct[:, n:n + 1], ALU.add))
                    prev_dve = op("dve", lambda e, ct=col_t, c=ch: e.tensor_copy(
                        t2s[:, 0:1], ct[:, c:c + 1]), inc="dve")
                    wait("act", "dve", prev_dve)
        wait("dve", "act", prev_abs)
        op("dve", lambda e, ct=col_t, n=NCH, d=DEN: e.tensor_scalar_mul(
            ct[:, 0:n], ct[:, 0:n], 1.0 / d))
    alpha_act_mark = ctr["act"]
    alpha_done = op("dve", lambda e: e.tensor_copy(t1s[:, 0:1], a1c[:, 0:1]),
                    inc="dve")

    if dbg:
        for _s in ("kr0", "kr1", "vl0", "vl1"):
            wait("pool", _s, ctr[_s])
        dma("pool", dbgK[:, :], agk_i[:, :], "sg0")
        dma("pool", dbgV[:, :], agv_i[:, :], "sg0")
    rgs = [[0, 1, 2, 3], [4, 5, 6, 7]]
    for _s in ("kr0", "kr1"):
        wait("pool", _s, ctr[_s])
    op("pool", lambda e: e.collective_compute(
        "AllGather", ALU.bypass, replica_groups=rgs,
        ins=[agk_i.ap().opt()], outs=[agk_o.ap().opt()]), inc="scc", n=1)
    for _s in ("vl0", "vl1"):
        wait("pool", _s, ctr[_s])
    op("pool", lambda e: e.collective_compute(
        "AllGather", ALU.bypass, replica_groups=rgs,
        ins=[agv_i.ap().opt()], outs=[agv_o.ap().opt()]), inc="scc", n=1)

    if dbg:
        wait("sync", "dve", alpha_done)
        dma("sync", dbgW[:, 0:NHC], a1c[:, 0:NHC], "sat")
        dma("sync", dbgW[:, NHC:2 * NHC], b1s[:, 0:NHC], "sat")
        dma("sync", dbgW[:, 2 * NHC:2 * NHC + NOC], a2c[:, 0:NOC], "sat")
        dma("sync", dbgW[:, 2 * NHC + NOC:], b2s[:, 0:NOC], "sat")
    r1.close()
    ktb = sb("ktb", [128, 4 * LPC], BF16)
    vtb = sb("vtb", [128, 4 * LPC], BF16)

    exp_free = {0: None, 1: None, 2: None}
    pt_free = {0: None, 1: None, 2: None}
    attn_last_pe = None
    sat_last = None
    po_war = None
    pd_war = None

    RCH = NKC // 4
    ktsem = ["skt", "skt1", "skt2", "skt3"]
    vtsem = ["svt", "svt1", "svt2", "svt3"]
    kt_rv = [None] * 4
    vt_rv = [None] * 4

    def load_kv(hp, gate_scv=None, gate_pvs=None):
        for r in range(4):
            if gate_scv is not None:
                wait("sync", "pe", gate_scv[(r + 1) * RCH - 1])
            kt_rv[r] = dma("sync", ktb[:, r * LPC:(r + 1) * LPC],
                           agk_o[r * D + hp * 128: r * D + (hp + 1) * 128, :],
                           ktsem[r])
            if gate_pvs is not None:
                wait("sync", "pe", gate_pvs[(r + 1) * RCH - 1])
            for q4 in range(LPC // 512):
                base = (r * (LPC // 128) + q4 * 4) * 128
                vt_rv[r] = dma(
                    "sync",
                    vtb[:, base:base + 512].rearrange("p (c d) -> p c d", d=128),
                    agv_o[r * LPC + q4 * 512: r * LPC + (q4 + 1) * 512,
                          hp * 128:(hp + 1) * 128]
                    .rearrange("(c p) d -> p c d", p=128), vtsem[r])

    for hp in range(NHP):
        if hp == 0:
            wait("sync", "scc", 2)
            for _s in ("kr0", "kr1", "vl0", "vl1"):
                wait("sync", _s, r1_marks[_s])
            wait("sync", "dve", r1_marks["dve"])
            wait("sync", "act", r1_marks["act"])
            wait("sync", "pe", r1_marks["pe"])
            load_kv(0)

        for qb in range(NTB):
            qsl = slice(hp * LPC + qb * 512, hp * LPC + (qb + 1) * 512)
            denv = None
            scv = {}
            pvs = {}
            LKA = 2
            for ci in range(NKC + LKA):
                if ci < NKC:
                    c = ci
                    w = c % 3
                    wait("pe", "act", exp_free[w])
                    if qb == 0 and c % RCH == 0:
                        wait("pe", ktsem[c // RCH], kt_rv[c // RCH])
                    mm(ps_s[:, w * 1024: w * 1024 + 512],
                       ktb[0:64, c * 128:(c + 1) * 128], qrope[0:64, qsl],
                       True, True, tp=(0, 0))
                    scv[c] = mm(ps_s[:, w * 1024 + 512: w * 1024 + 1024],
                                ktb[64:128, c * 128:(c + 1) * 128],
                                qrope[64:128, qsl],
                                True, True, tp=(64, 0), inc="pe")
                if ci >= LKA:
                    c = ci - LKA
                    w = c % 3
                    wait("act", "pe", scv[c])
                    wait("act", "pe", pt_free[w])
                    expv = op("act", lambda e, w=w: e.activation(
                        pt[:, w * 1024:(w + 1) * 1024],
                        ps_s[:, w * 1024:(w + 1) * 1024],
                        AF.Exp, scale=0.125), inc="act")
                    exp_free[w] = expv
                    wait("pe", "act", expv)
                    if c == 0:
                        wait("pe", "dve", po_war)
                    if qb == 0 and c % RCH == 0:
                        wait("pe", vtsem[c // RCH], vt_rv[c // RCH])
                    mm(ps_o[0:64, :], vtb[:, c * 128: c * 128 + 64],
                       pt[:, w * 1024: w * 1024 + 512],
                       c == 0, c == NKC - 1, tp=(0, 0), sgc=True)
                    pvv = mm(ps_o[64:128, :], vtb[:, c * 128 + 64: c * 128 + 128],
                             pt[:, w * 1024 + 512: w * 1024 + 1024],
                             c == 0, c == NKC - 1, tp=(0, 64), inc="pe", sgc=True)
                    pt_free[w] = pvv
                    pvs[c] = pvv
                    if c % 2 == 1:
                        g = c // 2
                        wp = (c - 1) % 3
                        if g == 0:
                            wait("pe", "dve", pd_war)
                        st = (g == 0)
                        sp = (g == NKC // 2 - 1)
                        mm(ps_d[0:1, :], onesb[:, :],
                           pt[:, wp * 1024: wp * 1024 + 512], st, sp, tp=(0, 0),
                           sgc=True)
                        mm(ps_d[32:33, :], onesb[:, :],
                           pt[:, wp * 1024 + 512: wp * 1024 + 1024], st, sp,
                           tp=(0, 32), sgc=True)
                        mm(ps_d[64:65, :], onesb[:, :],
                           pt[:, w * 1024: w * 1024 + 512], st, sp,
                           tp=(0, 64), sgc=True)
                        denv = mm(ps_d[96:97, :], onesb[:, :],
                                  pt[:, w * 1024 + 512: w * 1024 + 1024], st, sp,
                                  tp=(0, 96), inc="pe", sgc=True)
                        pt_free[wp] = denv
                        pt_free[w] = denv
            attn_last_pe = ctr["pe"]
            if qb == NTB - 1 and hp + 1 < NHP:
                load_kv(hp + 1, gate_scv=scv, gate_pvs=pvs)
            wait("dve", "pe", denv)
            pd_war = op("dve", lambda e: e.tensor_copy(den_s[:, :], ps_d[:, :]),
                        inc="dve")
            op("dve", lambda e: e.tensor_copy(t1s[0:64, :], den_s[64:128, :]))
            op("dve", lambda e: e.tensor_tensor(
                recin[0:64, :], den_s[0:64, :], t1s[0:64, :], ALU.add))
            dnr = op("dve", lambda e: e.tensor_copy(recb[0:1, :], recin[32:33, :]),
                     inc="dve")
            wait("pe", "dve", dnr)
            mm(ps_d[0:64, :], onesr[:, :], recin[0:1, :], True, True, tp=(0, 0))
            bcv = mm(ps_d[64:128, :], onesr[:, :], recb[0:1, :], True, True,
                     tp=(0, 64), inc="pe")
            wait("dve", "pe", bcv)
            pd_war = op("dve", lambda e: e.tensor_copy(recin[:, :], ps_d[:, :]),
                        inc="dve")
            op("dve", lambda e: e.reciprocal(recb[:, :], recin[:, :]))
            wait("dve", "sat", sat_last)
            po_war = op("dve", lambda e: e.tensor_tensor(
                atu_t[:, :], ps_o[:, :], recb[:, :], ALU.mult), inc="dve")
            wait("sync", "dve", po_war)
            sat_last = dma("sync",
                           atu_d[hp * 128:(hp + 1) * 128, qb * 512:(qb + 1) * 512],
                           atu_t[:, :], "sat")

    awo_b = [vtb[:, 0:NFC * 512], vtb[:, NFC * 512:2 * NFC * 512]]
    xq_b = [xq[:, :], ktb[:, :].bitcast(F32)]
    aw_sds = [None, None]
    x_sds = [None, None]
    s1_war = [None, None, None, None]
    s2_war = [None, None, None, None]
    ffn2_last = None
    y_dve = None
    prev_fev = None
    wo_prj_last = None

    aw_sem = ["sao", "svt"]
    x_sem = ["sx", "skt"]

    def load_qb(j):
        aw_sds[j % 2] = dma("sync", awo_b[j % 2].rearrange("p (c t) -> p c t",
                                                           t=512),
                            atu_d[:, j * 512:(j + 1) * 512]
                            .rearrange("(c p) t -> p c t", p=128), aw_sem[j % 2])
        x_sds[j % 2] = dma("sync", xq_b[j % 2].rearrange("p (c t) -> p c t",
                                                         t=512),
                           xT[:, j * 512:(j + 1) * 512]
                           .rearrange("(c p) t -> p c t", p=128), x_sem[j % 2])

    for qb in range(NTB):
        awo = awo_b[qb % 2]
        xqs = xq_b[qb % 2]
        aw_sd = aw_sds[qb % 2]
        x_sd = x_sds[qb % 2]
        if qb == 0:
            wait("sync", "sat", ctr["sat"])
            load_qb(0)
            aw_sd, x_sd = aw_sds[0], x_sds[0]
            if NTB > 1:
                load_qb(1)
        elif qb + 1 < NTB:
            wait("sync", "pe", wo_prj_last)
            wait("sync", "sout", 16 * qb)
            load_qb(qb + 1)
        if dbg:
            wait("act", "sat", ctr["sat"])
            wait("dve", "sat", ctr["sat"])
        a2vs = []
        prj = None
        for oc in range(NOC):
            acc = ps_s[:, (oc % 2) * 512:(oc % 2) * 512 + 512]
            if oc == 0:
                wait("pe", aw_sem[qb % 2], aw_sd)
                wait("pe", "swc%d" % wo_par, wo_cast)
                wait("pe", "dve", max(po_war or 0, prev_fev or 0) or None)
                if qb == 0:
                    wait("pe", "act", ctr["act"])
            if oc >= 2:
                wait("pe", "act", a2vs[oc - 2])
            for fc in range(NFC):
                prj = mm(acc[:, :],
                         wbuf[wo_par][:, fc * D + oc * 128: fc * D + (oc + 1) * 128],
                         awo[:, fc * 512:(fc + 1) * 512], fc == 0, fc == NFC - 1,
                         inc="pe" if fc == NFC - 1 else None)
            wo_prj_last = prj
            wait("act", "pe", prj)
            a2vs.append(op("act", lambda e, o=oc, a=acc: e.copy(
                a2f[:, o * 512:(o + 1) * 512], a[:, :]), inc="act"))
        a2v = a2vs[-1]
        if dbg:
            wait("sync", "act", a2v)
            dma("sync", dbgA[:, qb * 512:(qb + 1) * 512]
                .rearrange("(c p) t -> p c t", p=128),
                a2f[:, :].rearrange("p (c t) -> p c t", t=512), "sat")
        bc2m = ps_s[:, 0:512]
        bc2r = ps_s[:, 512:1024]
        bcv2 = layer_norm_stats(a2f, bc2m, bc2r, pre_act=a2v, pre_pe=prj,
                                pre_dve=prev_fev)
        wait("dve", "pe", bcv2)
        wait("dve", x_sem[qb % 2], x_sd)
        rv = None
        for fc in range(NFC):
            op("dve", lambda e, f=fc, b=bc2m: e.tensor_tensor(
                t1s[:, :], a2f[:, f * 512:(f + 1) * 512], b, ALU.subtract))
            op("dve", lambda e, b=bc2r: e.tensor_tensor(t2s[:, :], t1s[:, :], b, ALU.mult))
            rv = op("dve", lambda e, f=fc, x=xqs: e.tensor_tensor(
                x[:, f * 512:(f + 1) * 512], x[:, f * 512:(f + 1) * 512],
                t2s[:, :], ALU.add), inc="dve")
        if dbg:
            wait("sync", "dve", rv)
            dma("sync", dbgB[:, qb * 512:(qb + 1) * 512]
                .rearrange("(c p) t -> p c t", p=128),
                xqs.rearrange("p (c t) -> p c t", t=512), "sat")
        bc3m = ps_s[:, 1024:1536]
        bc3r = ps_s[:, 1536:2048]
        bcv3 = layer_norm_stats(xqs, bc3m, bc3r, pre_dve=rv)
        wait("dve", "pe", bcv3)
        ln3v = None
        lnv = {}
        for fc in range(NFC):
            op("dve", lambda e, f=fc, b=bc3m, x=xqs: e.tensor_tensor(
                t1s[:, :], x[:, f * 512:(f + 1) * 512], b, ALU.subtract))
            ln3v = op("dve", lambda e, f=fc, b=bc3r: e.tensor_tensor(
                lns[:, f * 512:(f + 1) * 512], t1s[:, :], b, ALU.mult),
                inc="dve")
            lnv[fc] = ln3v
        if dbg:
            wait("sync", "dve", ln3v)
            dma("sync", dbgC[:, qb * 512:(qb + 1) * 512]
                .rearrange("(c p) t -> p c t", p=128),
                lns[:, :].rearrange("p (c t) -> p c t", t=512), "sat")
        fev = None
        for half in range(2):
            if dbg:
                wait("act", "sat", ctr["sat"])
            gel = {}
            for idx, hc in enumerate(range(NHC)):
                hci = hc
                k = idx % 4
                wait("sync", "pe", s1_war[k])
                s1_sd = dma("sync",
                            s1b[k][:, :].rearrange("p (c h) -> p c h", h=128),
                            sgn1[:, hc * 128:(hc + 1) * 128]
                            .rearrange("(c p) h -> p c h", p=128), "s1%d" % k)
                acc = ps_s[:, (idx % 4) * 512:(idx % 4) * 512 + 512]
                wait("pe", "s1%d" % k, s1_sd)
                if idx < 4:
                    wait("pe", "dve", max(alpha_done, fev or 0))
                    if half == 0:
                        wait("pe", "act", a2v)
                wait("pe", "act", gel.get(idx - 4))
                prj = None
                for fc in range(NFC):
                    if idx == 0 and half == 0:
                        wait("pe", "dve", lnv[fc])
                    prj = mm(acc[:, 0:256],
                             s1b[k][:, fc * 128:(fc + 1) * 128],
                             lns[:, fc * 512 + half * 256: fc * 512 + half * 256 + 256],
                             fc == 0, fc == NFC - 1,
                             inc="pe" if fc == NFC - 1 else None)
                s1_war[k] = prj
                wait("act", "pe", prj)
                if dbg and qb == 0 and half == 0 and hc in (29, 31):
                    pcp = op("act", lambda e, a=acc, h=hc: e.copy(
                        atu_t[:, (0 if h == 31 else 256):(256 if h == 31 else 512)],
                        a[:, 0:256]), inc="act")
                    if hc == 29 or hc == 31:
                        dbg_pcp = pcp
                dst = phb[:, hc * 256:(hc + 1) * 256]
                gfn = AF.Identity if gelu_copy else AF.Gelu
                gel[hci] = op("act", lambda e, h=hc, a=acc, d=dst, gfn=gfn:
                    e.activation(d, a[:, 0:256], gfn,
                                 bias=b1s[:, h:h + 1], scale=a1c[:, h:h + 1]),
                    inc="act")
            if dbg and qb == 0 and half == 0:
                wait("sync", "act", dbg_pcp)
                dma("sync", dbgP[:, :], atu_t[:, :], "sat")
            if dbg:
                wait("sync", "act", max(gel.values()))
                dma("sync", dbgG[:, qb * 512 + half * 256: qb * 512 + half * 256 + 256]
                    .rearrange("(c p) t -> p c t", p=128),
                    phb[:, 0:NHC * 256].rearrange("p (c t) -> p c t", t=256), "sat")
            f2_tiles = [ps_s[:, 0:256], ps_s[:, 512:768], ps_s[:, 1024:1280],
                        ps_s[:, 1536:1792], ps_s[:, 2048:2304],
                        ps_s[:, 2560:2816], ps_o[:, 0:256], ps_d[:, 0:256]]
            f2 = None
            for hc in range(NHC):
                k = hc % 4
                wait("sync", "pe", s2_war[k])
                s2_sd = dma("sync", s2b[k][:, :],
                            sgn2[hc * 128:(hc + 1) * 128, :], "s2%d" % k)
                wait("pe", "s2%d" % k, s2_sd)
                wait("pe", "act", gel[hc])
                if hc == 0:
                    wait("pe", "act", max(gel.values()))
                    wait("pe", "dve", fev)
                for oc in range(NOC):
                    f2 = mm(f2_tiles[oc],
                            s2b[k][:, oc * 128:(oc + 1) * 128],
                            phb[:, hc * 256:(hc + 1) * 256],
                            hc == 0, hc == NHC - 1, sgc=True,
                            inc="pe" if oc == NOC - 1 else None)
                s2_war[k] = f2
            ffn2_last = f2
            wait("dve", "pe", f2)
            if dbg:
                wait("dve", "sat", ctr["sat"])
            for oc in range(NOC):
                fev = op("dve", lambda e, o=oc, h=half: e.tensor_scalar(
                    a2f[:, o * 512 + h * 256: o * 512 + h * 256 + 256],
                    f2_tiles[o],
                    a2c[:, o:o + 1], b2s[:, o:o + 1], ALU.mult, ALU.add),
                    inc="dve")
        prev_fev = fev
        if dbg:
            wait("sync", "dve", fev)
            dma("sync", dbgE[:, qb * 512:(qb + 1) * 512]
                .rearrange("(c p) t -> p c t", p=128),
                a2f[:, :].rearrange("p (c t) -> p c t", t=512), "sat")
        if dbg:
            wait("dve", "sat", ctr["sat"])
        bc4m = ps_s[:, 1024:1536]
        bc4r = ps_s[:, 1536:2048]
        bcv4 = layer_norm_stats(a2f, bc4m, bc4r, pre_dve=fev)
        wait("dve", "pe", bcv4)
        for fc in range(NFC):
            op("dve", lambda e, f=fc, b=bc4m: e.tensor_tensor(
                t1s[:, :], a2f[:, f * 512:(f + 1) * 512], b, ALU.subtract))
            op("dve", lambda e, b=bc4r: e.tensor_tensor(t2s[:, :], t1s[:, :], b,
                                                        ALU.mult))
            y_dve = op("dve", lambda e, f=fc, x=xqs: e.tensor_tensor(
                x[:, f * 512:(f + 1) * 512], x[:, f * 512:(f + 1) * 512],
                t2s[:, :], ALU.add), inc="dve")
        wait("sync", "dve", y_dve)
        dma("sync", yT[:, qb * 512:(qb + 1) * 512]
            .rearrange("(c p) t -> p c t", p=128),
            xqs.rearrange("p (c t) -> p c t", t=512), "sout")

    if dbg:
        wait("sync", "dve", ctr["dve"])
        wait("sync", "act", ctr["act"])
        dma("sync", dbgW2[:, 0:NHC], a1c[:, 0:NHC], "sat")
        dma("sync", dbgW2[:, NHC:2 * NHC], b1s[:, 0:NHC], "sat")
        dma("sync", dbgW2[:, 2 * NHC:2 * NHC + NOC], a2c[:, 0:NOC], "sat")
        dma("sync", dbgW2[:, 2 * NHC + NOC:], b2s[:, 0:NOC], "sat")
    wait("sync", "sout", 16 * NTB)

    with nc.Block() as block:
        @block.tensor
        def _(e):
            for f in steps["pe"]:
                f(e)

        @block.scalar
        def _(e):
            for f in steps["act"]:
                f(e)

        @block.vector
        def _(e):
            for f in steps["dve"]:
                f(e)

        @block.gpsimd
        def _(e):
            for f in steps["pool"]:
                f(e)

        @block.sync
        def _(e):
            for f in steps["sync"]:
                f(e)

    ctx.close()
    return nc


def _stage_inputs(cfg, x, W_q, W_k, W_v, W_o, w1, b1, w2, b2):
    D, LPC, HID = cfg["D"], cfg["LPC"], cfg["HID"]
    NH, NHC, NOC = cfg["NH"], cfg["NHC"], cfg["NOC"]
    f32 = np.float32
    perm = np.concatenate([
        np.concatenate([np.arange(h * 64, (h + 1) * 64, 2),
                        np.arange(h * 64 + 1, (h + 1) * 64, 2)])
        for h in range(NH)])
    com = {
        "wqT": np.ascontiguousarray(np.asarray(W_q, f32).T[:, perm]),
        "wkT": np.ascontiguousarray(np.asarray(W_k, f32).T[:, perm]),
        "wvT": np.ascontiguousarray(np.asarray(W_v, f32).T),
        "woT": np.ascontiguousarray(np.asarray(W_o, f32).T),
        "w1T": np.ascontiguousarray(np.asarray(w1, f32).T),
        "w1n": np.ascontiguousarray(np.asarray(w1, f32)),
        "w2T": np.ascontiguousarray(np.asarray(w2, f32).T),
        "w2n": np.ascontiguousarray(np.asarray(w2, f32)),
        "b1c": np.ascontiguousarray(np.asarray(b1, f32).reshape(NHC, 128).T),
        "b2c": np.ascontiguousarray(np.asarray(b2, f32).reshape(NOC, 128).T),
    }
    Pm = np.zeros((128, 128), f32)
    for r in range(128):
        Pm[r, r ^ 32] = 1.0
    com["Pm"] = Pm
    inv = (BASE ** (-(np.arange(32, dtype=np.float64)) / 32.0))
    xnp = np.asarray(x, f32)
    in_maps = []
    for c in range(8):
        b, q0 = c // 4, (c % 4) * LPC
        pos = (q0 + np.arange(LPC, dtype=np.float64))
        ang = np.outer(inv, pos)
        cc = np.cos(ang).astype(f32)
        ss = np.sin(ang).astype(f32)
        CC = np.tile(cc, (4, 1))
        SS = np.concatenate([-ss, ss, -ss, ss], axis=0)
        m = dict(com)
        m["CC"] = np.ascontiguousarray(CC)
        m["SS"] = np.ascontiguousarray(SS)
        m["xT"] = np.ascontiguousarray(xnp[b, q0:q0 + LPC, :].T)
        in_maps.append(m)
    return in_maps


def kernel(L0_attn_x, L0_attn_W_q, L0_attn_W_k, L0_attn_W_v, L0_attn_W_o,
           L0_ffn_w1, L0_ffn_b1, L0_ffn_w2, L0_ffn_b2):
    cfg = make_cfg()
    in_maps = _stage_inputs(cfg, L0_attn_x, L0_attn_W_q, L0_attn_W_k, L0_attn_W_v,
                            L0_attn_W_o, L0_ffn_w1, L0_ffn_b1, L0_ffn_w2, L0_ffn_b2)
    nc = build_program(cfg, detect_races=False)
    res = run_bass_kernel_spmd(nc, in_maps, core_ids=list(range(8)))
    LPC = cfg["LPC"]
    out = np.empty((2, 4 * LPC, cfg["D"]), np.float32)
    for c in range(8):
        b, q0 = c // 4, (c % 4) * LPC
        out[b, q0:q0 + LPC, :] = res.results[c]["yT"].T
    return out


# version 20
# speedup vs baseline: 1.0197x; 1.0110x over previous
import math
import numpy as np

import concourse.bass as bass
import concourse.mybir as mybir
from concourse.bass_utils import run_bass_kernel_spmd

F32 = mybir.dt.float32
BF16 = mybir.dt.bfloat16
AF = mybir.ActivationFunctionType
ALU = mybir.AluOpType

BASE = 10000.0


def make_cfg(D=1024, LPC=2048, HID=4096):
    assert D % 128 == 0 and LPC % 512 == 0 and HID % 1024 == 0
    return dict(
        D=D, LPC=LPC, HID=HID,
        NH=D // 64, NHP=D // 128, NFC=D // 128,
        NTB=LPC // 512, NKC=4 * LPC // 128,
        NHC=HID // 128, NOC=D // 128, L=4 * LPC,
    )


def build_program(cfg, detect_races=True, dbg=False, gelu_copy=False):
    D, LPC, HID = cfg["D"], cfg["LPC"], cfg["HID"]
    NFC, NHP, NTB, NKC = cfg["NFC"], cfg["NHP"], cfg["NTB"], cfg["NKC"]
    NHC, NOC = cfg["NHC"], cfg["NOC"]
    from contextlib import ExitStack

    nc = bass.Bass("TRN2", target_bir_lowering=False,
                   detect_race_conditions=detect_races)

    dt = nc.dram_tensor
    xT = dt("xT", [D, LPC], F32, kind="ExternalInput")
    wqT = dt("wqT", [D, D], F32, kind="ExternalInput")
    wkT = dt("wkT", [D, D], F32, kind="ExternalInput")
    wvT = dt("wvT", [D, D], F32, kind="ExternalInput")
    woT = dt("woT", [D, D], F32, kind="ExternalInput")
    w1T = dt("w1T", [D, HID], F32, kind="ExternalInput")
    w1n = dt("w1n", [HID, D], F32, kind="ExternalInput")
    w2T = dt("w2T", [HID, D], F32, kind="ExternalInput")
    w2n = dt("w2n", [D, HID], F32, kind="ExternalInput")
    b1c = dt("b1c", [128, NHC], F32, kind="ExternalInput")
    b2c = dt("b2c", [128, NOC], F32, kind="ExternalInput")
    CCd = dt("CC", [128, LPC], F32, kind="ExternalInput")
    SSd = dt("SS", [128, LPC], F32, kind="ExternalInput")
    Pmd = dt("Pm", [128, 128], F32, kind="ExternalInput")
    yT = dt("yT", [D, LPC], F32, kind="ExternalOutput")

    dk = dict(kind="ExternalOutput") if dbg else {}
    agk_i = dt("agk_i", [D, LPC], BF16)
    agk_o = dt("agk_o", [4 * D, LPC], BF16)
    agv_i = dt("agv_i", [LPC, D], BF16)
    agv_o = dt("agv_o", [4 * LPC, D], BF16)
    sgn1 = dt("sgn1", [D, HID], BF16, **dk)
    sgn2 = dt("sgn2", [HID, D], BF16, **dk)
    atu_d = dt("atu_d", [D, LPC], BF16, **dk)
    if dbg:
        dbgK = dt("dbgK", [D, LPC], BF16, kind="ExternalOutput")
        dbgV = dt("dbgV", [LPC, D], BF16, kind="ExternalOutput")
        dbgA = dt("dbgA", [D, LPC], F32, kind="ExternalOutput")
        dbgB = dt("dbgB", [D, LPC], F32, kind="ExternalOutput")
        dbgE = dt("dbgE", [D, LPC], F32, kind="ExternalOutput")
        dbgC = dt("dbgC", [D, LPC], BF16, kind="ExternalOutput")
        dbgG = dt("dbgG", [HID, LPC], BF16, kind="ExternalOutput")
        dbgW = dt("dbgW", [128, 2 * (NHC + NOC)], F32, kind="ExternalOutput")
        dbgP = dt("dbgP", [128, 512], BF16, kind="ExternalOutput")
        dbgW2 = dt("dbgW2", [128, 2 * (NHC + NOC)], F32, kind="ExternalOutput")

    ctx = ExitStack()

    sb = nc.alloc_sbuf_tensor
    qrope = sb("qrope", [128, NHP * LPC], BF16)
    lns = sb("lns", [128, NFC * 512], BF16)
    xq = sb("xq", [128, NFC * 512], F32)
    guard1 = sb("guard1", [128, 256], F32)
    a2f = sb("a2f", [128, NFC * 512], F32)
    guard2 = sb("guard2", [128, 256], F32)
    wbuf = [sb("wbuf0", [128, NFC * D], BF16), sb("wbuf1", [128, NFC * D], BF16)]
    pt = sb("pt", [128, 3072], BF16)
    atu_t = sb("atu_t", [128, 512], BF16)
    den_s = sb("den_s", [128, 512], F32)
    recin = sb("recin", [128, 512], F32)
    recb = sb("recb", [128, 512], F32)
    rows = sb("rows", [1, 2048], F32)
    onesD = sb("onesD", [128, 1], F32)
    onesb = sb("onesb", [128, 1], BF16)
    onesr = sb("onesr", [1, 64], F32)
    a1c = sb("a1c", [128, NHC + 2], F32)
    a2c = sb("a2c", [128, NOC + 2], F32)
    b1s = sb("b1s", [128, NHC + 2], F32)
    b2s = sb("b2s", [128, NOC + 2], F32)
    t1s = sb("t1s", [128, 512], F32)
    t2s = sb("t2s", [128, 512], F32)
    phb = sb("phb", [128, max(NHC * 256, NFC * 512)], BF16)
    guard3 = sb("guard3", [128, 256], BF16)
    s1b = [sb("s1b%d" % i, [128, NFC * 128], BF16) for i in range(4)]
    s2b = [sb("s2b%d" % i, [128, D], BF16) for i in range(4)]

    r1 = ExitStack()
    CCs = r1.enter_context(nc.sbuf_tensor("CCs", [128, LPC], F32))
    SSs = r1.enter_context(nc.sbuf_tensor("SSs", [128, LPC], F32))
    Pms = r1.enter_context(nc.sbuf_tensor("Pms", [128, 128], F32))
    wst = [r1.enter_context(nc.sbuf_tensor("wst0", [128, 1024], F32)),
           r1.enter_context(nc.sbuf_tensor("wst1", [128, 1024], F32))]
    qpre = [r1.enter_context(nc.sbuf_tensor("qpre%d" % i, [128, 512], F32))
            for i in range(3)]
    sgs = [r1.enter_context(nc.sbuf_tensor("sgs0", [128, 512], BF16)),
           r1.enter_context(nc.sbuf_tensor("sgs1", [128, 512], BF16))]
    krt = [r1.enter_context(nc.sbuf_tensor("krt0", [128, 512], BF16)),
           r1.enter_context(nc.sbuf_tensor("krt1", [128, 512], BF16))]
    vlc = [r1.enter_context(nc.sbuf_tensor("vlc0", [128, D], BF16)),
           r1.enter_context(nc.sbuf_tensor("vlc1", [128, D], BF16))]

    ps = nc.alloc_psum_tensor
    ps_s = ps("ps_s", [128, 3072], F32)
    ps_o = ps("ps_o", [128, 512], F32)
    ps_d = ps("ps_d", [128, 512], F32)
    ps_g1 = ps_s[:, 0:512]
    ps_g2 = ps_s[:, 512:1024]
    bcA = ps_s[:, 2048:2560]
    bcB = ps_s[:, 2560:3072]
    sw_slots = [bcA, bcB]

    ENGS = ("pe", "act", "dve", "pool", "sync")
    steps = {k: [] for k in ENGS}
    sems, ctr = {}, {}
    for name in ("sc", "sw0", "sw1", "sx", "sao", "skt", "svt",
                 "skt1", "skt2", "skt3", "svt1", "svt2", "svt3",
                 "s10", "s11", "s12", "s13", "s20", "s21", "s22", "s23",
                 "kr0", "kr1", "vl0", "vl1", "sg0", "sg1", "swc0", "swc1",
                 "scc", "pe", "act", "dve", "sat", "sout"):
        sems[name] = ctx.enter_context(nc.semaphore(name))
        ctr[name] = 0

    def wait(eng, sem, val):
        if val is None or val <= 0:
            return
        steps[eng].append(lambda e, s=sems[sem], v=val: e.wait_ge(s, v))

    def op(eng, fn, inc=None, n=1):
        if inc is None:
            steps[eng].append(fn)
            return None
        ctr[inc] += n
        steps[eng].append(lambda e, fn=fn, s=sems[inc], n=n: fn(e).then_inc(s, n))
        return ctr[inc]

    def mm(out, lhsT, rhs, start, stop, tp=None, inc=None, sgc=False):
        return op("pe", lambda e, o=out, l=lhsT, r=rhs, s=start, t=stop, p=tp,
                  g=sgc: e.matmul(o, l, r, start=s, stop=t, tile_position=p,
                                  skip_group_check=g), inc=inc)

    def dma(eng, dst, src, inc):
        return op(eng, lambda e, d=dst, s=src: e.dma_start(out=d, in_=s), inc=inc, n=16)

    dma("sync", CCs[:, :], CCd[:, :], "sc")
    dma("sync", SSs[:, :], SSd[:, :], "sc")
    dma("sync", Pms[:, :], Pmd[:, :], "sc")
    dma("sync", b1s[:, 0:NHC], b1c[:, :], "sc")
    v_consts = dma("sync", b2s[:, 0:NOC], b2c[:, :], "sc")
    for _eng in ("pe", "act", "dve"):
        wait(_eng, "sc", v_consts)

    op("dve", lambda e: e.memset(a1c[:, :], 0.0))
    op("dve", lambda e: e.memset(a2c[:, :], 0.0))
    op("dve", lambda e: e.memset(onesD[:, :], 1.0 / D))
    op("dve", lambda e: e.memset(onesb[:, :], 1.0))
    op("dve", lambda e: e.memset(onesr[:, :], 1.0))
    op("dve", lambda e: e.memset(ps_d[:, :], 0.0))
    op("dve", lambda e: e.memset(ps_s[:, :], 0.0))
    v_init = op("dve", lambda e: e.memset(ps_o[:, :], 0.0), inc="dve")

    def layer_norm_stats(src, bc1, bc2, pre_pe=None, pre_dve=None, pre_act=None,
                         pre_sd=None):
        if pre_sd:
            wait("pe", pre_sd[0], pre_sd[1])
        wait("pe", "pe", pre_pe)
        wait("pe", "act", pre_act)
        wait("pe", "dve", pre_dve)
        for fc in range(NFC):
            mm(ps_d[0:1, :], onesD[:, :], src[:, fc * 512:(fc + 1) * 512],
               fc == 0, fc == NFC - 1, tp=(0, 0), sgc=True)
        msqv = {}
        if pre_sd:
            wait("dve", pre_sd[0], pre_sd[1])
        wait("dve", "act", pre_act)
        wait("dve", "dve", pre_dve)
        for fc in range(NFC):
            tscr = t1s if fc % 2 == 0 else t2s
            if fc >= 2:
                wait("dve", "pe", msqv[fc - 2])
            sqv = op("dve", lambda e, t=tscr, s=src, f=fc:
                     e.tensor_tensor(t[:, :], s[:, f * 512:(f + 1) * 512],
                                     s[:, f * 512:(f + 1) * 512], ALU.mult),
                     inc="dve")
            wait("pe", "dve", sqv)
            msqv[fc] = mm(ps_d[64:65, :], onesD[:, :], tscr[:, :],
                          fc == 0, fc == NFC - 1, tp=(0, 64), inc="pe", sgc=True)
        return ln_tail(bc1, bc2, msqv[NFC - 1])

    def ln_tail(bc1, bc2, last_msq):
        wait("dve", "pe", last_msq)
        op("dve", lambda e: e.tensor_copy(rows[0:1, 0:512], ps_d[0:1, :]))
        op("dve", lambda e: e.tensor_copy(rows[0:1, 1024:1536], ps_d[64:65, :]))
        op("dve", lambda e: e.tensor_tensor(rows[0:1, 1536:2048], rows[0:1, 0:512],
                                            rows[0:1, 0:512], ALU.mult))
        op("dve", lambda e: e.tensor_tensor(
            rows[0:1, 1024:1536], rows[0:1, 1024:1536], rows[0:1, 1536:2048],
            ALU.subtract))
        dv = op("dve", lambda e: e.tensor_scalar_add(
            rows[0:1, 1024:1536], rows[0:1, 1024:1536], 1e-5), inc="dve")
        wait("act", "dve", dv)
        op("act", lambda e: e.activation(rows[0:1, 1024:1536], rows[0:1, 1024:1536],
                                         AF.Ln))
        av = op("act", lambda e: e.activation(rows[0:1, 512:1024],
                                              rows[0:1, 1024:1536],
                                              AF.Exp, scale=-0.5), inc="act")
        wait("pe", "act", av)
        mm(bc1[0:64, :], onesr[:, :], rows[0:1, 0:512], True, True, tp=(0, 0))
        mm(bc1[64:128, :], onesr[:, :], rows[0:1, 0:512], True, True, tp=(0, 64))
        mm(bc2[0:64, :], onesr[:, :], rows[0:1, 512:1024], True, True, tp=(0, 0))
        return mm(bc2[64:128, :], onesr[:, :], rows[0:1, 512:1024], True, True,
                  tp=(0, 64), inc="pe")

    wst_war = [None, None]
    wbuf_war = [None, None]

    def load_weight(wdram, par):
        wait("pool", "pe", wbuf_war[par])
        return dma("pool",
                   wbuf[par][:, :].rearrange("p (c n) -> p c n", n=D),
                   wdram[:, :].rearrange("(c p) n -> p c n", p=128),
                   "swc%d" % par)

    widx = 0
    kv_stores = 0
    evac_war3 = [None, None, None]
    evac_war4 = [None, None, None, None]
    slot_t2 = [None, None]
    qp_rd_pe = [None, None, None]
    qp_rd_dve = [None, None, None]
    kst = [None, None]
    vst = [None, None]
    acc_war = None

    prev_bcv = None
    for tb in range(NTB):
        if tb > 0:
            wait("sync", "pe", prev_bcv)
            wait("sync", "dve", lns_ready)
        x_sd = dma("sync", xq[:, :].rearrange("p (c t) -> p c t", t=512),
                   xT[:, tb * 512:(tb + 1) * 512]
                   .rearrange("(c p) t -> p c t", p=128), "sx")
        bcv = layer_norm_stats(xq, bcA, bcB, pre_sd=("sx", x_sd),
                               pre_dve=max(v_init if tb == 0 else 0,
                                           acc_war or 0) or None)
        lastap = None
        wait("dve", "pe", bcv)
        for fc in range(NFC):
            op("dve", lambda e, f=fc: e.tensor_tensor(
                t1s[:, :], xq[:, f * 512:(f + 1) * 512], bcA, ALU.subtract))
            lastap = op("dve", lambda e, f=fc: e.tensor_tensor(
                lns[:, f * 512:(f + 1) * 512], t1s[:, :], bcB, ALU.mult),
                inc="dve")
        lns_ready = lastap
        prev_bcv = bcv

        for (wdram, is_q) in ((wqT, True), (wkT, False)):
            par = widx % 2
            widx += 1
            castv = load_weight(wdram, par)
            wbuf_war[par] = None
            prjs = {}

            def emit_swap(oc, tb=tb):
                qp = qpre[oc % 3]
                slot = sw_slots[oc % 2]
                wait("pe", "act", evac_war3[oc % 3])
                wait("pe", "dve", slot_t2[oc % 2])
                swv = mm(slot, Pms[:, :], qp[:, :], True, True, inc="pe")
                qp_rd_pe[oc % 3] = swv
                wait("dve", "act", evac_war3[oc % 3])
                dv1 = op("dve", lambda e, q=qp, t=tb: e.tensor_tensor(
                    t1s[:, :], q[:, :], CCs[:, t * 512:(t + 1) * 512], ALU.mult),
                    inc="dve")
                qp_rd_dve[oc % 3] = dv1
                wait("dve", "pe", swv)
                t2v = op("dve", lambda e, s=slot, t=tb: e.tensor_tensor(
                    t2s[:, :], s, SSs[:, t * 512:(t + 1) * 512], ALU.mult),
                    inc="dve")
                slot_t2[oc % 2] = t2v
                if is_q:
                    dst = qrope[:, oc * LPC + tb * 512: oc * LPC + (tb + 1) * 512]
                    op("dve", lambda e, d=dst: e.tensor_tensor(
                        d, t1s[:, :], t2s[:, :], ALU.add))
                else:
                    kb = krt[oc % 2]
                    wait("dve", "kr%d" % (oc % 2), kst[oc % 2])
                    rv = op("dve", lambda e, d=kb: e.tensor_tensor(
                        d[:, :], t1s[:, :], t2s[:, :], ALU.add), inc="dve")
                    wait("pool", "dve", rv)
                    kst[oc % 2] = dma(
                        "pool",
                        agk_i[oc * 128:(oc + 1) * 128, tb * 512:(tb + 1) * 512],
                        kb[:, :], "kr%d" % (oc % 2))

            for oc in range(NHP):
                acc = ps_s[:, (oc % 4) * 512:(oc % 4) * 512 + 512]
                if oc == 0:
                    wait("pe", "swc%d" % par, castv)
                    wait("pe", "dve", lns_ready)
                wait("pe", "act", evac_war4[oc % 4])
                prj = None
                for fc in range(NFC):
                    prj = mm(acc,
                             wbuf[par][:, fc * D + oc * 128: fc * D + (oc + 1) * 128],
                             lns[:, fc * 512:(fc + 1) * 512],
                             fc == 0, fc == NFC - 1,
                             inc="pe" if fc == NFC - 1 else None)
                prjs[oc] = prj
                wbuf_war[par] = prj
                qp = qpre[oc % 3]
                wait("act", "pe", prj)
                wait("act", "pe", qp_rd_pe[oc % 3])
                wait("act", "dve", qp_rd_dve[oc % 3])
                ev = op("act", lambda e, q=qp, a=acc: e.copy(q[:, :], a[:, :]),
                        inc="act")
                evac_war3[oc % 3] = ev
                evac_war4[oc % 4] = ev
                if oc >= 1:
                    emit_swap(oc - 1)
            emit_swap(NHP - 1)

        par = widx % 2
        widx += 1
        castv = load_weight(wvT, par)
        wbuf_war[par] = None
        for tc in range(4):
            vb = vlc[tc % 2]
            prj = None
            if tc == 0:
                wait("pe", "swc%d" % par, castv)
                wait("pe", "dve", lns_ready)
            wait("pe", "dve", acc_war)
            vw = min(512, D)
            if tc == 0:
                wait("pe", "act", max(v for v in evac_war4 if v) if any(evac_war4) else None)
            for oh in range(max(1, D // 512)):
                accv = ps_g1 if oh % 2 == 0 else ps_g2
                for fc in range(NFC):
                    prj = mm(accv[:, 0:vw],
                             lns[:, fc * 512 + tc * 128: fc * 512 + (tc + 1) * 128],
                             wbuf[par][:, fc * D + oh * vw: fc * D + (oh + 1) * vw],
                             fc == 0, fc == NFC - 1,
                             inc="pe" if fc == NFC - 1 else None)
            wbuf_war[par] = prj
            wait("dve", "pe", prj)
            wait("dve", "vl%d" % (tc % 2), vst[tc % 2])
            for oh in range(max(1, D // 512)):
                accv = ps_g1 if oh % 2 == 0 else ps_g2
                vv = op("dve", lambda e, v=vb, a=accv, o=oh, vw=vw: e.tensor_copy(
                    v[:, o * vw:(o + 1) * vw], a[:, 0:vw]), inc="dve")
            acc_war = vv
            wait("pool", "dve", vv)
            vst[tc % 2] = dma(
                "pool", agv_i[(tb * 4 + tc) * 128:(tb * 4 + tc + 1) * 128, :],
                vb[:, :], "vl%d" % (tc % 2))

    wo_par = widx % 2
    widx += 1
    wo_cast = load_weight(woT, wo_par)
    wbuf_war[wo_par] = None

    r1_marks = dict(ctr)
    sgstage = [xq[:, 0:1024], xq[:, 1024:2048]]
    sgout = [s2b[2][:, 0:512], s2b[3][:, 0:512]]
    wait("sync", "pe", ctr["pe"])
    wait("sync", "dve", ctr["dve"])
    sg_war = [None, None]
    i = 0
    sg_hist = {}
    for (src_d, dst_d, RP, CT) in ((w1T, sgn1, NFC, HID), (w2T, sgn2, NHC, D)):
        CJ, CW = max(1, CT // 512), min(512, CT)
        for rp in range(RP):
            for j in range(CJ):
                k = i % 2
                wait("sync", "act", sg_hist.get(i - 2))
                ld = dma("sync", sgstage[k][:, 0:CW],
                         src_d[rp * 128:(rp + 1) * 128, j * CW:(j + 1) * CW],
                         "sw%d" % k)
                wait("act", "sw%d" % k, ld)
                wait("act", "sg%d" % k, sg_war[k])
                sv = op("act", lambda e, k=k, CW=CW: e.sign(sgout[k][:, 0:CW],
                                                            sgstage[k][:, 0:CW]),
                        inc="act")
                sg_hist[i] = sv
                wait("pool", "act", sv)
                sg_war[k] = dma("pool",
                                dst_d[rp * 128:(rp + 1) * 128, j * CW:(j + 1) * CW],
                                sgout[k][:, 0:CW], "sg%d" % k)
                i += 1
    wait("sync", "act", ctr["act"])
    astage = [xq[:, 2048:3072], xq[:, 3072:4096]]
    ab_hist = {}
    ai = 0
    prev_abs = None
    for (src_d, col_t, NCH, DEN) in ((w1n, a1c, NHC, D), (w2n, a2c, NOC, HID)):
        pieces = max(1, DEN // 1024)
        pw = min(DEN, 1024)
        for ch in range(NCH):
            for pz in range(pieces):
                k = ai % 2
                wait("sync", "act", ab_hist.get(ai - 2))
                ldv = dma("sync", astage[k][:, 0:pw],
                          src_d[ch * 128:(ch + 1) * 128, pz * pw:(pz + 1) * pw],
                          "sw%d" % k)
                wait("act", "sw%d" % k, ldv)
                dst_col = col_t[:, NCH:NCH + 1] if pieces > 1 else col_t[:, ch:ch + 1]
                prev_abs = op("act", lambda e, pw=pw, dc=dst_col, k=k: e.activation(
                    s1b[k][:, 0:pw], astage[k][:, 0:pw], AF.Abs, accum_out=dc),
                    inc="act")
                ab_hist[ai] = prev_abs
                ai += 1
                if pieces > 1:
                    wait("dve", "act", prev_abs)
                    if pz == 0:
                        op("dve", lambda e, ct=col_t, c=ch, n=NCH: e.tensor_copy(
                            ct[:, c:c + 1], ct[:, n:n + 1]))
                    else:
                        op("dve", lambda e, ct=col_t, c=ch, n=NCH: e.tensor_tensor(
                            ct[:, c:c + 1], ct[:, c:c + 1], ct[:, n:n + 1], ALU.add))
                    prev_dve = op("dve", lambda e, ct=col_t, c=ch: e.tensor_copy(
                        t2s[:, 0:1], ct[:, c:c + 1]), inc="dve")
                    wait("act", "dve", prev_dve)
        wait("dve", "act", prev_abs)
        op("dve", lambda e, ct=col_t, n=NCH, d=DEN: e.tensor_scalar_mul(
            ct[:, 0:n], ct[:, 0:n], 1.0 / d))
    alpha_act_mark = ctr["act"]
    alpha_done = op("dve", lambda e: e.tensor_copy(t1s[:, 0:1], a1c[:, 0:1]),
                    inc="dve")

    if dbg:
        for _s in ("kr0", "kr1", "vl0", "vl1"):
            wait("pool", _s, ctr[_s])
        dma("pool", dbgK[:, :], agk_i[:, :], "sg0")
        dma("pool", dbgV[:, :], agv_i[:, :], "sg0")
    rgs = [[0, 1, 2, 3], [4, 5, 6, 7]]
    for _s in ("kr0", "kr1"):
        wait("pool", _s, ctr[_s])
    op("pool", lambda e: e.collective_compute(
        "AllGather", ALU.bypass, replica_groups=rgs,
        ins=[agk_i.ap().opt()], outs=[agk_o.ap().opt()]), inc="scc", n=1)
    for _s in ("vl0", "vl1"):
        wait("pool", _s, ctr[_s])
    op("pool", lambda e: e.collective_compute(
        "AllGather", ALU.bypass, replica_groups=rgs,
        ins=[agv_i.ap().opt()], outs=[agv_o.ap().opt()]), inc="scc", n=1)

    if dbg:
        wait("sync", "dve", alpha_done)
        dma("sync", dbgW[:, 0:NHC], a1c[:, 0:NHC], "sat")
        dma("sync", dbgW[:, NHC:2 * NHC], b1s[:, 0:NHC], "sat")
        dma("sync", dbgW[:, 2 * NHC:2 * NHC + NOC], a2c[:, 0:NOC], "sat")
        dma("sync", dbgW[:, 2 * NHC + NOC:], b2s[:, 0:NOC], "sat")
    r1.close()
    ktb = sb("ktb", [128, 4 * LPC], BF16)
    vtb = sb("vtb", [128, 4 * LPC], BF16)

    exp_free = {0: None, 1: None, 2: None}
    pt_free = {0: None, 1: None, 2: None}
    attn_last_pe = None
    sat_last = None
    po_war = None
    pd_war = None

    RCH = NKC // 4
    ktsem = ["skt", "skt1", "skt2", "skt3"]
    vtsem = ["svt", "svt1", "svt2", "svt3"]
    kt_rv = [None] * 4
    vt_rv = [None] * 4

    def load_kv(hp, gate_scv=None, gate_pvs=None):
        for r in range(4):
            if gate_scv is not None:
                wait("sync", "pe", gate_scv[(r + 1) * RCH - 1])
            kt_rv[r] = dma("sync", ktb[:, r * LPC:(r + 1) * LPC],
                           agk_o[r * D + hp * 128: r * D + (hp + 1) * 128, :],
                           ktsem[r])
            if gate_pvs is not None:
                wait("sync", "pe", gate_pvs[(r + 1) * RCH - 1])
            for q4 in range(LPC // 512):
                base = (r * (LPC // 128) + q4 * 4) * 128
                vt_rv[r] = dma(
                    "sync",
                    vtb[:, base:base + 512].rearrange("p (c d) -> p c d", d=128),
                    agv_o[r * LPC + q4 * 512: r * LPC + (q4 + 1) * 512,
                          hp * 128:(hp + 1) * 128]
                    .rearrange("(c p) d -> p c d", p=128), vtsem[r])

    for hp in range(NHP):
        if hp == 0:
            wait("sync", "scc", 2)
            for _s in ("kr0", "kr1", "vl0", "vl1"):
                wait("sync", _s, r1_marks[_s])
            wait("sync", "dve", r1_marks["dve"])
            wait("sync", "act", r1_marks["act"])
            wait("sync", "pe", r1_marks["pe"])
            load_kv(0)

        for qb in range(NTB):
            qsl = slice(hp * LPC + qb * 512, hp * LPC + (qb + 1) * 512)
            denv = None
            scv = {}
            pvs = {}
            LKA = 2
            for ci in range(NKC + LKA):
                if ci < NKC:
                    c = ci
                    w = c % 3
                    wait("pe", "act", exp_free[w])
                    if qb == 0 and c % RCH == 0:
                        wait("pe", ktsem[c // RCH], kt_rv[c // RCH])
                    mm(ps_s[:, w * 1024: w * 1024 + 512],
                       ktb[0:64, c * 128:(c + 1) * 128], qrope[0:64, qsl],
                       True, True, tp=(0, 0))
                    scv[c] = mm(ps_s[:, w * 1024 + 512: w * 1024 + 1024],
                                ktb[64:128, c * 128:(c + 1) * 128],
                                qrope[64:128, qsl],
                                True, True, tp=(64, 0), inc="pe")
                if ci >= LKA:
                    c = ci - LKA
                    w = c % 3
                    wait("act", "pe", scv[c])
                    wait("act", "pe", pt_free[w])
                    expv = op("act", lambda e, w=w: e.activation(
                        pt[:, w * 1024:(w + 1) * 1024],
                        ps_s[:, w * 1024:(w + 1) * 1024],
                        AF.Exp, scale=0.125), inc="act")
                    exp_free[w] = expv
                    wait("pe", "act", expv)
                    if c == 0:
                        wait("pe", "dve", po_war)
                    if qb == 0 and c % RCH == 0:
                        wait("pe", vtsem[c // RCH], vt_rv[c // RCH])
                    mm(ps_o[0:64, :], vtb[:, c * 128: c * 128 + 64],
                       pt[:, w * 1024: w * 1024 + 512],
                       c == 0, c == NKC - 1, tp=(0, 0), sgc=True)
                    pvv = mm(ps_o[64:128, :], vtb[:, c * 128 + 64: c * 128 + 128],
                             pt[:, w * 1024 + 512: w * 1024 + 1024],
                             c == 0, c == NKC - 1, tp=(0, 64), inc="pe", sgc=True)
                    pt_free[w] = pvv
                    pvs[c] = pvv
                    if c % 2 == 1:
                        g = c // 2
                        wp = (c - 1) % 3
                        if g == 0:
                            wait("pe", "dve", pd_war)
                        st = (g == 0)
                        sp = (g == NKC // 2 - 1)
                        mm(ps_d[0:1, :], onesb[:, :],
                           pt[:, wp * 1024: wp * 1024 + 512], st, sp, tp=(0, 0),
                           sgc=True)
                        mm(ps_d[32:33, :], onesb[:, :],
                           pt[:, wp * 1024 + 512: wp * 1024 + 1024], st, sp,
                           tp=(0, 32), sgc=True)
                        mm(ps_d[64:65, :], onesb[:, :],
                           pt[:, w * 1024: w * 1024 + 512], st, sp,
                           tp=(0, 64), sgc=True)
                        denv = mm(ps_d[96:97, :], onesb[:, :],
                                  pt[:, w * 1024 + 512: w * 1024 + 1024], st, sp,
                                  tp=(0, 96), inc="pe", sgc=True)
                        pt_free[wp] = denv
                        pt_free[w] = denv
            attn_last_pe = ctr["pe"]
            if qb == NTB - 1 and hp + 1 < NHP:
                load_kv(hp + 1, gate_scv=scv, gate_pvs=pvs)
            wait("dve", "pe", denv)
            pd_war = op("dve", lambda e: e.tensor_copy(den_s[:, :], ps_d[:, :]),
                        inc="dve")
            op("dve", lambda e: e.tensor_copy(t1s[0:64, :], den_s[64:128, :]))
            op("dve", lambda e: e.tensor_tensor(
                recin[0:64, :], den_s[0:64, :], t1s[0:64, :], ALU.add))
            dnr = op("dve", lambda e: e.tensor_copy(recb[0:1, :], recin[32:33, :]),
                     inc="dve")
            wait("pe", "dve", dnr)
            mm(ps_d[0:64, :], onesr[:, :], recin[0:1, :], True, True, tp=(0, 0))
            bcv = mm(ps_d[64:128, :], onesr[:, :], recb[0:1, :], True, True,
                     tp=(0, 64), inc="pe")
            wait("dve", "pe", bcv)
            pd_war = op("dve", lambda e: e.tensor_copy(recin[:, :], ps_d[:, :]),
                        inc="dve")
            op("dve", lambda e: e.reciprocal(recb[:, :], recin[:, :]))
            wait("dve", "sat", sat_last)
            po_war = op("dve", lambda e: e.tensor_tensor(
                atu_t[:, :], ps_o[:, :], recb[:, :], ALU.mult), inc="dve")
            wait("sync", "dve", po_war)
            sat_last = dma("sync",
                           atu_d[hp * 128:(hp + 1) * 128, qb * 512:(qb + 1) * 512],
                           atu_t[:, :], "sat")

    awo_b = [vtb[:, 0:NFC * 512], vtb[:, NFC * 512:2 * NFC * 512]]
    xq_b = [xq[:, :], ktb[:, :].bitcast(F32)]
    aw_sds = [None, None]
    x_sds = [None, None]
    s1_war = [None, None, None, None]
    s2_war = [None, None, None, None]
    ffn2_last = None
    y_dve = None
    prev_fev = None
    wo_prj_last = None

    aw_sem = ["sao", "svt"]
    x_sem = ["sx", "skt"]

    def load_qb(j):
        aw_sds[j % 2] = dma("sync", awo_b[j % 2].rearrange("p (c t) -> p c t",
                                                           t=512),
                            atu_d[:, j * 512:(j + 1) * 512]
                            .rearrange("(c p) t -> p c t", p=128), aw_sem[j % 2])
        x_sds[j % 2] = dma("sync", xq_b[j % 2].rearrange("p (c t) -> p c t",
                                                         t=512),
                           xT[:, j * 512:(j + 1) * 512]
                           .rearrange("(c p) t -> p c t", p=128), x_sem[j % 2])

    for qb in range(NTB):
        awo = awo_b[qb % 2]
        xqs = xq_b[qb % 2]
        aw_sd = aw_sds[qb % 2]
        x_sd = x_sds[qb % 2]
        if qb == 0:
            wait("sync", "sat", ctr["sat"])
            load_qb(0)
            aw_sd, x_sd = aw_sds[0], x_sds[0]
            if NTB > 1:
                load_qb(1)
        elif qb + 1 < NTB:
            wait("sync", "pe", wo_prj_last)
            wait("sync", "sout", 16 * qb)
            load_qb(qb + 1)
        if dbg:
            wait("act", "sat", ctr["sat"])
            wait("dve", "sat", ctr["sat"])
        a2vs = []
        prj = None
        for oc in range(NOC):
            acc = ps_s[:, (oc % 2) * 512:(oc % 2) * 512 + 512]
            if oc == 0:
                wait("pe", aw_sem[qb % 2], aw_sd)
                wait("pe", "swc%d" % wo_par, wo_cast)
                wait("pe", "dve", max(po_war or 0, prev_fev or 0) or None)
                if qb == 0:
                    wait("pe", "act", ctr["act"])
            if oc >= 2:
                wait("pe", "act", a2vs[oc - 2])
            for fc in range(NFC):
                prj = mm(acc[:, :],
                         wbuf[wo_par][:, fc * D + oc * 128: fc * D + (oc + 1) * 128],
                         awo[:, fc * 512:(fc + 1) * 512], fc == 0, fc == NFC - 1,
                         inc="pe" if fc == NFC - 1 else None)
            wo_prj_last = prj
            wait("act", "pe", prj)
            a2vs.append(op("act", lambda e, o=oc, a=acc: e.copy(
                a2f[:, o * 512:(o + 1) * 512], a[:, :]), inc="act"))
        a2v = a2vs[-1]
        if dbg:
            wait("sync", "act", a2v)
            dma("sync", dbgA[:, qb * 512:(qb + 1) * 512]
                .rearrange("(c p) t -> p c t", p=128),
                a2f[:, :].rearrange("p (c t) -> p c t", t=512), "sat")
        bc2m = ps_s[:, 0:512]
        bc2r = ps_s[:, 512:1024]
        bcv2 = layer_norm_stats(a2f, bc2m, bc2r, pre_act=a2v, pre_pe=prj,
                                pre_dve=prev_fev)
        wait("dve", "pe", bcv2)
        wait("dve", x_sem[qb % 2], x_sd)
        bc3m = ps_s[:, 1024:1536]
        bc3r = ps_s[:, 1536:2048]
        rv = None
        msqv3 = {}
        for fc in range(NFC):
            op("dve", lambda e, f=fc, b=bc2m: e.tensor_tensor(
                den_s[:, :], a2f[:, f * 512:(f + 1) * 512], b, ALU.subtract))
            op("dve", lambda e, b=bc2r: e.tensor_tensor(den_s[:, :], den_s[:, :],
                                                        b, ALU.mult))
            rv = op("dve", lambda e, f=fc, x=xqs: e.tensor_tensor(
                x[:, f * 512:(f + 1) * 512], x[:, f * 512:(f + 1) * 512],
                den_s[:, :], ALU.add), inc="dve")
            wait("pe", "dve", rv)
            mm(ps_d[0:1, :], onesD[:, :], xqs[:, fc * 512:(fc + 1) * 512],
               fc == 0, fc == NFC - 1, tp=(0, 0), sgc=True)
            tscr = t1s if fc % 2 == 0 else t2s
            if fc >= 2:
                wait("dve", "pe", msqv3[fc - 2])
            sqv = op("dve", lambda e, t=tscr, f=fc, x=xqs: e.tensor_tensor(
                t[:, :], x[:, f * 512:(f + 1) * 512],
                x[:, f * 512:(f + 1) * 512], ALU.mult), inc="dve")
            wait("pe", "dve", sqv)
            msqv3[fc] = mm(ps_d[64:65, :], onesD[:, :], tscr[:, :],
                           fc == 0, fc == NFC - 1, tp=(0, 64), inc="pe", sgc=True)
        if dbg:
            wait("sync", "dve", rv)
            dma("sync", dbgB[:, qb * 512:(qb + 1) * 512]
                .rearrange("(c p) t -> p c t", p=128),
                xqs.rearrange("p (c t) -> p c t", t=512), "sat")
        bcv3 = ln_tail(bc3m, bc3r, msqv3[NFC - 1])
        wait("dve", "pe", bcv3)
        ln3v = None
        lnv = {}
        for fc in range(NFC):
            op("dve", lambda e, f=fc, b=bc3m, x=xqs: e.tensor_tensor(
                t1s[:, :], x[:, f * 512:(f + 1) * 512], b, ALU.subtract))
            ln3v = op("dve", lambda e, f=fc, b=bc3r: e.tensor_tensor(
                lns[:, f * 512:(f + 1) * 512], t1s[:, :], b, ALU.mult),
                inc="dve")
            lnv[fc] = ln3v
        if dbg:
            wait("sync", "dve", ln3v)
            dma("sync", dbgC[:, qb * 512:(qb + 1) * 512]
                .rearrange("(c p) t -> p c t", p=128),
                lns[:, :].rearrange("p (c t) -> p c t", t=512), "sat")
        fev = None
        for half in range(2):
            if dbg:
                wait("act", "sat", ctr["sat"])
            gel = {}
            for idx, hc in enumerate(range(NHC)):
                hci = hc
                k = idx % 4
                wait("sync", "pe", s1_war[k])
                s1_sd = dma("sync",
                            s1b[k][:, :].rearrange("p (c h) -> p c h", h=128),
                            sgn1[:, hc * 128:(hc + 1) * 128]
                            .rearrange("(c p) h -> p c h", p=128), "s1%d" % k)
                acc = ps_s[:, (idx % 4) * 512:(idx % 4) * 512 + 512]
                wait("pe", "s1%d" % k, s1_sd)
                if idx < 4:
                    wait("pe", "dve", max(alpha_done, fev or 0))
                    if half == 0:
                        wait("pe", "act", a2v)
                wait("pe", "act", gel.get(idx - 4))
                prj = None
                for fc in range(NFC):
                    if idx == 0 and half == 0:
                        wait("pe", "dve", lnv[fc])
                    prj = mm(acc[:, 0:256],
                             s1b[k][:, fc * 128:(fc + 1) * 128],
                             lns[:, fc * 512 + half * 256: fc * 512 + half * 256 + 256],
                             fc == 0, fc == NFC - 1,
                             inc="pe" if fc == NFC - 1 else None)
                s1_war[k] = prj
                wait("act", "pe", prj)
                if dbg and qb == 0 and half == 0 and hc in (29, 31):
                    pcp = op("act", lambda e, a=acc, h=hc: e.copy(
                        atu_t[:, (0 if h == 31 else 256):(256 if h == 31 else 512)],
                        a[:, 0:256]), inc="act")
                    if hc == 29 or hc == 31:
                        dbg_pcp = pcp
                dst = phb[:, hc * 256:(hc + 1) * 256]
                gfn = AF.Identity if gelu_copy else AF.Gelu
                gel[hci] = op("act", lambda e, h=hc, a=acc, d=dst, gfn=gfn:
                    e.activation(d, a[:, 0:256], gfn,
                                 bias=b1s[:, h:h + 1], scale=a1c[:, h:h + 1]),
                    inc="act")
            if dbg and qb == 0 and half == 0:
                wait("sync", "act", dbg_pcp)
                dma("sync", dbgP[:, :], atu_t[:, :], "sat")
            if dbg:
                wait("sync", "act", max(gel.values()))
                dma("sync", dbgG[:, qb * 512 + half * 256: qb * 512 + half * 256 + 256]
                    .rearrange("(c p) t -> p c t", p=128),
                    phb[:, 0:NHC * 256].rearrange("p (c t) -> p c t", t=256), "sat")
            f2_tiles = [ps_s[:, 0:256], ps_s[:, 512:768], ps_s[:, 1024:1280],
                        ps_s[:, 1536:1792], ps_s[:, 2048:2304],
                        ps_s[:, 2560:2816], ps_o[:, 0:256], ps_d[:, 0:256]]
            f2 = None
            for hc in range(NHC):
                k = hc % 4
                wait("sync", "pe", s2_war[k])
                s2_sd = dma("sync", s2b[k][:, :],
                            sgn2[hc * 128:(hc + 1) * 128, :], "s2%d" % k)
                wait("pe", "s2%d" % k, s2_sd)
                wait("pe", "act", gel[hc])
                if hc == 0:
                    wait("pe", "act", max(gel.values()))
                    wait("pe", "dve", fev)
                for oc in range(NOC):
                    f2 = mm(f2_tiles[oc],
                            s2b[k][:, oc * 128:(oc + 1) * 128],
                            phb[:, hc * 256:(hc + 1) * 256],
                            hc == 0, hc == NHC - 1, sgc=True,
                            inc="pe" if oc == NOC - 1 else None)
                s2_war[k] = f2
            ffn2_last = f2
            wait("dve", "pe", f2)
            if dbg:
                wait("dve", "sat", ctr["sat"])
            for oc in range(NOC):
                fev = op("dve", lambda e, o=oc, h=half: e.tensor_scalar(
                    a2f[:, o * 512 + h * 256: o * 512 + h * 256 + 256],
                    f2_tiles[o],
                    a2c[:, o:o + 1], b2s[:, o:o + 1], ALU.mult, ALU.add),
                    inc="dve")
        prev_fev = fev
        if dbg:
            wait("sync", "dve", fev)
            dma("sync", dbgE[:, qb * 512:(qb + 1) * 512]
                .rearrange("(c p) t -> p c t", p=128),
                a2f[:, :].rearrange("p (c t) -> p c t", t=512), "sat")
        if dbg:
            wait("dve", "sat", ctr["sat"])
        bc4m = ps_s[:, 1024:1536]
        bc4r = ps_s[:, 1536:2048]
        bcv4 = layer_norm_stats(a2f, bc4m, bc4r, pre_dve=fev)
        wait("dve", "pe", bcv4)
        for fc in range(NFC):
            op("dve", lambda e, f=fc, b=bc4m: e.tensor_tensor(
                t1s[:, :], a2f[:, f * 512:(f + 1) * 512], b, ALU.subtract))
            op("dve", lambda e, b=bc4r: e.tensor_tensor(t2s[:, :], t1s[:, :], b,
                                                        ALU.mult))
            y_dve = op("dve", lambda e, f=fc, x=xqs: e.tensor_tensor(
                x[:, f * 512:(f + 1) * 512], x[:, f * 512:(f + 1) * 512],
                t2s[:, :], ALU.add), inc="dve")
        wait("sync", "dve", y_dve)
        dma("sync", yT[:, qb * 512:(qb + 1) * 512]
            .rearrange("(c p) t -> p c t", p=128),
            xqs.rearrange("p (c t) -> p c t", t=512), "sout")

    if dbg:
        wait("sync", "dve", ctr["dve"])
        wait("sync", "act", ctr["act"])
        dma("sync", dbgW2[:, 0:NHC], a1c[:, 0:NHC], "sat")
        dma("sync", dbgW2[:, NHC:2 * NHC], b1s[:, 0:NHC], "sat")
        dma("sync", dbgW2[:, 2 * NHC:2 * NHC + NOC], a2c[:, 0:NOC], "sat")
        dma("sync", dbgW2[:, 2 * NHC + NOC:], b2s[:, 0:NOC], "sat")
    wait("sync", "sout", 16 * NTB)

    with nc.Block() as block:
        @block.tensor
        def _(e):
            for f in steps["pe"]:
                f(e)

        @block.scalar
        def _(e):
            for f in steps["act"]:
                f(e)

        @block.vector
        def _(e):
            for f in steps["dve"]:
                f(e)

        @block.gpsimd
        def _(e):
            for f in steps["pool"]:
                f(e)

        @block.sync
        def _(e):
            for f in steps["sync"]:
                f(e)

    ctx.close()
    return nc


def _stage_inputs(cfg, x, W_q, W_k, W_v, W_o, w1, b1, w2, b2):
    D, LPC, HID = cfg["D"], cfg["LPC"], cfg["HID"]
    NH, NHC, NOC = cfg["NH"], cfg["NHC"], cfg["NOC"]
    f32 = np.float32
    perm = np.concatenate([
        np.concatenate([np.arange(h * 64, (h + 1) * 64, 2),
                        np.arange(h * 64 + 1, (h + 1) * 64, 2)])
        for h in range(NH)])
    com = {
        "wqT": np.ascontiguousarray(np.asarray(W_q, f32).T[:, perm]),
        "wkT": np.ascontiguousarray(np.asarray(W_k, f32).T[:, perm]),
        "wvT": np.ascontiguousarray(np.asarray(W_v, f32).T),
        "woT": np.ascontiguousarray(np.asarray(W_o, f32).T),
        "w1T": np.ascontiguousarray(np.asarray(w1, f32).T),
        "w1n": np.ascontiguousarray(np.asarray(w1, f32)),
        "w2T": np.ascontiguousarray(np.asarray(w2, f32).T),
        "w2n": np.ascontiguousarray(np.asarray(w2, f32)),
        "b1c": np.ascontiguousarray(np.asarray(b1, f32).reshape(NHC, 128).T),
        "b2c": np.ascontiguousarray(np.asarray(b2, f32).reshape(NOC, 128).T),
    }
    Pm = np.zeros((128, 128), f32)
    for r in range(128):
        Pm[r, r ^ 32] = 1.0
    com["Pm"] = Pm
    inv = (BASE ** (-(np.arange(32, dtype=np.float64)) / 32.0))
    xnp = np.asarray(x, f32)
    in_maps = []
    for c in range(8):
        b, q0 = c // 4, (c % 4) * LPC
        pos = (q0 + np.arange(LPC, dtype=np.float64))
        ang = np.outer(inv, pos)
        cc = np.cos(ang).astype(f32)
        ss = np.sin(ang).astype(f32)
        CC = np.tile(cc, (4, 1))
        SS = np.concatenate([-ss, ss, -ss, ss], axis=0)
        m = dict(com)
        m["CC"] = np.ascontiguousarray(CC)
        m["SS"] = np.ascontiguousarray(SS)
        m["xT"] = np.ascontiguousarray(xnp[b, q0:q0 + LPC, :].T)
        in_maps.append(m)
    return in_maps


def kernel(L0_attn_x, L0_attn_W_q, L0_attn_W_k, L0_attn_W_v, L0_attn_W_o,
           L0_ffn_w1, L0_ffn_b1, L0_ffn_w2, L0_ffn_b2):
    cfg = make_cfg()
    in_maps = _stage_inputs(cfg, L0_attn_x, L0_attn_W_q, L0_attn_W_k, L0_attn_W_v,
                            L0_attn_W_o, L0_ffn_w1, L0_ffn_b1, L0_ffn_w2, L0_ffn_b2)
    nc = build_program(cfg, detect_races=False)
    res = run_bass_kernel_spmd(nc, in_maps, core_ids=list(range(8)))
    LPC = cfg["LPC"]
    out = np.empty((2, 4 * LPC, cfg["D"]), np.float32)
    for c in range(8):
        b, q0 = c // 4, (c % 4) * LPC
        out[b, q0:q0 + LPC, :] = res.results[c]["yT"].T
    return out


# version 21
# speedup vs baseline: 1.0210x; 1.0012x over previous
import math
import numpy as np

import concourse.bass as bass
import concourse.mybir as mybir
from concourse.bass_utils import run_bass_kernel_spmd

F32 = mybir.dt.float32
BF16 = mybir.dt.bfloat16
AF = mybir.ActivationFunctionType
ALU = mybir.AluOpType

BASE = 10000.0


def make_cfg(D=1024, LPC=2048, HID=4096):
    assert D % 128 == 0 and LPC % 512 == 0 and HID % 1024 == 0
    return dict(
        D=D, LPC=LPC, HID=HID,
        NH=D // 64, NHP=D // 128, NFC=D // 128,
        NTB=LPC // 512, NKC=4 * LPC // 128,
        NHC=HID // 128, NOC=D // 128, L=4 * LPC,
    )


def build_program(cfg, detect_races=True, dbg=False, gelu_copy=False):
    D, LPC, HID = cfg["D"], cfg["LPC"], cfg["HID"]
    NFC, NHP, NTB, NKC = cfg["NFC"], cfg["NHP"], cfg["NTB"], cfg["NKC"]
    NHC, NOC = cfg["NHC"], cfg["NOC"]
    from contextlib import ExitStack

    nc = bass.Bass("TRN2", target_bir_lowering=False,
                   detect_race_conditions=detect_races)

    dt = nc.dram_tensor
    xT = dt("xT", [D, LPC], F32, kind="ExternalInput")
    wqT = dt("wqT", [D, D], F32, kind="ExternalInput")
    wkT = dt("wkT", [D, D], F32, kind="ExternalInput")
    wvT = dt("wvT", [D, D], F32, kind="ExternalInput")
    woT = dt("woT", [D, D], F32, kind="ExternalInput")
    w1T = dt("w1T", [D, HID], F32, kind="ExternalInput")
    w1n = dt("w1n", [HID, D], F32, kind="ExternalInput")
    w2T = dt("w2T", [HID, D], F32, kind="ExternalInput")
    w2n = dt("w2n", [D, HID], F32, kind="ExternalInput")
    b1c = dt("b1c", [128, NHC], F32, kind="ExternalInput")
    b2c = dt("b2c", [128, NOC], F32, kind="ExternalInput")
    CCd = dt("CC", [128, LPC], F32, kind="ExternalInput")
    SSd = dt("SS", [128, LPC], F32, kind="ExternalInput")
    Pmd = dt("Pm", [128, 128], F32, kind="ExternalInput")
    yT = dt("yT", [D, LPC], F32, kind="ExternalOutput")

    dk = dict(kind="ExternalOutput") if dbg else {}
    agk_i = dt("agk_i", [D, LPC], BF16)
    agk_o = dt("agk_o", [4 * D, LPC], BF16)
    agv_i = dt("agv_i", [LPC, D], BF16)
    agv_o = dt("agv_o", [4 * LPC, D], BF16)
    sgn1 = dt("sgn1", [D, HID], BF16, **dk)
    sgn2 = dt("sgn2", [HID, D], BF16, **dk)
    atu_d = dt("atu_d", [D, LPC], BF16, **dk)
    if dbg:
        dbgK = dt("dbgK", [D, LPC], BF16, kind="ExternalOutput")
        dbgV = dt("dbgV", [LPC, D], BF16, kind="ExternalOutput")
        dbgA = dt("dbgA", [D, LPC], F32, kind="ExternalOutput")
        dbgB = dt("dbgB", [D, LPC], F32, kind="ExternalOutput")
        dbgE = dt("dbgE", [D, LPC], F32, kind="ExternalOutput")
        dbgC = dt("dbgC", [D, LPC], BF16, kind="ExternalOutput")
        dbgG = dt("dbgG", [HID, LPC], BF16, kind="ExternalOutput")
        dbgW = dt("dbgW", [128, 2 * (NHC + NOC)], F32, kind="ExternalOutput")
        dbgP = dt("dbgP", [128, 512], BF16, kind="ExternalOutput")
        dbgW2 = dt("dbgW2", [128, 2 * (NHC + NOC)], F32, kind="ExternalOutput")

    ctx = ExitStack()

    sb = nc.alloc_sbuf_tensor
    qrope = sb("qrope", [128, NHP * LPC], BF16)
    lns = sb("lns", [128, NFC * 512], BF16)
    xq = sb("xq", [128, NFC * 512], F32)
    guard1 = sb("guard1", [128, 256], F32)
    a2f = sb("a2f", [128, NFC * 512], F32)
    guard2 = sb("guard2", [128, 256], F32)
    wbuf = [sb("wbuf0", [128, NFC * D], BF16), sb("wbuf1", [128, NFC * D], BF16)]
    pt = sb("pt", [128, 3072], BF16)
    atu_t = sb("atu_t", [128, 512], BF16)
    den_s = sb("den_s", [128, 512], F32)
    recin = sb("recin", [128, 512], F32)
    recb = sb("recb", [128, 512], F32)
    rows = sb("rows", [1, 2048], F32)
    onesD = sb("onesD", [128, 1], F32)
    onesb = sb("onesb", [128, 1], BF16)
    onesr = sb("onesr", [1, 64], F32)
    a1c = sb("a1c", [128, NHC + 2], F32)
    a2c = sb("a2c", [128, NOC + 2], F32)
    b1s = sb("b1s", [128, NHC + 2], F32)
    b2s = sb("b2s", [128, NOC + 2], F32)
    t1s = sb("t1s", [128, 512], F32)
    t2s = sb("t2s", [128, 512], F32)
    phb = sb("phb", [128, max(NHC * 256, NFC * 512)], BF16)
    guard3 = sb("guard3", [128, 256], BF16)
    s1b = [sb("s1b%d" % i, [128, NFC * 128], BF16) for i in range(4)]
    s2b = [sb("s2b%d" % i, [128, D], BF16) for i in range(4)]

    r1 = ExitStack()
    CCs = r1.enter_context(nc.sbuf_tensor("CCs", [128, LPC], F32))
    SSs = r1.enter_context(nc.sbuf_tensor("SSs", [128, LPC], F32))
    Pms = r1.enter_context(nc.sbuf_tensor("Pms", [128, 128], F32))
    wst = [r1.enter_context(nc.sbuf_tensor("wst0", [128, 1024], F32)),
           r1.enter_context(nc.sbuf_tensor("wst1", [128, 1024], F32))]
    qpre = [r1.enter_context(nc.sbuf_tensor("qpre%d" % i, [128, 512], F32))
            for i in range(3)]
    sgs = [r1.enter_context(nc.sbuf_tensor("sgs0", [128, 512], BF16)),
           r1.enter_context(nc.sbuf_tensor("sgs1", [128, 512], BF16))]
    krt = [r1.enter_context(nc.sbuf_tensor("krt0", [128, 512], BF16)),
           r1.enter_context(nc.sbuf_tensor("krt1", [128, 512], BF16))]
    vlc = [r1.enter_context(nc.sbuf_tensor("vlc0", [128, D], BF16)),
           r1.enter_context(nc.sbuf_tensor("vlc1", [128, D], BF16))]

    ps = nc.alloc_psum_tensor
    ps_s = ps("ps_s", [128, 3072], F32)
    ps_o = ps("ps_o", [128, 512], F32)
    ps_d = ps("ps_d", [128, 512], F32)
    ps_g1 = ps_s[:, 0:512]
    ps_g2 = ps_s[:, 512:1024]
    bcA = ps_s[:, 2048:2560]
    bcB = ps_s[:, 2560:3072]
    sw_slots = [bcA, bcB]

    ENGS = ("pe", "act", "dve", "pool", "sync")
    steps = {k: [] for k in ENGS}
    sems, ctr = {}, {}
    for name in ("sc", "sw0", "sw1", "sx", "sao", "skt", "svt",
                 "skt1", "skt2", "skt3", "svt1", "svt2", "svt3",
                 "s10", "s11", "s12", "s13", "s20", "s21", "s22", "s23",
                 "kr0", "kr1", "vl0", "vl1", "sg0", "sg1", "swc0", "swc1",
                 "scc", "pe", "act", "dve", "sat", "sout"):
        sems[name] = ctx.enter_context(nc.semaphore(name))
        ctr[name] = 0

    def wait(eng, sem, val):
        if val is None or val <= 0:
            return
        steps[eng].append(lambda e, s=sems[sem], v=val: e.wait_ge(s, v))

    def op(eng, fn, inc=None, n=1):
        if inc is None:
            steps[eng].append(fn)
            return None
        ctr[inc] += n
        steps[eng].append(lambda e, fn=fn, s=sems[inc], n=n: fn(e).then_inc(s, n))
        return ctr[inc]

    def mm(out, lhsT, rhs, start, stop, tp=None, inc=None, sgc=False):
        return op("pe", lambda e, o=out, l=lhsT, r=rhs, s=start, t=stop, p=tp,
                  g=sgc: e.matmul(o, l, r, start=s, stop=t, tile_position=p,
                                  skip_group_check=g), inc=inc)

    def dma(eng, dst, src, inc):
        return op(eng, lambda e, d=dst, s=src: e.dma_start(out=d, in_=s), inc=inc, n=16)

    dma("sync", CCs[:, :], CCd[:, :], "sc")
    dma("sync", SSs[:, :], SSd[:, :], "sc")
    dma("sync", Pms[:, :], Pmd[:, :], "sc")
    dma("sync", b1s[:, 0:NHC], b1c[:, :], "sc")
    v_consts = dma("sync", b2s[:, 0:NOC], b2c[:, :], "sc")
    for _eng in ("pe", "act", "dve"):
        wait(_eng, "sc", v_consts)

    op("dve", lambda e: e.memset(a1c[:, :], 0.0))
    op("dve", lambda e: e.memset(a2c[:, :], 0.0))
    op("dve", lambda e: e.memset(onesD[:, :], 1.0 / D))
    op("dve", lambda e: e.memset(onesb[:, :], 1.0))
    op("dve", lambda e: e.memset(onesr[:, :], 1.0))
    op("dve", lambda e: e.memset(ps_d[:, :], 0.0))
    op("dve", lambda e: e.memset(ps_s[:, :], 0.0))
    v_init = op("dve", lambda e: e.memset(ps_o[:, :], 0.0), inc="dve")

    def layer_norm_stats(src, bc1, bc2, pre_pe=None, pre_dve=None, pre_act=None,
                         pre_sd=None):
        if pre_sd:
            wait("pe", pre_sd[0], pre_sd[1])
        wait("pe", "pe", pre_pe)
        wait("pe", "act", pre_act)
        wait("pe", "dve", pre_dve)
        for fc in range(NFC):
            mm(ps_d[0:1, :], onesD[:, :], src[:, fc * 512:(fc + 1) * 512],
               fc == 0, fc == NFC - 1, tp=(0, 0), sgc=True)
        msqv = {}
        if pre_sd:
            wait("dve", pre_sd[0], pre_sd[1])
        wait("dve", "act", pre_act)
        wait("dve", "dve", pre_dve)
        for fc in range(NFC):
            tscr = t1s if fc % 2 == 0 else t2s
            if fc >= 2:
                wait("dve", "pe", msqv[fc - 2])
            sqv = op("dve", lambda e, t=tscr, s=src, f=fc:
                     e.tensor_tensor(t[:, :], s[:, f * 512:(f + 1) * 512],
                                     s[:, f * 512:(f + 1) * 512], ALU.mult),
                     inc="dve")
            wait("pe", "dve", sqv)
            msqv[fc] = mm(ps_d[64:65, :], onesD[:, :], tscr[:, :],
                          fc == 0, fc == NFC - 1, tp=(0, 64), inc="pe", sgc=True)
        return ln_tail(bc1, bc2, msqv[NFC - 1])

    def ln_tail(bc1, bc2, last_msq):
        wait("dve", "pe", last_msq)
        op("dve", lambda e: e.tensor_copy(rows[0:1, 0:512], ps_d[0:1, :]))
        op("dve", lambda e: e.tensor_copy(rows[0:1, 1024:1536], ps_d[64:65, :]))
        op("dve", lambda e: e.tensor_tensor(rows[0:1, 1536:2048], rows[0:1, 0:512],
                                            rows[0:1, 0:512], ALU.mult))
        op("dve", lambda e: e.tensor_tensor(
            rows[0:1, 1024:1536], rows[0:1, 1024:1536], rows[0:1, 1536:2048],
            ALU.subtract))
        dv = op("dve", lambda e: e.tensor_scalar_add(
            rows[0:1, 1024:1536], rows[0:1, 1024:1536], 1e-5), inc="dve")
        wait("act", "dve", dv)
        op("act", lambda e: e.activation(rows[0:1, 1024:1536], rows[0:1, 1024:1536],
                                         AF.Ln))
        av = op("act", lambda e: e.activation(rows[0:1, 512:1024],
                                              rows[0:1, 1024:1536],
                                              AF.Exp, scale=-0.5), inc="act")
        wait("pe", "act", av)
        mm(bc1[0:64, :], onesr[:, :], rows[0:1, 0:512], True, True, tp=(0, 0))
        mm(bc1[64:128, :], onesr[:, :], rows[0:1, 0:512], True, True, tp=(0, 64))
        mm(bc2[0:64, :], onesr[:, :], rows[0:1, 512:1024], True, True, tp=(0, 0))
        return mm(bc2[64:128, :], onesr[:, :], rows[0:1, 512:1024], True, True,
                  tp=(0, 64), inc="pe")

    wst_war = [None, None]
    wbuf_war = [None, None]

    def load_weight(wdram, par):
        wait("pool", "pe", wbuf_war[par])
        return dma("pool",
                   wbuf[par][:, :].rearrange("p (c n) -> p c n", n=D),
                   wdram[:, :].rearrange("(c p) n -> p c n", p=128),
                   "swc%d" % par)

    widx = 0
    kv_stores = 0
    evac_war3 = [None, None, None]
    evac_war4 = [None, None, None, None]
    slot_t2 = [None, None]
    qp_rd_pe = [None, None, None]
    qp_rd_dve = [None, None, None]
    kst = [None, None]
    vst = [None, None]
    acc_war = None

    prev_bcv = None
    for tb in range(NTB):
        if tb > 0:
            wait("sync", "pe", prev_bcv)
            wait("sync", "dve", lns_ready)
        x_sd = dma("sync", xq[:, :].rearrange("p (c t) -> p c t", t=512),
                   xT[:, tb * 512:(tb + 1) * 512]
                   .rearrange("(c p) t -> p c t", p=128), "sx")
        bcv = layer_norm_stats(xq, bcA, bcB, pre_sd=("sx", x_sd),
                               pre_dve=max(v_init if tb == 0 else 0,
                                           acc_war or 0) or None)
        lastap = None
        wait("dve", "pe", bcv)
        for fc in range(NFC):
            op("dve", lambda e, f=fc: e.tensor_tensor(
                t1s[:, :], xq[:, f * 512:(f + 1) * 512], bcA, ALU.subtract))
            lastap = op("dve", lambda e, f=fc: e.tensor_tensor(
                lns[:, f * 512:(f + 1) * 512], t1s[:, :], bcB, ALU.mult),
                inc="dve")
        lns_ready = lastap
        prev_bcv = bcv

        for (wdram, is_q) in ((wqT, True), (wkT, False)):
            par = widx % 2
            widx += 1
            castv = load_weight(wdram, par)
            wbuf_war[par] = None
            prjs = {}

            def emit_swap(oc, tb=tb):
                qp = qpre[oc % 3]
                slot = sw_slots[oc % 2]
                wait("pe", "act", evac_war3[oc % 3])
                wait("pe", "dve", slot_t2[oc % 2])
                swv = mm(slot, Pms[:, :], qp[:, :], True, True, inc="pe")
                qp_rd_pe[oc % 3] = swv
                wait("dve", "act", evac_war3[oc % 3])
                dv1 = op("dve", lambda e, q=qp, t=tb: e.tensor_tensor(
                    t1s[:, :], q[:, :], CCs[:, t * 512:(t + 1) * 512], ALU.mult),
                    inc="dve")
                qp_rd_dve[oc % 3] = dv1
                wait("dve", "pe", swv)
                t2v = op("dve", lambda e, s=slot, t=tb: e.tensor_tensor(
                    t2s[:, :], s, SSs[:, t * 512:(t + 1) * 512], ALU.mult),
                    inc="dve")
                slot_t2[oc % 2] = t2v
                if is_q:
                    dst = qrope[:, oc * LPC + tb * 512: oc * LPC + (tb + 1) * 512]
                    op("dve", lambda e, d=dst: e.tensor_tensor(
                        d, t1s[:, :], t2s[:, :], ALU.add))
                else:
                    kb = krt[oc % 2]
                    wait("dve", "kr%d" % (oc % 2), kst[oc % 2])
                    rv = op("dve", lambda e, d=kb: e.tensor_tensor(
                        d[:, :], t1s[:, :], t2s[:, :], ALU.add), inc="dve")
                    wait("pool", "dve", rv)
                    kst[oc % 2] = dma(
                        "pool",
                        agk_i[oc * 128:(oc + 1) * 128, tb * 512:(tb + 1) * 512],
                        kb[:, :], "kr%d" % (oc % 2))

            for oc in range(NHP):
                acc = ps_s[:, (oc % 4) * 512:(oc % 4) * 512 + 512]
                if oc == 0:
                    wait("pe", "swc%d" % par, castv)
                    wait("pe", "dve", lns_ready)
                wait("pe", "act", evac_war4[oc % 4])
                prj = None
                for fc in range(NFC):
                    prj = mm(acc,
                             wbuf[par][:, fc * D + oc * 128: fc * D + (oc + 1) * 128],
                             lns[:, fc * 512:(fc + 1) * 512],
                             fc == 0, fc == NFC - 1,
                             inc="pe" if fc == NFC - 1 else None)
                prjs[oc] = prj
                wbuf_war[par] = prj
                qp = qpre[oc % 3]
                wait("act", "pe", prj)
                wait("act", "pe", qp_rd_pe[oc % 3])
                wait("act", "dve", qp_rd_dve[oc % 3])
                ev = op("act", lambda e, q=qp, a=acc: e.copy(q[:, :], a[:, :]),
                        inc="act")
                evac_war3[oc % 3] = ev
                evac_war4[oc % 4] = ev
                if oc >= 1:
                    emit_swap(oc - 1)
            emit_swap(NHP - 1)

        par = widx % 2
        widx += 1
        castv = load_weight(wvT, par)
        wbuf_war[par] = None
        for tc in range(4):
            vb = vlc[tc % 2]
            prj = None
            if tc == 0:
                wait("pe", "swc%d" % par, castv)
                wait("pe", "dve", lns_ready)
            wait("pe", "dve", acc_war)
            vw = min(512, D)
            if tc == 0:
                wait("pe", "act", max(v for v in evac_war4 if v) if any(evac_war4) else None)
            for oh in range(max(1, D // 512)):
                accv = ps_g1 if oh % 2 == 0 else ps_g2
                for fc in range(NFC):
                    prj = mm(accv[:, 0:vw],
                             lns[:, fc * 512 + tc * 128: fc * 512 + (tc + 1) * 128],
                             wbuf[par][:, fc * D + oh * vw: fc * D + (oh + 1) * vw],
                             fc == 0, fc == NFC - 1,
                             inc="pe" if fc == NFC - 1 else None)
            wbuf_war[par] = prj
            wait("dve", "pe", prj)
            wait("dve", "vl%d" % (tc % 2), vst[tc % 2])
            for oh in range(max(1, D // 512)):
                accv = ps_g1 if oh % 2 == 0 else ps_g2
                vv = op("dve", lambda e, v=vb, a=accv, o=oh, vw=vw: e.tensor_copy(
                    v[:, o * vw:(o + 1) * vw], a[:, 0:vw]), inc="dve")
            acc_war = vv
            wait("pool", "dve", vv)
            vst[tc % 2] = dma(
                "pool", agv_i[(tb * 4 + tc) * 128:(tb * 4 + tc + 1) * 128, :],
                vb[:, :], "vl%d" % (tc % 2))

    wo_par = widx % 2
    widx += 1
    wo_cast = load_weight(woT, wo_par)
    wbuf_war[wo_par] = None

    r1_marks = dict(ctr)
    sgstage = [xq[:, 0:1024], xq[:, 1024:2048]]
    sgout = [s2b[2][:, 0:512], s2b[3][:, 0:512]]
    wait("sync", "pe", ctr["pe"])
    wait("sync", "dve", ctr["dve"])
    sg_war = [None, None]
    i = 0
    sg_hist = {}
    for (src_d, dst_d, RP, CT) in ((w1T, sgn1, NFC, HID), (w2T, sgn2, NHC, D)):
        CJ, CW = max(1, CT // 512), min(512, CT)
        for rp in range(RP):
            for j in range(CJ):
                k = i % 2
                wait("sync", "act", sg_hist.get(i - 2))
                ld = dma("sync", sgstage[k][:, 0:CW],
                         src_d[rp * 128:(rp + 1) * 128, j * CW:(j + 1) * CW],
                         "sw%d" % k)
                wait("act", "sw%d" % k, ld)
                wait("act", "sg%d" % k, sg_war[k])
                sv = op("act", lambda e, k=k, CW=CW: e.sign(sgout[k][:, 0:CW],
                                                            sgstage[k][:, 0:CW]),
                        inc="act")
                sg_hist[i] = sv
                wait("pool", "act", sv)
                sg_war[k] = dma("pool",
                                dst_d[rp * 128:(rp + 1) * 128, j * CW:(j + 1) * CW],
                                sgout[k][:, 0:CW], "sg%d" % k)
                i += 1
    wait("sync", "act", ctr["act"])
    astage = [xq[:, 2048:3072], xq[:, 3072:4096]]
    ab_hist = {}
    ai = 0
    prev_abs = None
    for (src_d, col_t, NCH, DEN) in ((w1n, a1c, NHC, D), (w2n, a2c, NOC, HID)):
        pieces = max(1, DEN // 1024)
        pw = min(DEN, 1024)
        for ch in range(NCH):
            for pz in range(pieces):
                k = ai % 2
                wait("sync", "act", ab_hist.get(ai - 2))
                ldv = dma("sync", astage[k][:, 0:pw],
                          src_d[ch * 128:(ch + 1) * 128, pz * pw:(pz + 1) * pw],
                          "sw%d" % k)
                wait("act", "sw%d" % k, ldv)
                dst_col = col_t[:, NCH:NCH + 1] if pieces > 1 else col_t[:, ch:ch + 1]
                prev_abs = op("act", lambda e, pw=pw, dc=dst_col, k=k: e.activation(
                    s1b[k][:, 0:pw], astage[k][:, 0:pw], AF.Abs, accum_out=dc),
                    inc="act")
                ab_hist[ai] = prev_abs
                ai += 1
                if pieces > 1:
                    wait("dve", "act", prev_abs)
                    if pz == 0:
                        op("dve", lambda e, ct=col_t, c=ch, n=NCH: e.tensor_copy(
                            ct[:, c:c + 1], ct[:, n:n + 1]))
                    else:
                        op("dve", lambda e, ct=col_t, c=ch, n=NCH: e.tensor_tensor(
                            ct[:, c:c + 1], ct[:, c:c + 1], ct[:, n:n + 1], ALU.add))
                    prev_dve = op("dve", lambda e, ct=col_t, c=ch: e.tensor_copy(
                        t2s[:, 0:1], ct[:, c:c + 1]), inc="dve")
                    wait("act", "dve", prev_dve)
        wait("dve", "act", prev_abs)
        op("dve", lambda e, ct=col_t, n=NCH, d=DEN: e.tensor_scalar_mul(
            ct[:, 0:n], ct[:, 0:n], 1.0 / d))
    alpha_act_mark = ctr["act"]
    alpha_done = op("dve", lambda e: e.tensor_copy(t1s[:, 0:1], a1c[:, 0:1]),
                    inc="dve")

    if dbg:
        for _s in ("kr0", "kr1", "vl0", "vl1"):
            wait("pool", _s, ctr[_s])
        dma("pool", dbgK[:, :], agk_i[:, :], "sg0")
        dma("pool", dbgV[:, :], agv_i[:, :], "sg0")
    rgs = [[0, 1, 2, 3], [4, 5, 6, 7]]
    for _s in ("kr0", "kr1"):
        wait("pool", _s, ctr[_s])
    op("pool", lambda e: e.collective_compute(
        "AllGather", ALU.bypass, replica_groups=rgs,
        ins=[agk_i.ap().opt()], outs=[agk_o.ap().opt()]), inc="scc", n=1)
    for _s in ("vl0", "vl1"):
        wait("pool", _s, ctr[_s])
    op("pool", lambda e: e.collective_compute(
        "AllGather", ALU.bypass, replica_groups=rgs,
        ins=[agv_i.ap().opt()], outs=[agv_o.ap().opt()]), inc="scc", n=1)

    if dbg:
        wait("sync", "dve", alpha_done)
        dma("sync", dbgW[:, 0:NHC], a1c[:, 0:NHC], "sat")
        dma("sync", dbgW[:, NHC:2 * NHC], b1s[:, 0:NHC], "sat")
        dma("sync", dbgW[:, 2 * NHC:2 * NHC + NOC], a2c[:, 0:NOC], "sat")
        dma("sync", dbgW[:, 2 * NHC + NOC:], b2s[:, 0:NOC], "sat")
    r1.close()
    ktb = sb("ktb", [128, 4 * LPC], BF16)
    vtb = sb("vtb", [128, 4 * LPC], BF16)

    exp_free = {0: None, 1: None, 2: None}
    pt_free = {0: None, 1: None, 2: None}
    attn_last_pe = None
    sat_last = None
    po_war = None
    pd_war = None

    RCH = NKC // 4
    ktsem = ["skt", "skt1", "skt2", "skt3"]
    vtsem = ["svt", "svt1", "svt2", "svt3"]
    kt_rv = [None] * 4
    vt_rv = [None] * 4

    def load_kv(hp, gate_scv=None, gate_pvs=None):
        for r in range(4):
            if gate_scv is not None:
                wait("sync", "pe", gate_scv[(r + 1) * RCH - 1])
            kt_rv[r] = dma("sync", ktb[:, r * LPC:(r + 1) * LPC],
                           agk_o[r * D + hp * 128: r * D + (hp + 1) * 128, :],
                           ktsem[r])
            if gate_pvs is not None:
                wait("sync", "pe", gate_pvs[(r + 1) * RCH - 1])
            for q4 in range(LPC // 512):
                base = (r * (LPC // 128) + q4 * 4) * 128
                vt_rv[r] = dma(
                    "sync",
                    vtb[:, base:base + 512].rearrange("p (c d) -> p c d", d=128),
                    agv_o[r * LPC + q4 * 512: r * LPC + (q4 + 1) * 512,
                          hp * 128:(hp + 1) * 128]
                    .rearrange("(c p) d -> p c d", p=128), vtsem[r])

    for hp in range(NHP):
        if hp == 0:
            wait("sync", "scc", 2)
            for _s in ("kr0", "kr1", "vl0", "vl1"):
                wait("sync", _s, r1_marks[_s])
            wait("sync", "dve", r1_marks["dve"])
            wait("sync", "act", r1_marks["act"])
            wait("sync", "pe", r1_marks["pe"])
            load_kv(0)

        for qb in range(NTB):
            qsl = slice(hp * LPC + qb * 512, hp * LPC + (qb + 1) * 512)
            denv = None
            scv = {}
            pvs = {}
            LKA = 2
            for ci in range(NKC + LKA):
                if ci < NKC:
                    c = ci
                    w = c % 3
                    wait("pe", "act", exp_free[w])
                    if qb == 0 and c % RCH == 0:
                        wait("pe", ktsem[c // RCH], kt_rv[c // RCH])
                    mm(ps_s[:, w * 1024: w * 1024 + 512],
                       ktb[0:64, c * 128:(c + 1) * 128], qrope[0:64, qsl],
                       True, True, tp=(0, 0))
                    scv[c] = mm(ps_s[:, w * 1024 + 512: w * 1024 + 1024],
                                ktb[64:128, c * 128:(c + 1) * 128],
                                qrope[64:128, qsl],
                                True, True, tp=(64, 0), inc="pe")
                if ci >= LKA:
                    c = ci - LKA
                    w = c % 3
                    wait("act", "pe", scv[c])
                    wait("act", "pe", pt_free[w])
                    expv = op("act", lambda e, w=w: e.activation(
                        pt[:, w * 1024:(w + 1) * 1024],
                        ps_s[:, w * 1024:(w + 1) * 1024],
                        AF.Exp, scale=0.125), inc="act")
                    exp_free[w] = expv
                    wait("pe", "act", expv)
                    if c == 0:
                        wait("pe", "dve", po_war)
                    if qb == 0 and c % RCH == 0:
                        wait("pe", vtsem[c // RCH], vt_rv[c // RCH])
                    mm(ps_o[0:64, :], vtb[:, c * 128: c * 128 + 64],
                       pt[:, w * 1024: w * 1024 + 512],
                       c == 0, c == NKC - 1, tp=(0, 0), sgc=True)
                    pvv = mm(ps_o[64:128, :], vtb[:, c * 128 + 64: c * 128 + 128],
                             pt[:, w * 1024 + 512: w * 1024 + 1024],
                             c == 0, c == NKC - 1, tp=(0, 64), inc="pe", sgc=True)
                    pt_free[w] = pvv
                    pvs[c] = pvv
                    if c % 2 == 1:
                        g = c // 2
                        wp = (c - 1) % 3
                        if g == 0:
                            wait("pe", "dve", pd_war)
                        st = (g == 0)
                        sp = (g == NKC // 2 - 1)
                        mm(ps_d[0:1, :], onesb[:, :],
                           pt[:, wp * 1024: wp * 1024 + 512], st, sp, tp=(0, 0),
                           sgc=True)
                        mm(ps_d[32:33, :], onesb[:, :],
                           pt[:, wp * 1024 + 512: wp * 1024 + 1024], st, sp,
                           tp=(0, 32), sgc=True)
                        mm(ps_d[64:65, :], onesb[:, :],
                           pt[:, w * 1024: w * 1024 + 512], st, sp,
                           tp=(0, 64), sgc=True)
                        denv = mm(ps_d[96:97, :], onesb[:, :],
                                  pt[:, w * 1024 + 512: w * 1024 + 1024], st, sp,
                                  tp=(0, 96), inc="pe", sgc=True)
                        pt_free[wp] = denv
                        pt_free[w] = denv
            attn_last_pe = ctr["pe"]
            if qb == NTB - 1 and hp + 1 < NHP:
                load_kv(hp + 1, gate_scv=scv, gate_pvs=pvs)
            wait("dve", "pe", denv)
            pd_war = op("dve", lambda e: e.tensor_copy(den_s[:, :], ps_d[:, :]),
                        inc="dve")
            op("dve", lambda e: e.tensor_copy(t1s[0:64, :], den_s[64:128, :]))
            op("dve", lambda e: e.tensor_tensor(
                recin[0:64, :], den_s[0:64, :], t1s[0:64, :], ALU.add))
            dnr = op("dve", lambda e: e.tensor_copy(recb[0:1, :], recin[32:33, :]),
                     inc="dve")
            wait("pe", "dve", dnr)
            mm(ps_d[0:64, :], onesr[:, :], recin[0:1, :], True, True, tp=(0, 0))
            bcv = mm(ps_d[64:128, :], onesr[:, :], recb[0:1, :], True, True,
                     tp=(0, 64), inc="pe")
            wait("dve", "pe", bcv)
            pd_war = op("dve", lambda e: e.tensor_copy(recin[:, :], ps_d[:, :]),
                        inc="dve")
            op("dve", lambda e: e.reciprocal(recb[:, :], recin[:, :]))
            wait("dve", "sat", sat_last)
            po_war = op("dve", lambda e: e.tensor_tensor(
                atu_t[:, :], ps_o[:, :], recb[:, :], ALU.mult), inc="dve")
            wait("sync", "dve", po_war)
            sat_last = dma("sync",
                           atu_d[hp * 128:(hp + 1) * 128, qb * 512:(qb + 1) * 512],
                           atu_t[:, :], "sat")

    awo_b = [vtb[:, 0:NFC * 512], vtb[:, NFC * 512:2 * NFC * 512]]
    xq_b = [xq[:, :], ktb[:, :].bitcast(F32)]
    aw_sds = [None, None]
    x_sds = [None, None]
    s1_war = [None, None, None, None]
    s2_war = [None, None, None, None]
    ffn2_last = None
    y_dve = None
    prev_fev = None
    wo_prj_last = None
    y_fcs_prev = None

    aw_sem = ["sao", "svt"]
    x_sem = ["sx", "skt"]

    def load_qb(j):
        aw_sds[j % 2] = dma("sync", awo_b[j % 2].rearrange("p (c t) -> p c t",
                                                           t=512),
                            atu_d[:, j * 512:(j + 1) * 512]
                            .rearrange("(c p) t -> p c t", p=128), aw_sem[j % 2])
        x_sds[j % 2] = dma("sync", xq_b[j % 2].rearrange("p (c t) -> p c t",
                                                         t=512),
                           xT[:, j * 512:(j + 1) * 512]
                           .rearrange("(c p) t -> p c t", p=128), x_sem[j % 2])

    for qb in range(NTB):
        awo = awo_b[qb % 2]
        xqs = xq_b[qb % 2]
        aw_sd = aw_sds[qb % 2]
        x_sd = x_sds[qb % 2]
        if qb == 0:
            wait("sync", "sat", ctr["sat"])
            load_qb(0)
            aw_sd, x_sd = aw_sds[0], x_sds[0]
            if NTB > 1:
                load_qb(1)
        elif qb + 1 < NTB:
            wait("sync", "pe", wo_prj_last)
            wait("sync", "sout", 16 * qb)
            load_qb(qb + 1)
        if dbg:
            wait("act", "sat", ctr["sat"])
            wait("dve", "sat", ctr["sat"])
        a2vs = []
        prj = None
        for oc in range(NOC):
            acc = ps_s[:, (oc % 2) * 512:(oc % 2) * 512 + 512]
            if oc == 0:
                wait("pe", aw_sem[qb % 2], aw_sd)
                wait("pe", "swc%d" % wo_par, wo_cast)
                wait("pe", "dve", max(po_war or 0, prev_fev or 0) or None)
                if qb == 0:
                    wait("pe", "act", ctr["act"])
            if oc >= 2:
                wait("pe", "act", a2vs[oc - 2])
            for fc in range(NFC):
                prj = mm(acc[:, :],
                         wbuf[wo_par][:, fc * D + oc * 128: fc * D + (oc + 1) * 128],
                         awo[:, fc * 512:(fc + 1) * 512], fc == 0, fc == NFC - 1,
                         inc="pe" if fc == NFC - 1 else None)
            wo_prj_last = prj
            wait("act", "pe", prj)
            if y_fcs_prev is not None:
                wait("act", "dve", y_fcs_prev[oc])
            a2vs.append(op("act", lambda e, o=oc, a=acc: e.copy(
                a2f[:, o * 512:(o + 1) * 512], a[:, :]), inc="act"))
        a2v = a2vs[-1]
        if dbg:
            wait("sync", "act", a2v)
            dma("sync", dbgA[:, qb * 512:(qb + 1) * 512]
                .rearrange("(c p) t -> p c t", p=128),
                a2f[:, :].rearrange("p (c t) -> p c t", t=512), "sat")
        bc2m = ps_s[:, 0:512]
        bc2r = ps_s[:, 512:1024]
        msqv2 = {}
        wait("pe", "dve", prev_fev)
        for fc in range(NFC):
            wait("pe", "act", a2vs[fc])
            mm(ps_d[0:1, :], onesD[:, :], a2f[:, fc * 512:(fc + 1) * 512],
               fc == 0, fc == NFC - 1, tp=(0, 0), sgc=True)
            tscr = t1s if fc % 2 == 0 else t2s
            if fc >= 2:
                wait("dve", "pe", msqv2[fc - 2])
            wait("dve", "act", a2vs[fc])
            sqv = op("dve", lambda e, t=tscr, f=fc: e.tensor_tensor(
                t[:, :], a2f[:, f * 512:(f + 1) * 512],
                a2f[:, f * 512:(f + 1) * 512], ALU.mult), inc="dve")
            wait("pe", "dve", sqv)
            msqv2[fc] = mm(ps_d[64:65, :], onesD[:, :], tscr[:, :],
                           fc == 0, fc == NFC - 1, tp=(0, 64), inc="pe", sgc=True)
        bcv2 = ln_tail(bc2m, bc2r, msqv2[NFC - 1])
        wait("dve", "pe", bcv2)
        wait("dve", x_sem[qb % 2], x_sd)
        bc3m = ps_s[:, 1024:1536]
        bc3r = ps_s[:, 1536:2048]
        rv = None
        msqv3 = {}
        for fc in range(NFC):
            op("dve", lambda e, f=fc, b=bc2m: e.tensor_tensor(
                den_s[:, :], a2f[:, f * 512:(f + 1) * 512], b, ALU.subtract))
            op("dve", lambda e, b=bc2r: e.tensor_tensor(den_s[:, :], den_s[:, :],
                                                        b, ALU.mult))
            rv = op("dve", lambda e, f=fc, x=xqs: e.tensor_tensor(
                x[:, f * 512:(f + 1) * 512], x[:, f * 512:(f + 1) * 512],
                den_s[:, :], ALU.add), inc="dve")
            wait("pe", "dve", rv)
            mm(ps_d[0:1, :], onesD[:, :], xqs[:, fc * 512:(fc + 1) * 512],
               fc == 0, fc == NFC - 1, tp=(0, 0), sgc=True)
            tscr = t1s if fc % 2 == 0 else t2s
            if fc >= 2:
                wait("dve", "pe", msqv3[fc - 2])
            sqv = op("dve", lambda e, t=tscr, f=fc, x=xqs: e.tensor_tensor(
                t[:, :], x[:, f * 512:(f + 1) * 512],
                x[:, f * 512:(f + 1) * 512], ALU.mult), inc="dve")
            wait("pe", "dve", sqv)
            msqv3[fc] = mm(ps_d[64:65, :], onesD[:, :], tscr[:, :],
                           fc == 0, fc == NFC - 1, tp=(0, 64), inc="pe", sgc=True)
        if dbg:
            wait("sync", "dve", rv)
            dma("sync", dbgB[:, qb * 512:(qb + 1) * 512]
                .rearrange("(c p) t -> p c t", p=128),
                xqs.rearrange("p (c t) -> p c t", t=512), "sat")
        bcv3 = ln_tail(bc3m, bc3r, msqv3[NFC - 1])
        wait("dve", "pe", bcv3)
        ln3v = None
        lnv = {}
        for fc in range(NFC):
            op("dve", lambda e, f=fc, b=bc3m, x=xqs: e.tensor_tensor(
                t1s[:, :], x[:, f * 512:(f + 1) * 512], b, ALU.subtract))
            ln3v = op("dve", lambda e, f=fc, b=bc3r: e.tensor_tensor(
                lns[:, f * 512:(f + 1) * 512], t1s[:, :], b, ALU.mult),
                inc="dve")
            lnv[fc] = ln3v
        if dbg:
            wait("sync", "dve", ln3v)
            dma("sync", dbgC[:, qb * 512:(qb + 1) * 512]
                .rearrange("(c p) t -> p c t", p=128),
                lns[:, :].rearrange("p (c t) -> p c t", t=512), "sat")
        fev = None
        for half in range(2):
            if dbg:
                wait("act", "sat", ctr["sat"])
            gel = {}
            for idx, hc in enumerate(range(NHC)):
                hci = hc
                k = idx % 4
                wait("sync", "pe", s1_war[k])
                s1_sd = dma("sync",
                            s1b[k][:, :].rearrange("p (c h) -> p c h", h=128),
                            sgn1[:, hc * 128:(hc + 1) * 128]
                            .rearrange("(c p) h -> p c h", p=128), "s1%d" % k)
                acc = ps_s[:, (idx % 4) * 512:(idx % 4) * 512 + 512]
                wait("pe", "s1%d" % k, s1_sd)
                if idx < 4:
                    wait("pe", "dve", max(alpha_done, fev or 0))
                    if half == 0:
                        wait("pe", "act", a2v)
                wait("pe", "act", gel.get(idx - 4))
                prj = None
                for fc in range(NFC):
                    if idx == 0 and half == 0:
                        wait("pe", "dve", lnv[fc])
                    prj = mm(acc[:, 0:256],
                             s1b[k][:, fc * 128:(fc + 1) * 128],
                             lns[:, fc * 512 + half * 256: fc * 512 + half * 256 + 256],
                             fc == 0, fc == NFC - 1,
                             inc="pe" if fc == NFC - 1 else None)
                s1_war[k] = prj
                wait("act", "pe", prj)
                if dbg and qb == 0 and half == 0 and hc in (29, 31):
                    pcp = op("act", lambda e, a=acc, h=hc: e.copy(
                        atu_t[:, (0 if h == 31 else 256):(256 if h == 31 else 512)],
                        a[:, 0:256]), inc="act")
                    if hc == 29 or hc == 31:
                        dbg_pcp = pcp
                dst = phb[:, hc * 256:(hc + 1) * 256]
                gfn = AF.Identity if gelu_copy else AF.Gelu
                gel[hci] = op("act", lambda e, h=hc, a=acc, d=dst, gfn=gfn:
                    e.activation(d, a[:, 0:256], gfn,
                                 bias=b1s[:, h:h + 1], scale=a1c[:, h:h + 1]),
                    inc="act")
            if dbg and qb == 0 and half == 0:
                wait("sync", "act", dbg_pcp)
                dma("sync", dbgP[:, :], atu_t[:, :], "sat")
            if dbg:
                wait("sync", "act", max(gel.values()))
                dma("sync", dbgG[:, qb * 512 + half * 256: qb * 512 + half * 256 + 256]
                    .rearrange("(c p) t -> p c t", p=128),
                    phb[:, 0:NHC * 256].rearrange("p (c t) -> p c t", t=256), "sat")
            f2_tiles = [ps_s[:, 0:256], ps_s[:, 512:768], ps_s[:, 1024:1280],
                        ps_s[:, 1536:1792], ps_s[:, 2048:2304],
                        ps_s[:, 2560:2816], ps_o[:, 0:256], ps_d[:, 0:256]]
            f2 = None
            for hc in range(NHC):
                k = hc % 4
                wait("sync", "pe", s2_war[k])
                s2_sd = dma("sync", s2b[k][:, :],
                            sgn2[hc * 128:(hc + 1) * 128, :], "s2%d" % k)
                wait("pe", "s2%d" % k, s2_sd)
                wait("pe", "act", gel[hc])
                if hc == 0:
                    wait("pe", "act", max(gel.values()))
                    wait("pe", "dve", fev)
                for oc in range(NOC):
                    f2 = mm(f2_tiles[oc],
                            s2b[k][:, oc * 128:(oc + 1) * 128],
                            phb[:, hc * 256:(hc + 1) * 256],
                            hc == 0, hc == NHC - 1, sgc=True,
                            inc="pe" if oc == NOC - 1 else None)
                s2_war[k] = f2
            ffn2_last = f2
            wait("dve", "pe", f2)
            if dbg:
                wait("dve", "sat", ctr["sat"])
            for oc in range(NOC):
                fev = op("dve", lambda e, o=oc, h=half: e.tensor_scalar(
                    a2f[:, o * 512 + h * 256: o * 512 + h * 256 + 256],
                    f2_tiles[o],
                    a2c[:, o:o + 1], b2s[:, o:o + 1], ALU.mult, ALU.add),
                    inc="dve")
        prev_fev = fev
        if dbg:
            wait("sync", "dve", fev)
            dma("sync", dbgE[:, qb * 512:(qb + 1) * 512]
                .rearrange("(c p) t -> p c t", p=128),
                a2f[:, :].rearrange("p (c t) -> p c t", t=512), "sat")
        if dbg:
            wait("dve", "sat", ctr["sat"])
        bc4m = ps_s[:, 1024:1536]
        bc4r = ps_s[:, 1536:2048]
        bcv4 = layer_norm_stats(a2f, bc4m, bc4r, pre_dve=fev)
        wait("dve", "pe", bcv4)
        y_fcs = []
        for fc in range(NFC):
            op("dve", lambda e, f=fc, b=bc4m: e.tensor_tensor(
                t1s[:, :], a2f[:, f * 512:(f + 1) * 512], b, ALU.subtract))
            op("dve", lambda e, b=bc4r: e.tensor_tensor(t2s[:, :], t1s[:, :], b,
                                                        ALU.mult))
            y_dve = op("dve", lambda e, f=fc, x=xqs: e.tensor_tensor(
                x[:, f * 512:(f + 1) * 512], x[:, f * 512:(f + 1) * 512],
                t2s[:, :], ALU.add), inc="dve")
            y_fcs.append(y_dve)
        y_fcs_prev = y_fcs
        wait("sync", "dve", y_dve)
        dma("sync", yT[:, qb * 512:(qb + 1) * 512]
            .rearrange("(c p) t -> p c t", p=128),
            xqs.rearrange("p (c t) -> p c t", t=512), "sout")

    if dbg:
        wait("sync", "dve", ctr["dve"])
        wait("sync", "act", ctr["act"])
        dma("sync", dbgW2[:, 0:NHC], a1c[:, 0:NHC], "sat")
        dma("sync", dbgW2[:, NHC:2 * NHC], b1s[:, 0:NHC], "sat")
        dma("sync", dbgW2[:, 2 * NHC:2 * NHC + NOC], a2c[:, 0:NOC], "sat")
        dma("sync", dbgW2[:, 2 * NHC + NOC:], b2s[:, 0:NOC], "sat")
    wait("sync", "sout", 16 * NTB)

    with nc.Block() as block:
        @block.tensor
        def _(e):
            for f in steps["pe"]:
                f(e)

        @block.scalar
        def _(e):
            for f in steps["act"]:
                f(e)

        @block.vector
        def _(e):
            for f in steps["dve"]:
                f(e)

        @block.gpsimd
        def _(e):
            for f in steps["pool"]:
                f(e)

        @block.sync
        def _(e):
            for f in steps["sync"]:
                f(e)

    ctx.close()
    return nc


def _stage_inputs(cfg, x, W_q, W_k, W_v, W_o, w1, b1, w2, b2):
    D, LPC, HID = cfg["D"], cfg["LPC"], cfg["HID"]
    NH, NHC, NOC = cfg["NH"], cfg["NHC"], cfg["NOC"]
    f32 = np.float32
    perm = np.concatenate([
        np.concatenate([np.arange(h * 64, (h + 1) * 64, 2),
                        np.arange(h * 64 + 1, (h + 1) * 64, 2)])
        for h in range(NH)])
    com = {
        "wqT": np.ascontiguousarray(np.asarray(W_q, f32).T[:, perm]),
        "wkT": np.ascontiguousarray(np.asarray(W_k, f32).T[:, perm]),
        "wvT": np.ascontiguousarray(np.asarray(W_v, f32).T),
        "woT": np.ascontiguousarray(np.asarray(W_o, f32).T),
        "w1T": np.ascontiguousarray(np.asarray(w1, f32).T),
        "w1n": np.ascontiguousarray(np.asarray(w1, f32)),
        "w2T": np.ascontiguousarray(np.asarray(w2, f32).T),
        "w2n": np.ascontiguousarray(np.asarray(w2, f32)),
        "b1c": np.ascontiguousarray(np.asarray(b1, f32).reshape(NHC, 128).T),
        "b2c": np.ascontiguousarray(np.asarray(b2, f32).reshape(NOC, 128).T),
    }
    Pm = np.zeros((128, 128), f32)
    for r in range(128):
        Pm[r, r ^ 32] = 1.0
    com["Pm"] = Pm
    inv = (BASE ** (-(np.arange(32, dtype=np.float64)) / 32.0))
    xnp = np.asarray(x, f32)
    in_maps = []
    for c in range(8):
        b, q0 = c // 4, (c % 4) * LPC
        pos = (q0 + np.arange(LPC, dtype=np.float64))
        ang = np.outer(inv, pos)
        cc = np.cos(ang).astype(f32)
        ss = np.sin(ang).astype(f32)
        CC = np.tile(cc, (4, 1))
        SS = np.concatenate([-ss, ss, -ss, ss], axis=0)
        m = dict(com)
        m["CC"] = np.ascontiguousarray(CC)
        m["SS"] = np.ascontiguousarray(SS)
        m["xT"] = np.ascontiguousarray(xnp[b, q0:q0 + LPC, :].T)
        in_maps.append(m)
    return in_maps


def kernel(L0_attn_x, L0_attn_W_q, L0_attn_W_k, L0_attn_W_v, L0_attn_W_o,
           L0_ffn_w1, L0_ffn_b1, L0_ffn_w2, L0_ffn_b2):
    cfg = make_cfg()
    in_maps = _stage_inputs(cfg, L0_attn_x, L0_attn_W_q, L0_attn_W_k, L0_attn_W_v,
                            L0_attn_W_o, L0_ffn_w1, L0_ffn_b1, L0_ffn_w2, L0_ffn_b2)
    nc = build_program(cfg, detect_races=False)
    res = run_bass_kernel_spmd(nc, in_maps, core_ids=list(range(8)))
    LPC = cfg["LPC"]
    out = np.empty((2, 4 * LPC, cfg["D"]), np.float32)
    for c in range(8):
        b, q0 = c // 4, (c % 4) * LPC
        out[b, q0:q0 + LPC, :] = res.results[c]["yT"].T
    return out


# version 22
# speedup vs baseline: 1.0291x; 1.0080x over previous
import math
import numpy as np

import concourse.bass as bass
import concourse.mybir as mybir
from concourse.bass_utils import run_bass_kernel_spmd

F32 = mybir.dt.float32
BF16 = mybir.dt.bfloat16
AF = mybir.ActivationFunctionType
ALU = mybir.AluOpType

BASE = 10000.0


def make_cfg(D=1024, LPC=2048, HID=4096):
    assert D % 128 == 0 and LPC % 512 == 0 and HID % 1024 == 0
    return dict(
        D=D, LPC=LPC, HID=HID,
        NH=D // 64, NHP=D // 128, NFC=D // 128,
        NTB=LPC // 512, NKC=4 * LPC // 128,
        NHC=HID // 128, NOC=D // 128, L=4 * LPC,
    )


def build_program(cfg, detect_races=True, dbg=False, gelu_copy=False):
    D, LPC, HID = cfg["D"], cfg["LPC"], cfg["HID"]
    NFC, NHP, NTB, NKC = cfg["NFC"], cfg["NHP"], cfg["NTB"], cfg["NKC"]
    NHC, NOC = cfg["NHC"], cfg["NOC"]
    from contextlib import ExitStack

    nc = bass.Bass("TRN2", target_bir_lowering=False,
                   detect_race_conditions=detect_races)

    dt = nc.dram_tensor
    xT = dt("xT", [D, LPC], F32, kind="ExternalInput")
    wqT = dt("wqT", [D, D], F32, kind="ExternalInput")
    wkT = dt("wkT", [D, D], F32, kind="ExternalInput")
    wvT = dt("wvT", [D, D], F32, kind="ExternalInput")
    woT = dt("woT", [D, D], F32, kind="ExternalInput")
    w1T = dt("w1T", [D, HID], F32, kind="ExternalInput")
    w1n = dt("w1n", [HID, D], F32, kind="ExternalInput")
    w2T = dt("w2T", [HID, D], F32, kind="ExternalInput")
    w2n = dt("w2n", [D, HID], F32, kind="ExternalInput")
    b1c = dt("b1c", [128, NHC], F32, kind="ExternalInput")
    b2c = dt("b2c", [128, NOC], F32, kind="ExternalInput")
    CCd = dt("CC", [128, LPC], F32, kind="ExternalInput")
    SSd = dt("SS", [128, LPC], F32, kind="ExternalInput")
    Pmd = dt("Pm", [128, 128], F32, kind="ExternalInput")
    yT = dt("yT", [D, LPC], F32, kind="ExternalOutput")

    dk = dict(kind="ExternalOutput") if dbg else {}
    agk_i = dt("agk_i", [D, LPC], BF16)
    agk_o = dt("agk_o", [4 * D, LPC], BF16)
    agv_i = dt("agv_i", [LPC, D], BF16)
    agv_o = dt("agv_o", [4 * LPC, D], BF16)
    sgn1 = dt("sgn1", [D, HID], BF16, **dk)
    sgn2 = dt("sgn2", [HID, D], BF16, **dk)
    atu_d = dt("atu_d", [D, LPC], BF16, **dk)
    if dbg:
        dbgK = dt("dbgK", [D, LPC], BF16, kind="ExternalOutput")
        dbgV = dt("dbgV", [LPC, D], BF16, kind="ExternalOutput")
        dbgA = dt("dbgA", [D, LPC], F32, kind="ExternalOutput")
        dbgB = dt("dbgB", [D, LPC], F32, kind="ExternalOutput")
        dbgE = dt("dbgE", [D, LPC], F32, kind="ExternalOutput")
        dbgC = dt("dbgC", [D, LPC], BF16, kind="ExternalOutput")
        dbgG = dt("dbgG", [HID, LPC], BF16, kind="ExternalOutput")
        dbgW = dt("dbgW", [128, 2 * (NHC + NOC)], F32, kind="ExternalOutput")
        dbgP = dt("dbgP", [128, 512], BF16, kind="ExternalOutput")
        dbgW2 = dt("dbgW2", [128, 2 * (NHC + NOC)], F32, kind="ExternalOutput")

    ctx = ExitStack()

    sb = nc.alloc_sbuf_tensor
    qrope = sb("qrope", [128, NHP * LPC], BF16)
    lns = sb("lns", [128, NFC * 512], BF16)
    xq = sb("xq", [128, NFC * 512], F32)
    guard1 = sb("guard1", [128, 256], F32)
    a2f = sb("a2f", [128, NFC * 512], F32)
    guard2 = sb("guard2", [128, 256], F32)
    wbuf = [sb("wbuf0", [128, NFC * D], BF16), sb("wbuf1", [128, NFC * D], BF16)]
    pt = sb("pt", [128, 3072], BF16)
    atu_t = sb("atu_t", [128, 512], BF16)
    den_s = sb("den_s", [128, 512], F32)
    recin = sb("recin", [128, 512], F32)
    recb = sb("recb", [128, 512], F32)
    rows = sb("rows", [1, 2048], F32)
    onesD = sb("onesD", [128, 1], F32)
    onesb = sb("onesb", [128, 1], BF16)
    onesr = sb("onesr", [1, 64], F32)
    a1c = sb("a1c", [128, NHC + 2], F32)
    a2c = sb("a2c", [128, NOC + 2], F32)
    b1s = sb("b1s", [128, NHC + 2], F32)
    b2s = sb("b2s", [128, NOC + 2], F32)
    t1s = sb("t1s", [128, 512], F32)
    t2s = sb("t2s", [128, 512], F32)
    phb = sb("phb", [128, max(NHC * 256, NFC * 512)], BF16)
    guard3 = sb("guard3", [128, 256], BF16)
    s1b = [sb("s1b%d" % i, [128, NFC * 128], BF16) for i in range(4)]
    s2b = [sb("s2b%d" % i, [128, D], BF16) for i in range(4)]

    r1 = ExitStack()
    CCs = r1.enter_context(nc.sbuf_tensor("CCs", [128, LPC], F32))
    SSs = r1.enter_context(nc.sbuf_tensor("SSs", [128, LPC], F32))
    Pms = r1.enter_context(nc.sbuf_tensor("Pms", [128, 128], F32))
    wst = [r1.enter_context(nc.sbuf_tensor("wst0", [128, 1024], F32)),
           r1.enter_context(nc.sbuf_tensor("wst1", [128, 1024], F32))]
    qpre = [r1.enter_context(nc.sbuf_tensor("qpre%d" % i, [128, 512], F32))
            for i in range(3)]
    sgs = [r1.enter_context(nc.sbuf_tensor("sgs0", [128, 512], BF16)),
           r1.enter_context(nc.sbuf_tensor("sgs1", [128, 512], BF16))]
    krt = [r1.enter_context(nc.sbuf_tensor("krt0", [128, 512], BF16)),
           r1.enter_context(nc.sbuf_tensor("krt1", [128, 512], BF16))]
    vlc = [r1.enter_context(nc.sbuf_tensor("vlc0", [128, D], BF16)),
           r1.enter_context(nc.sbuf_tensor("vlc1", [128, D], BF16))]

    ps = nc.alloc_psum_tensor
    ps_s = ps("ps_s", [128, 3072], F32)
    ps_o = ps("ps_o", [128, 512], F32)
    ps_d = ps("ps_d", [128, 512], F32)
    ps_g1 = ps_s[:, 0:512]
    ps_g2 = ps_s[:, 512:1024]
    bcA = ps_s[:, 2048:2560]
    bcB = ps_s[:, 2560:3072]
    sw_slots = [bcA, bcB]

    ENGS = ("pe", "act", "dve", "pool", "sync")
    steps = {k: [] for k in ENGS}
    sems, ctr = {}, {}
    for name in ("sc", "sw0", "sw1", "sx", "sao", "skt", "svt",
                 "skt1", "skt2", "skt3", "svt1", "svt2", "svt3",
                 "s10", "s11", "s12", "s13", "s20", "s21", "s22", "s23",
                 "kr0", "kr1", "vl0", "vl1", "sg0", "sg1", "swc0", "swc1",
                 "scc", "pe", "act", "dve", "sat", "sout"):
        sems[name] = ctx.enter_context(nc.semaphore(name))
        ctr[name] = 0

    def wait(eng, sem, val):
        if val is None or val <= 0:
            return
        steps[eng].append(lambda e, s=sems[sem], v=val: e.wait_ge(s, v))

    def op(eng, fn, inc=None, n=1):
        if inc is None:
            steps[eng].append(fn)
            return None
        ctr[inc] += n
        steps[eng].append(lambda e, fn=fn, s=sems[inc], n=n: fn(e).then_inc(s, n))
        return ctr[inc]

    def mm(out, lhsT, rhs, start, stop, tp=None, inc=None, sgc=False):
        return op("pe", lambda e, o=out, l=lhsT, r=rhs, s=start, t=stop, p=tp,
                  g=sgc: e.matmul(o, l, r, start=s, stop=t, tile_position=p,
                                  skip_group_check=g), inc=inc)

    def dma(eng, dst, src, inc):
        return op(eng, lambda e, d=dst, s=src: e.dma_start(out=d, in_=s), inc=inc, n=16)

    dma("sync", CCs[:, :], CCd[:, :], "sc")
    dma("sync", SSs[:, :], SSd[:, :], "sc")
    dma("sync", Pms[:, :], Pmd[:, :], "sc")
    dma("sync", b1s[:, 0:NHC], b1c[:, :], "sc")
    v_consts = dma("sync", b2s[:, 0:NOC], b2c[:, :], "sc")
    for _eng in ("pe", "act", "dve"):
        wait(_eng, "sc", v_consts)

    op("dve", lambda e: e.memset(a1c[:, :], 0.0))
    op("dve", lambda e: e.memset(a2c[:, :], 0.0))
    op("dve", lambda e: e.memset(onesD[:, :], 1.0 / D))
    op("dve", lambda e: e.memset(onesb[:, :], 1.0))
    op("dve", lambda e: e.memset(onesr[:, :], 1.0))
    op("dve", lambda e: e.memset(ps_d[:, :], 0.0))
    op("dve", lambda e: e.memset(ps_s[:, :], 0.0))
    v_init = op("dve", lambda e: e.memset(ps_o[:, :], 0.0), inc="dve")

    def layer_norm_stats(src, bc1, bc2, pre_pe=None, pre_dve=None, pre_act=None,
                         pre_sd=None):
        if pre_sd:
            wait("pe", pre_sd[0], pre_sd[1])
        wait("pe", "pe", pre_pe)
        wait("pe", "act", pre_act)
        wait("pe", "dve", pre_dve)
        for fc in range(NFC):
            mm(ps_d[0:1, :], onesD[:, :], src[:, fc * 512:(fc + 1) * 512],
               fc == 0, fc == NFC - 1, tp=(0, 0), sgc=True)
        msqv = {}
        if pre_sd:
            wait("dve", pre_sd[0], pre_sd[1])
        wait("dve", "act", pre_act)
        wait("dve", "dve", pre_dve)
        for fc in range(NFC):
            tscr = t1s if fc % 2 == 0 else t2s
            if fc >= 2:
                wait("dve", "pe", msqv[fc - 2])
            sqv = op("dve", lambda e, t=tscr, s=src, f=fc:
                     e.tensor_tensor(t[:, :], s[:, f * 512:(f + 1) * 512],
                                     s[:, f * 512:(f + 1) * 512], ALU.mult),
                     inc="dve")
            wait("pe", "dve", sqv)
            msqv[fc] = mm(ps_d[64:65, :], onesD[:, :], tscr[:, :],
                          fc == 0, fc == NFC - 1, tp=(0, 64), inc="pe", sgc=True)
        return ln_tail(bc1, bc2, msqv[NFC - 1])

    def ln_tail(bc1, bc2, last_msq):
        wait("dve", "pe", last_msq)
        op("dve", lambda e: e.tensor_copy(rows[0:1, 0:512], ps_d[0:1, :]))
        op("dve", lambda e: e.tensor_copy(rows[0:1, 1024:1536], ps_d[64:65, :]))
        op("dve", lambda e: e.tensor_tensor(rows[0:1, 1536:2048], rows[0:1, 0:512],
                                            rows[0:1, 0:512], ALU.mult))
        op("dve", lambda e: e.tensor_tensor(
            rows[0:1, 1024:1536], rows[0:1, 1024:1536], rows[0:1, 1536:2048],
            ALU.subtract))
        dv = op("dve", lambda e: e.tensor_scalar_add(
            rows[0:1, 1024:1536], rows[0:1, 1024:1536], 1e-5), inc="dve")
        wait("act", "dve", dv)
        op("act", lambda e: e.activation(rows[0:1, 1024:1536], rows[0:1, 1024:1536],
                                         AF.Ln))
        av = op("act", lambda e: e.activation(rows[0:1, 512:1024],
                                              rows[0:1, 1024:1536],
                                              AF.Exp, scale=-0.5), inc="act")
        wait("pe", "act", av)
        mm(bc1[0:64, :], onesr[:, :], rows[0:1, 0:512], True, True, tp=(0, 0))
        mm(bc1[64:128, :], onesr[:, :], rows[0:1, 0:512], True, True, tp=(0, 64))
        mm(bc2[0:64, :], onesr[:, :], rows[0:1, 512:1024], True, True, tp=(0, 0))
        return mm(bc2[64:128, :], onesr[:, :], rows[0:1, 512:1024], True, True,
                  tp=(0, 64), inc="pe")

    wst_war = [None, None]
    wbuf_war = [None, None]

    def load_weight(wdram, par):
        wait("pool", "pe", wbuf_war[par])
        return dma("pool",
                   wbuf[par][:, :].rearrange("p (c n) -> p c n", n=D),
                   wdram[:, :].rearrange("(c p) n -> p c n", p=128),
                   "swc%d" % par)

    widx = 0
    kv_stores = 0
    evac_war3 = [None, None, None]
    evac_war4 = [None, None, None, None]
    slot_t2 = [None, None]
    qp_rd_pe = [None, None, None]
    qp_rd_dve = [None, None, None]
    kst = [None, None]
    vst = [None, None]
    acc_war = None

    prev_bcv = None
    for tb in range(NTB):
        if tb > 0:
            wait("sync", "pe", prev_bcv)
            wait("sync", "dve", lns_ready)
        x_sd = dma("sync", xq[:, :].rearrange("p (c t) -> p c t", t=512),
                   xT[:, tb * 512:(tb + 1) * 512]
                   .rearrange("(c p) t -> p c t", p=128), "sx")
        bcv = layer_norm_stats(xq, bcA, bcB, pre_sd=("sx", x_sd),
                               pre_dve=max(v_init if tb == 0 else 0,
                                           acc_war or 0) or None)
        lastap = None
        wait("dve", "pe", bcv)
        for fc in range(NFC):
            op("dve", lambda e, f=fc: e.tensor_tensor(
                t1s[:, :], xq[:, f * 512:(f + 1) * 512], bcA, ALU.subtract))
            lastap = op("dve", lambda e, f=fc: e.tensor_tensor(
                lns[:, f * 512:(f + 1) * 512], t1s[:, :], bcB, ALU.mult),
                inc="dve")
        lns_ready = lastap
        prev_bcv = bcv

        for (wdram, is_q) in ((wqT, True), (wkT, False)):
            par = widx % 2
            widx += 1
            castv = load_weight(wdram, par)
            wbuf_war[par] = None
            prjs = {}

            def emit_swap(oc, tb=tb):
                qp = qpre[oc % 3]
                slot = sw_slots[oc % 2]
                wait("pe", "act", evac_war3[oc % 3])
                wait("pe", "dve", slot_t2[oc % 2])
                swv = mm(slot, Pms[:, :], qp[:, :], True, True, inc="pe")
                qp_rd_pe[oc % 3] = swv
                wait("dve", "act", evac_war3[oc % 3])
                dv1 = op("dve", lambda e, q=qp, t=tb: e.tensor_tensor(
                    t1s[:, :], q[:, :], CCs[:, t * 512:(t + 1) * 512], ALU.mult),
                    inc="dve")
                qp_rd_dve[oc % 3] = dv1
                wait("dve", "pe", swv)
                t2v = op("dve", lambda e, s=slot, t=tb: e.tensor_tensor(
                    t2s[:, :], s, SSs[:, t * 512:(t + 1) * 512], ALU.mult),
                    inc="dve")
                slot_t2[oc % 2] = t2v
                if is_q:
                    dst = qrope[:, oc * LPC + tb * 512: oc * LPC + (tb + 1) * 512]
                    op("dve", lambda e, d=dst: e.tensor_tensor(
                        d, t1s[:, :], t2s[:, :], ALU.add))
                else:
                    kb = krt[oc % 2]
                    wait("dve", "kr%d" % (oc % 2), kst[oc % 2])
                    rv = op("dve", lambda e, d=kb: e.tensor_tensor(
                        d[:, :], t1s[:, :], t2s[:, :], ALU.add), inc="dve")
                    wait("pool", "dve", rv)
                    kst[oc % 2] = dma(
                        "pool",
                        agk_i[oc * 128:(oc + 1) * 128, tb * 512:(tb + 1) * 512],
                        kb[:, :], "kr%d" % (oc % 2))

            for oc in range(NHP):
                acc = ps_s[:, (oc % 4) * 512:(oc % 4) * 512 + 512]
                if oc == 0:
                    wait("pe", "swc%d" % par, castv)
                    wait("pe", "dve", lns_ready)
                wait("pe", "act", evac_war4[oc % 4])
                prj = None
                for fc in range(NFC):
                    prj = mm(acc,
                             wbuf[par][:, fc * D + oc * 128: fc * D + (oc + 1) * 128],
                             lns[:, fc * 512:(fc + 1) * 512],
                             fc == 0, fc == NFC - 1,
                             inc="pe" if fc == NFC - 1 else None)
                prjs[oc] = prj
                wbuf_war[par] = prj
                qp = qpre[oc % 3]
                wait("act", "pe", prj)
                wait("act", "pe", qp_rd_pe[oc % 3])
                wait("act", "dve", qp_rd_dve[oc % 3])
                ev = op("act", lambda e, q=qp, a=acc: e.copy(q[:, :], a[:, :]),
                        inc="act")
                evac_war3[oc % 3] = ev
                evac_war4[oc % 4] = ev
                if oc >= 1:
                    emit_swap(oc - 1)
            emit_swap(NHP - 1)

        par = widx % 2
        widx += 1
        castv = load_weight(wvT, par)
        wbuf_war[par] = None
        for tc in range(4):
            vb = vlc[tc % 2]
            prj = None
            if tc == 0:
                wait("pe", "swc%d" % par, castv)
                wait("pe", "dve", lns_ready)
            wait("pe", "dve", acc_war)
            vw = min(512, D)
            if tc == 0:
                wait("pe", "act", max(v for v in evac_war4 if v) if any(evac_war4) else None)
            for oh in range(max(1, D // 512)):
                accv = ps_g1 if oh % 2 == 0 else ps_g2
                for fc in range(NFC):
                    prj = mm(accv[:, 0:vw],
                             lns[:, fc * 512 + tc * 128: fc * 512 + (tc + 1) * 128],
                             wbuf[par][:, fc * D + oh * vw: fc * D + (oh + 1) * vw],
                             fc == 0, fc == NFC - 1,
                             inc="pe" if fc == NFC - 1 else None)
            wbuf_war[par] = prj
            wait("dve", "pe", prj)
            wait("dve", "vl%d" % (tc % 2), vst[tc % 2])
            for oh in range(max(1, D // 512)):
                accv = ps_g1 if oh % 2 == 0 else ps_g2
                vv = op("dve", lambda e, v=vb, a=accv, o=oh, vw=vw: e.tensor_copy(
                    v[:, o * vw:(o + 1) * vw], a[:, 0:vw]), inc="dve")
            acc_war = vv
            wait("pool", "dve", vv)
            vst[tc % 2] = dma(
                "pool", agv_i[(tb * 4 + tc) * 128:(tb * 4 + tc + 1) * 128, :],
                vb[:, :], "vl%d" % (tc % 2))

    wo_par = widx % 2
    widx += 1
    wo_cast = load_weight(woT, wo_par)
    wbuf_war[wo_par] = None

    r1_marks = dict(ctr)
    sgstage = [xq[:, 0:1024], xq[:, 1024:2048]]
    sgout = [s2b[2][:, 0:512], s2b[3][:, 0:512]]
    wait("sync", "pe", ctr["pe"])
    wait("sync", "dve", ctr["dve"])
    sg_war = [None, None]
    i = 0
    sg_hist = {}
    for (src_d, dst_d, RP, CT) in ((w1T, sgn1, NFC, HID), (w2T, sgn2, NHC, D)):
        CJ, CW = max(1, CT // 512), min(512, CT)
        for rp in range(RP):
            for j in range(CJ):
                k = i % 2
                wait("sync", "act", sg_hist.get(i - 2))
                ld = dma("sync", sgstage[k][:, 0:CW],
                         src_d[rp * 128:(rp + 1) * 128, j * CW:(j + 1) * CW],
                         "sw%d" % k)
                wait("act", "sw%d" % k, ld)
                wait("act", "sg%d" % k, sg_war[k])
                sv = op("act", lambda e, k=k, CW=CW: e.sign(sgout[k][:, 0:CW],
                                                            sgstage[k][:, 0:CW]),
                        inc="act")
                sg_hist[i] = sv
                wait("pool", "act", sv)
                sg_war[k] = dma("pool",
                                dst_d[rp * 128:(rp + 1) * 128, j * CW:(j + 1) * CW],
                                sgout[k][:, 0:CW], "sg%d" % k)
                i += 1
    wait("sync", "act", ctr["act"])
    astage = [xq[:, 2048:3072], xq[:, 3072:4096]]
    ab_hist = {}
    ai = 0
    prev_abs = None
    for (src_d, col_t, NCH, DEN) in ((w1n, a1c, NHC, D), (w2n, a2c, NOC, HID)):
        pieces = max(1, DEN // 1024)
        pw = min(DEN, 1024)
        for ch in range(NCH):
            for pz in range(pieces):
                k = ai % 2
                wait("sync", "act", ab_hist.get(ai - 2))
                ldv = dma("sync", astage[k][:, 0:pw],
                          src_d[ch * 128:(ch + 1) * 128, pz * pw:(pz + 1) * pw],
                          "sw%d" % k)
                wait("act", "sw%d" % k, ldv)
                dst_col = col_t[:, NCH:NCH + 1] if pieces > 1 else col_t[:, ch:ch + 1]
                prev_abs = op("act", lambda e, pw=pw, dc=dst_col, k=k: e.activation(
                    s1b[k][:, 0:pw], astage[k][:, 0:pw], AF.Abs, accum_out=dc),
                    inc="act")
                ab_hist[ai] = prev_abs
                ai += 1
                if pieces > 1:
                    wait("dve", "act", prev_abs)
                    if pz == 0:
                        op("dve", lambda e, ct=col_t, c=ch, n=NCH: e.tensor_copy(
                            ct[:, c:c + 1], ct[:, n:n + 1]))
                    else:
                        op("dve", lambda e, ct=col_t, c=ch, n=NCH: e.tensor_tensor(
                            ct[:, c:c + 1], ct[:, c:c + 1], ct[:, n:n + 1], ALU.add))
                    prev_dve = op("dve", lambda e, ct=col_t, c=ch: e.tensor_copy(
                        t2s[:, 0:1], ct[:, c:c + 1]), inc="dve")
                    wait("act", "dve", prev_dve)
        wait("dve", "act", prev_abs)
        op("dve", lambda e, ct=col_t, n=NCH, d=DEN: e.tensor_scalar_mul(
            ct[:, 0:n], ct[:, 0:n], 1.0 / d))
    alpha_act_mark = ctr["act"]
    alpha_done = op("dve", lambda e: e.tensor_copy(t1s[:, 0:1], a1c[:, 0:1]),
                    inc="dve")

    if dbg:
        for _s in ("kr0", "kr1", "vl0", "vl1"):
            wait("pool", _s, ctr[_s])
        dma("pool", dbgK[:, :], agk_i[:, :], "sg0")
        dma("pool", dbgV[:, :], agv_i[:, :], "sg0")
    rgs = [[0, 1, 2, 3], [4, 5, 6, 7]]
    for _s in ("kr0", "kr1"):
        wait("pool", _s, ctr[_s])
    op("pool", lambda e: e.collective_compute(
        "AllGather", ALU.bypass, replica_groups=rgs,
        ins=[agk_i.ap().opt()], outs=[agk_o.ap().opt()]), inc="scc", n=1)
    for _s in ("vl0", "vl1"):
        wait("pool", _s, ctr[_s])
    op("pool", lambda e: e.collective_compute(
        "AllGather", ALU.bypass, replica_groups=rgs,
        ins=[agv_i.ap().opt()], outs=[agv_o.ap().opt()]), inc="scc", n=1)

    if dbg:
        wait("sync", "dve", alpha_done)
        dma("sync", dbgW[:, 0:NHC], a1c[:, 0:NHC], "sat")
        dma("sync", dbgW[:, NHC:2 * NHC], b1s[:, 0:NHC], "sat")
        dma("sync", dbgW[:, 2 * NHC:2 * NHC + NOC], a2c[:, 0:NOC], "sat")
        dma("sync", dbgW[:, 2 * NHC + NOC:], b2s[:, 0:NOC], "sat")
    r1.close()
    ktb = sb("ktb", [128, 4 * LPC], BF16)
    vtb = sb("vtb", [128, 4 * LPC], BF16)

    exp_free = {0: None, 1: None, 2: None}
    pt_free = {0: None, 1: None, 2: None}
    attn_last_pe = None
    sat_last = None
    po_war = None
    pd_war = None

    RCH = NKC // 4
    ktsem = ["skt", "skt1", "skt2", "skt3"]
    vtsem = ["svt", "svt1", "svt2", "svt3"]
    kt_rv = [None] * 4
    vt_rv = [None] * 4

    def load_kv(hp, gate_scv=None, gate_pvs=None):
        for r in range(4):
            if gate_scv is not None:
                wait("sync", "pe", gate_scv[(r + 1) * RCH - 1])
            kt_rv[r] = dma("sync", ktb[:, r * LPC:(r + 1) * LPC],
                           agk_o[r * D + hp * 128: r * D + (hp + 1) * 128, :],
                           ktsem[r])
            if gate_pvs is not None:
                wait("sync", "pe", gate_pvs[(r + 1) * RCH - 1])
            for q4 in range(LPC // 512):
                base = (r * (LPC // 128) + q4 * 4) * 128
                vt_rv[r] = dma(
                    "sync",
                    vtb[:, base:base + 512].rearrange("p (c d) -> p c d", d=128),
                    agv_o[r * LPC + q4 * 512: r * LPC + (q4 + 1) * 512,
                          hp * 128:(hp + 1) * 128]
                    .rearrange("(c p) d -> p c d", p=128), vtsem[r])

    fin_pend = None

    def fin_stage_a():
        wait("dve", "pe", fin_pend["denv"])
        fin_pend["pdw"] = op("dve", lambda e: e.tensor_copy(den_s[:, :],
                                                            ps_d[:, :]),
                             inc="dve")
        op("dve", lambda e: e.tensor_copy(t1s[0:64, :], den_s[64:128, :]))
        op("dve", lambda e: e.tensor_tensor(
            recin[0:64, :], den_s[0:64, :], t1s[0:64, :], ALU.add))
        fin_pend["dnr"] = op("dve", lambda e: e.tensor_copy(recb[0:1, :],
                                                            recin[32:33, :]),
                             inc="dve")

    def fin_stage_b():
        wait("pe", "dve", fin_pend["dnr"])
        mm(ps_d[0:64, :], onesr[:, :], recin[0:1, :], True, True, tp=(0, 0))
        fin_pend["bcv"] = mm(ps_d[64:128, :], onesr[:, :], recb[0:1, :],
                             True, True, tp=(0, 64), inc="pe")

    def fin_stage_c():
        nonlocal pd_war, po_war, sat_last
        wait("dve", "pe", fin_pend["bcv"])
        pd_war = op("dve", lambda e: e.tensor_copy(recin[:, :], ps_d[:, :]),
                    inc="dve")
        op("dve", lambda e: e.reciprocal(recb[:, :], recin[:, :]))
        wait("dve", "sat", sat_last)
        po_war = op("dve", lambda e: e.tensor_tensor(
            atu_t[:, :], ps_o[:, :], recb[:, :], ALU.mult), inc="dve")
        wait("sync", "dve", po_war)
        sat_last = dma(
            "sync",
            atu_d[fin_pend["hp"] * 128:(fin_pend["hp"] + 1) * 128,
                  fin_pend["qb"] * 512:(fin_pend["qb"] + 1) * 512],
            atu_t[:, :], "sat")

    for hp in range(NHP):
        if hp == 0:
            wait("sync", "scc", 2)
            for _s in ("kr0", "kr1", "vl0", "vl1"):
                wait("sync", _s, r1_marks[_s])
            wait("sync", "dve", r1_marks["dve"])
            wait("sync", "act", r1_marks["act"])
            wait("sync", "pe", r1_marks["pe"])
            load_kv(0)

        for qb in range(NTB):
            qsl = slice(hp * LPC + qb * 512, hp * LPC + (qb + 1) * 512)
            denv = None
            scv = {}
            pvs = {}
            LKA = 2
            for ci in range(NKC + LKA):
                if ci < NKC:
                    c = ci
                    w = c % 3
                    wait("pe", "act", exp_free[w])
                    if qb == 0 and c % RCH == 0:
                        wait("pe", ktsem[c // RCH], kt_rv[c // RCH])
                    mm(ps_s[:, w * 1024: w * 1024 + 512],
                       ktb[0:64, c * 128:(c + 1) * 128], qrope[0:64, qsl],
                       True, True, tp=(0, 0))
                    scv[c] = mm(ps_s[:, w * 1024 + 512: w * 1024 + 1024],
                                ktb[64:128, c * 128:(c + 1) * 128],
                                qrope[64:128, qsl],
                                True, True, tp=(64, 0), inc="pe")
                    if fin_pend is not None:
                        if ci == 0:
                            fin_stage_a()
                        elif ci == 1:
                            fin_stage_b()
                        elif ci == 2:
                            fin_stage_c()
                            fin_pend = None
                if ci >= LKA:
                    c = ci - LKA
                    w = c % 3
                    wait("act", "pe", scv[c])
                    wait("act", "pe", pt_free[w])
                    expv = op("act", lambda e, w=w: e.activation(
                        pt[:, w * 1024:(w + 1) * 1024],
                        ps_s[:, w * 1024:(w + 1) * 1024],
                        AF.Exp, scale=0.125), inc="act")
                    exp_free[w] = expv
                    wait("pe", "act", expv)
                    if c == 0:
                        wait("pe", "dve", po_war)
                    if qb == 0 and c % RCH == 0:
                        wait("pe", vtsem[c // RCH], vt_rv[c // RCH])
                    mm(ps_o[0:64, :], vtb[:, c * 128: c * 128 + 64],
                       pt[:, w * 1024: w * 1024 + 512],
                       c == 0, c == NKC - 1, tp=(0, 0), sgc=True)
                    pvv = mm(ps_o[64:128, :], vtb[:, c * 128 + 64: c * 128 + 128],
                             pt[:, w * 1024 + 512: w * 1024 + 1024],
                             c == 0, c == NKC - 1, tp=(0, 64), inc="pe", sgc=True)
                    pt_free[w] = pvv
                    pvs[c] = pvv
                    if c % 2 == 1:
                        g = c // 2
                        wp = (c - 1) % 3
                        if g == 0:
                            wait("pe", "dve", pd_war)
                        st = (g == 0)
                        sp = (g == NKC // 2 - 1)
                        mm(ps_d[0:1, :], onesb[:, :],
                           pt[:, wp * 1024: wp * 1024 + 512], st, sp, tp=(0, 0),
                           sgc=True)
                        mm(ps_d[32:33, :], onesb[:, :],
                           pt[:, wp * 1024 + 512: wp * 1024 + 1024], st, sp,
                           tp=(0, 32), sgc=True)
                        mm(ps_d[64:65, :], onesb[:, :],
                           pt[:, w * 1024: w * 1024 + 512], st, sp,
                           tp=(0, 64), sgc=True)
                        denv = mm(ps_d[96:97, :], onesb[:, :],
                                  pt[:, w * 1024 + 512: w * 1024 + 1024], st, sp,
                                  tp=(0, 96), inc="pe", sgc=True)
                        pt_free[wp] = denv
                        pt_free[w] = denv
            attn_last_pe = ctr["pe"]
            if qb == NTB - 1 and hp + 1 < NHP:
                load_kv(hp + 1, gate_scv=scv, gate_pvs=pvs)
            fin_pend = {"denv": denv, "hp": hp, "qb": qb}

    fin_stage_a()
    fin_stage_b()
    fin_stage_c()
    fin_pend = None

    awo_b = [vtb[:, 0:NFC * 512], vtb[:, NFC * 512:2 * NFC * 512]]
    xq_b = [xq[:, :], ktb[:, :].bitcast(F32)]
    aw_sds = [None, None]
    x_sds = [None, None]
    s1_war = [None, None, None, None]
    s2_war = [None, None, None, None]
    ffn2_last = None
    y_dve = None
    prev_fev = None
    wo_prj_last = None
    y_fcs_prev = None

    aw_sem = ["sao", "svt"]
    x_sem = ["sx", "skt"]

    def load_qb(j):
        aw_sds[j % 2] = dma("sync", awo_b[j % 2].rearrange("p (c t) -> p c t",
                                                           t=512),
                            atu_d[:, j * 512:(j + 1) * 512]
                            .rearrange("(c p) t -> p c t", p=128), aw_sem[j % 2])
        x_sds[j % 2] = dma("sync", xq_b[j % 2].rearrange("p (c t) -> p c t",
                                                         t=512),
                           xT[:, j * 512:(j + 1) * 512]
                           .rearrange("(c p) t -> p c t", p=128), x_sem[j % 2])

    for qb in range(NTB):
        awo = awo_b[qb % 2]
        xqs = xq_b[qb % 2]
        aw_sd = aw_sds[qb % 2]
        x_sd = x_sds[qb % 2]
        if qb == 0:
            wait("sync", "sat", ctr["sat"])
            load_qb(0)
            aw_sd, x_sd = aw_sds[0], x_sds[0]
            if NTB > 1:
                load_qb(1)
        elif qb + 1 < NTB:
            wait("sync", "pe", wo_prj_last)
            wait("sync", "sout", 16 * qb)
            load_qb(qb + 1)
        if dbg:
            wait("act", "sat", ctr["sat"])
            wait("dve", "sat", ctr["sat"])
        a2vs = []
        prj = None
        for oc in range(NOC):
            acc = ps_s[:, (oc % 2) * 512:(oc % 2) * 512 + 512]
            if oc == 0:
                wait("pe", aw_sem[qb % 2], aw_sd)
                wait("pe", "swc%d" % wo_par, wo_cast)
                wait("pe", "dve", max(po_war or 0, prev_fev or 0) or None)
                if qb == 0:
                    wait("pe", "act", ctr["act"])
            if oc >= 2:
                wait("pe", "act", a2vs[oc - 2])
            for fc in range(NFC):
                prj = mm(acc[:, :],
                         wbuf[wo_par][:, fc * D + oc * 128: fc * D + (oc + 1) * 128],
                         awo[:, fc * 512:(fc + 1) * 512], fc == 0, fc == NFC - 1,
                         inc="pe" if fc == NFC - 1 else None)
            wo_prj_last = prj
            wait("act", "pe", prj)
            if y_fcs_prev is not None:
                wait("act", "dve", y_fcs_prev[oc])
            a2vs.append(op("act", lambda e, o=oc, a=acc: e.copy(
                a2f[:, o * 512:(o + 1) * 512], a[:, :]), inc="act"))
        a2v = a2vs[-1]
        if dbg:
            wait("sync", "act", a2v)
            dma("sync", dbgA[:, qb * 512:(qb + 1) * 512]
                .rearrange("(c p) t -> p c t", p=128),
                a2f[:, :].rearrange("p (c t) -> p c t", t=512), "sat")
        bc2m = ps_s[:, 0:512]
        bc2r = ps_s[:, 512:1024]
        msqv2 = {}
        wait("pe", "dve", prev_fev)
        for fc in range(NFC):
            wait("pe", "act", a2vs[fc])
            mm(ps_d[0:1, :], onesD[:, :], a2f[:, fc * 512:(fc + 1) * 512],
               fc == 0, fc == NFC - 1, tp=(0, 0), sgc=True)
            tscr = t1s if fc % 2 == 0 else t2s
            if fc >= 2:
                wait("dve", "pe", msqv2[fc - 2])
            wait("dve", "act", a2vs[fc])
            sqv = op("dve", lambda e, t=tscr, f=fc: e.tensor_tensor(
                t[:, :], a2f[:, f * 512:(f + 1) * 512],
                a2f[:, f * 512:(f + 1) * 512], ALU.mult), inc="dve")
            wait("pe", "dve", sqv)
            msqv2[fc] = mm(ps_d[64:65, :], onesD[:, :], tscr[:, :],
                           fc == 0, fc == NFC - 1, tp=(0, 64), inc="pe", sgc=True)
        bcv2 = ln_tail(bc2m, bc2r, msqv2[NFC - 1])
        wait("dve", "pe", bcv2)
        wait("dve", x_sem[qb % 2], x_sd)
        bc3m = ps_s[:, 1024:1536]
        bc3r = ps_s[:, 1536:2048]
        rv = None
        msqv3 = {}
        for fc in range(NFC):
            op("dve", lambda e, f=fc, b=bc2m: e.tensor_tensor(
                den_s[:, :], a2f[:, f * 512:(f + 1) * 512], b, ALU.subtract))
            op("dve", lambda e, b=bc2r: e.tensor_tensor(den_s[:, :], den_s[:, :],
                                                        b, ALU.mult))
            rv = op("dve", lambda e, f=fc, x=xqs: e.tensor_tensor(
                x[:, f * 512:(f + 1) * 512], x[:, f * 512:(f + 1) * 512],
                den_s[:, :], ALU.add), inc="dve")
            wait("pe", "dve", rv)
            mm(ps_d[0:1, :], onesD[:, :], xqs[:, fc * 512:(fc + 1) * 512],
               fc == 0, fc == NFC - 1, tp=(0, 0), sgc=True)
            tscr = t1s if fc % 2 == 0 else t2s
            if fc >= 2:
                wait("dve", "pe", msqv3[fc - 2])
            sqv = op("dve", lambda e, t=tscr, f=fc, x=xqs: e.tensor_tensor(
                t[:, :], x[:, f * 512:(f + 1) * 512],
                x[:, f * 512:(f + 1) * 512], ALU.mult), inc="dve")
            wait("pe", "dve", sqv)
            msqv3[fc] = mm(ps_d[64:65, :], onesD[:, :], tscr[:, :],
                           fc == 0, fc == NFC - 1, tp=(0, 64), inc="pe", sgc=True)
        if dbg:
            wait("sync", "dve", rv)
            dma("sync", dbgB[:, qb * 512:(qb + 1) * 512]
                .rearrange("(c p) t -> p c t", p=128),
                xqs.rearrange("p (c t) -> p c t", t=512), "sat")
        bcv3 = ln_tail(bc3m, bc3r, msqv3[NFC - 1])
        wait("dve", "pe", bcv3)
        ln3v = None
        lnv = {}
        for fc in range(NFC):
            op("dve", lambda e, f=fc, b=bc3m, x=xqs: e.tensor_tensor(
                t1s[:, :], x[:, f * 512:(f + 1) * 512], b, ALU.subtract))
            ln3v = op("dve", lambda e, f=fc, b=bc3r: e.tensor_tensor(
                lns[:, f * 512:(f + 1) * 512], t1s[:, :], b, ALU.mult),
                inc="dve")
            lnv[fc] = ln3v
        if dbg:
            wait("sync", "dve", ln3v)
            dma("sync", dbgC[:, qb * 512:(qb + 1) * 512]
                .rearrange("(c p) t -> p c t", p=128),
                lns[:, :].rearrange("p (c t) -> p c t", t=512), "sat")
        fev = None
        for half in range(2):
            if dbg:
                wait("act", "sat", ctr["sat"])
            gel = {}
            for idx, hc in enumerate(range(NHC)):
                hci = hc
                k = idx % 4
                wait("sync", "pe", s1_war[k])
                s1_sd = dma("sync",
                            s1b[k][:, :].rearrange("p (c h) -> p c h", h=128),
                            sgn1[:, hc * 128:(hc + 1) * 128]
                            .rearrange("(c p) h -> p c h", p=128), "s1%d" % k)
                acc = ps_s[:, (idx % 4) * 512:(idx % 4) * 512 + 512]
                wait("pe", "s1%d" % k, s1_sd)
                if idx < 4:
                    wait("pe", "dve", max(alpha_done, fev or 0))
                    if half == 0:
                        wait("pe", "act", a2v)
                wait("pe", "act", gel.get(idx - 4))
                prj = None
                for fc in range(NFC):
                    if idx == 0 and half == 0:
                        wait("pe", "dve", lnv[fc])
                    prj = mm(acc[:, 0:256],
                             s1b[k][:, fc * 128:(fc + 1) * 128],
                             lns[:, fc * 512 + half * 256: fc * 512 + half * 256 + 256],
                             fc == 0, fc == NFC - 1,
                             inc="pe" if fc == NFC - 1 else None)
                s1_war[k] = prj
                wait("act", "pe", prj)
                if dbg and qb == 0 and half == 0 and hc in (29, 31):
                    pcp = op("act", lambda e, a=acc, h=hc: e.copy(
                        atu_t[:, (0 if h == 31 else 256):(256 if h == 31 else 512)],
                        a[:, 0:256]), inc="act")
                    if hc == 29 or hc == 31:
                        dbg_pcp = pcp
                dst = phb[:, hc * 256:(hc + 1) * 256]
                gfn = AF.Identity if gelu_copy else AF.Gelu
                gel[hci] = op("act", lambda e, h=hc, a=acc, d=dst, gfn=gfn:
                    e.activation(d, a[:, 0:256], gfn,
                                 bias=b1s[:, h:h + 1], scale=a1c[:, h:h + 1]),
                    inc="act")
            if dbg and qb == 0 and half == 0:
                wait("sync", "act", dbg_pcp)
                dma("sync", dbgP[:, :], atu_t[:, :], "sat")
            if dbg:
                wait("sync", "act", max(gel.values()))
                dma("sync", dbgG[:, qb * 512 + half * 256: qb * 512 + half * 256 + 256]
                    .rearrange("(c p) t -> p c t", p=128),
                    phb[:, 0:NHC * 256].rearrange("p (c t) -> p c t", t=256), "sat")
            f2_tiles = [ps_s[:, 0:256], ps_s[:, 512:768], ps_s[:, 1024:1280],
                        ps_s[:, 1536:1792], ps_s[:, 2048:2304],
                        ps_s[:, 2560:2816], ps_o[:, 0:256], ps_d[:, 0:256]]
            f2 = None
            for hc in range(NHC):
                k = hc % 4
                wait("sync", "pe", s2_war[k])
                s2_sd = dma("sync", s2b[k][:, :],
                            sgn2[hc * 128:(hc + 1) * 128, :], "s2%d" % k)
                wait("pe", "s2%d" % k, s2_sd)
                wait("pe", "act", gel[hc])
                if hc == 0:
                    wait("pe", "act", max(gel.values()))
                    wait("pe", "dve", fev)
                for oc in range(NOC):
                    f2 = mm(f2_tiles[oc],
                            s2b[k][:, oc * 128:(oc + 1) * 128],
                            phb[:, hc * 256:(hc + 1) * 256],
                            hc == 0, hc == NHC - 1, sgc=True,
                            inc="pe" if oc == NOC - 1 else None)
                s2_war[k] = f2
            ffn2_last = f2
            wait("dve", "pe", f2)
            if dbg:
                wait("dve", "sat", ctr["sat"])
            for oc in range(NOC):
                fev = op("dve", lambda e, o=oc, h=half: e.tensor_scalar(
                    a2f[:, o * 512 + h * 256: o * 512 + h * 256 + 256],
                    f2_tiles[o],
                    a2c[:, o:o + 1], b2s[:, o:o + 1], ALU.mult, ALU.add),
                    inc="dve")
        prev_fev = fev
        if dbg:
            wait("sync", "dve", fev)
            dma("sync", dbgE[:, qb * 512:(qb + 1) * 512]
                .rearrange("(c p) t -> p c t", p=128),
                a2f[:, :].rearrange("p (c t) -> p c t", t=512), "sat")
        if dbg:
            wait("dve", "sat", ctr["sat"])
        bc4m = ps_s[:, 1024:1536]
        bc4r = ps_s[:, 1536:2048]
        bcv4 = layer_norm_stats(a2f, bc4m, bc4r, pre_dve=fev)
        wait("dve", "pe", bcv4)
        y_fcs = []
        for fc in range(NFC):
            op("dve", lambda e, f=fc, b=bc4m: e.tensor_tensor(
                t1s[:, :], a2f[:, f * 512:(f + 1) * 512], b, ALU.subtract))
            op("dve", lambda e, b=bc4r: e.tensor_tensor(t2s[:, :], t1s[:, :], b,
                                                        ALU.mult))
            y_dve = op("dve", lambda e, f=fc, x=xqs: e.tensor_tensor(
                x[:, f * 512:(f + 1) * 512], x[:, f * 512:(f + 1) * 512],
                t2s[:, :], ALU.add), inc="dve")
            y_fcs.append(y_dve)
        y_fcs_prev = y_fcs
        wait("sync", "dve", y_dve)
        dma("sync", yT[:, qb * 512:(qb + 1) * 512]
            .rearrange("(c p) t -> p c t", p=128),
            xqs.rearrange("p (c t) -> p c t", t=512), "sout")

    if dbg:
        wait("sync", "dve", ctr["dve"])
        wait("sync", "act", ctr["act"])
        dma("sync", dbgW2[:, 0:NHC], a1c[:, 0:NHC], "sat")
        dma("sync", dbgW2[:, NHC:2 * NHC], b1s[:, 0:NHC], "sat")
        dma("sync", dbgW2[:, 2 * NHC:2 * NHC + NOC], a2c[:, 0:NOC], "sat")
        dma("sync", dbgW2[:, 2 * NHC + NOC:], b2s[:, 0:NOC], "sat")
    wait("sync", "sout", 16 * NTB)

    with nc.Block() as block:
        @block.tensor
        def _(e):
            for f in steps["pe"]:
                f(e)

        @block.scalar
        def _(e):
            for f in steps["act"]:
                f(e)

        @block.vector
        def _(e):
            for f in steps["dve"]:
                f(e)

        @block.gpsimd
        def _(e):
            for f in steps["pool"]:
                f(e)

        @block.sync
        def _(e):
            for f in steps["sync"]:
                f(e)

    ctx.close()
    return nc


def _stage_inputs(cfg, x, W_q, W_k, W_v, W_o, w1, b1, w2, b2):
    D, LPC, HID = cfg["D"], cfg["LPC"], cfg["HID"]
    NH, NHC, NOC = cfg["NH"], cfg["NHC"], cfg["NOC"]
    f32 = np.float32
    perm = np.concatenate([
        np.concatenate([np.arange(h * 64, (h + 1) * 64, 2),
                        np.arange(h * 64 + 1, (h + 1) * 64, 2)])
        for h in range(NH)])
    com = {
        "wqT": np.ascontiguousarray(np.asarray(W_q, f32).T[:, perm]),
        "wkT": np.ascontiguousarray(np.asarray(W_k, f32).T[:, perm]),
        "wvT": np.ascontiguousarray(np.asarray(W_v, f32).T),
        "woT": np.ascontiguousarray(np.asarray(W_o, f32).T),
        "w1T": np.ascontiguousarray(np.asarray(w1, f32).T),
        "w1n": np.ascontiguousarray(np.asarray(w1, f32)),
        "w2T": np.ascontiguousarray(np.asarray(w2, f32).T),
        "w2n": np.ascontiguousarray(np.asarray(w2, f32)),
        "b1c": np.ascontiguousarray(np.asarray(b1, f32).reshape(NHC, 128).T),
        "b2c": np.ascontiguousarray(np.asarray(b2, f32).reshape(NOC, 128).T),
    }
    Pm = np.zeros((128, 128), f32)
    for r in range(128):
        Pm[r, r ^ 32] = 1.0
    com["Pm"] = Pm
    inv = (BASE ** (-(np.arange(32, dtype=np.float64)) / 32.0))
    xnp = np.asarray(x, f32)
    in_maps = []
    for c in range(8):
        b, q0 = c // 4, (c % 4) * LPC
        pos = (q0 + np.arange(LPC, dtype=np.float64))
        ang = np.outer(inv, pos)
        cc = np.cos(ang).astype(f32)
        ss = np.sin(ang).astype(f32)
        CC = np.tile(cc, (4, 1))
        SS = np.concatenate([-ss, ss, -ss, ss], axis=0)
        m = dict(com)
        m["CC"] = np.ascontiguousarray(CC)
        m["SS"] = np.ascontiguousarray(SS)
        m["xT"] = np.ascontiguousarray(xnp[b, q0:q0 + LPC, :].T)
        in_maps.append(m)
    return in_maps


def kernel(L0_attn_x, L0_attn_W_q, L0_attn_W_k, L0_attn_W_v, L0_attn_W_o,
           L0_ffn_w1, L0_ffn_b1, L0_ffn_w2, L0_ffn_b2):
    cfg = make_cfg()
    in_maps = _stage_inputs(cfg, L0_attn_x, L0_attn_W_q, L0_attn_W_k, L0_attn_W_v,
                            L0_attn_W_o, L0_ffn_w1, L0_ffn_b1, L0_ffn_w2, L0_ffn_b2)
    nc = build_program(cfg, detect_races=False)
    res = run_bass_kernel_spmd(nc, in_maps, core_ids=list(range(8)))
    LPC = cfg["LPC"]
    out = np.empty((2, 4 * LPC, cfg["D"]), np.float32)
    for c in range(8):
        b, q0 = c // 4, (c % 4) * LPC
        out[b, q0:q0 + LPC, :] = res.results[c]["yT"].T
    return out


# version 23 (final)
# speedup vs baseline: 1.0333x; 1.0040x over previous
import math
import numpy as np

import concourse.bass as bass
import concourse.mybir as mybir
from concourse.bass_utils import run_bass_kernel_spmd

F32 = mybir.dt.float32
BF16 = mybir.dt.bfloat16
AF = mybir.ActivationFunctionType
ALU = mybir.AluOpType

BASE = 10000.0


def make_cfg(D=1024, LPC=2048, HID=4096):
    assert D % 128 == 0 and LPC % 512 == 0 and HID % 1024 == 0
    return dict(
        D=D, LPC=LPC, HID=HID,
        NH=D // 64, NHP=D // 128, NFC=D // 128,
        NTB=LPC // 512, NKC=4 * LPC // 128,
        NHC=HID // 128, NOC=D // 128, L=4 * LPC,
    )


def build_program(cfg, detect_races=True, dbg=False, gelu_copy=False):
    D, LPC, HID = cfg["D"], cfg["LPC"], cfg["HID"]
    NFC, NHP, NTB, NKC = cfg["NFC"], cfg["NHP"], cfg["NTB"], cfg["NKC"]
    NHC, NOC = cfg["NHC"], cfg["NOC"]
    from contextlib import ExitStack

    nc = bass.Bass("TRN2", target_bir_lowering=False,
                   detect_race_conditions=detect_races)

    dt = nc.dram_tensor
    xT = dt("xT", [D, LPC], F32, kind="ExternalInput")
    wqT = dt("wqT", [D, D], F32, kind="ExternalInput")
    wkT = dt("wkT", [D, D], F32, kind="ExternalInput")
    wvT = dt("wvT", [D, D], F32, kind="ExternalInput")
    woT = dt("woT", [D, D], F32, kind="ExternalInput")
    w1T = dt("w1T", [D, HID], F32, kind="ExternalInput")
    w1n = dt("w1n", [HID, D], F32, kind="ExternalInput")
    w2T = dt("w2T", [HID, D], F32, kind="ExternalInput")
    w2n = dt("w2n", [D, HID], F32, kind="ExternalInput")
    b1c = dt("b1c", [128, NHC], F32, kind="ExternalInput")
    b2c = dt("b2c", [128, NOC], F32, kind="ExternalInput")
    CCd = dt("CC", [128, LPC], F32, kind="ExternalInput")
    SSd = dt("SS", [128, LPC], F32, kind="ExternalInput")
    Pmd = dt("Pm", [128, 128], F32, kind="ExternalInput")
    yT = dt("yT", [D, LPC], F32, kind="ExternalOutput")

    dk = dict(kind="ExternalOutput") if dbg else {}
    agk_i = dt("agk_i", [D, LPC], BF16)
    agk_o = dt("agk_o", [4 * D, LPC], BF16)
    agv_i = dt("agv_i", [LPC, D], BF16)
    agv_o = dt("agv_o", [4 * LPC, D], BF16)
    sgn1 = dt("sgn1", [D, HID], BF16, **dk)
    sgn2 = dt("sgn2", [HID, D], BF16, **dk)
    atu_d = dt("atu_d", [D, LPC], BF16, **dk)
    if dbg:
        dbgK = dt("dbgK", [D, LPC], BF16, kind="ExternalOutput")
        dbgV = dt("dbgV", [LPC, D], BF16, kind="ExternalOutput")
        dbgA = dt("dbgA", [D, LPC], F32, kind="ExternalOutput")
        dbgB = dt("dbgB", [D, LPC], F32, kind="ExternalOutput")
        dbgE = dt("dbgE", [D, LPC], F32, kind="ExternalOutput")
        dbgC = dt("dbgC", [D, LPC], BF16, kind="ExternalOutput")
        dbgG = dt("dbgG", [HID, LPC], BF16, kind="ExternalOutput")
        dbgW = dt("dbgW", [128, 2 * (NHC + NOC)], F32, kind="ExternalOutput")
        dbgP = dt("dbgP", [128, 512], BF16, kind="ExternalOutput")
        dbgW2 = dt("dbgW2", [128, 2 * (NHC + NOC)], F32, kind="ExternalOutput")

    ctx = ExitStack()

    sb = nc.alloc_sbuf_tensor
    qrope = sb("qrope", [128, NHP * LPC], BF16)
    lns = sb("lns", [128, NFC * 512], BF16)
    xq = sb("xq", [128, NFC * 512], F32)
    guard1 = sb("guard1", [128, 256], F32)
    a2f = sb("a2f", [128, NFC * 512], F32)
    guard2 = sb("guard2", [128, 256], F32)
    wbuf = [sb("wbuf0", [128, NFC * D], BF16), sb("wbuf1", [128, NFC * D], BF16)]
    pt = sb("pt", [128, 3072], BF16)
    atu_t = sb("atu_t", [128, 512], BF16)
    den_s = sb("den_s", [128, 512], F32)
    recin = sb("recin", [128, 512], F32)
    recb = sb("recb", [128, 512], F32)
    rows = sb("rows", [1, 2048], F32)
    onesD = sb("onesD", [128, 1], F32)
    onesb = sb("onesb", [128, 1], BF16)
    onesr = sb("onesr", [1, 64], F32)
    a1c = sb("a1c", [128, NHC + 2], F32)
    a2c = sb("a2c", [128, NOC + 2], F32)
    b1s = sb("b1s", [128, NHC + 2], F32)
    b2s = sb("b2s", [128, NOC + 2], F32)
    t1s = sb("t1s", [128, 512], F32)
    t2s = sb("t2s", [128, 512], F32)
    phb = sb("phb", [128, max(NHC * 256, NFC * 512)], BF16)
    guard3 = sb("guard3", [128, 256], BF16)
    s1b = [sb("s1b%d" % i, [128, NFC * 128], BF16) for i in range(4)]
    s2b = [sb("s2b%d" % i, [128, D], BF16) for i in range(4)]

    r1 = ExitStack()
    CCs = r1.enter_context(nc.sbuf_tensor("CCs", [128, LPC], F32))
    SSs = r1.enter_context(nc.sbuf_tensor("SSs", [128, LPC], F32))
    Pms = r1.enter_context(nc.sbuf_tensor("Pms", [128, 128], F32))
    wst = [r1.enter_context(nc.sbuf_tensor("wst0", [128, 1024], F32)),
           r1.enter_context(nc.sbuf_tensor("wst1", [128, 1024], F32))]
    qpre = [r1.enter_context(nc.sbuf_tensor("qpre%d" % i, [128, 512], F32))
            for i in range(3)]
    sgs = [r1.enter_context(nc.sbuf_tensor("sgs0", [128, 512], BF16)),
           r1.enter_context(nc.sbuf_tensor("sgs1", [128, 512], BF16))]
    krt = [r1.enter_context(nc.sbuf_tensor("krt0", [128, 512], BF16)),
           r1.enter_context(nc.sbuf_tensor("krt1", [128, 512], BF16))]
    vlc = [r1.enter_context(nc.sbuf_tensor("vlc0", [128, D], BF16)),
           r1.enter_context(nc.sbuf_tensor("vlc1", [128, D], BF16))]

    ps = nc.alloc_psum_tensor
    ps_s = ps("ps_s", [128, 3072], F32)
    ps_o = ps("ps_o", [128, 512], F32)
    ps_d = ps("ps_d", [128, 512], F32)
    ps_g1 = ps_s[:, 0:512]
    ps_g2 = ps_s[:, 512:1024]
    bcA = ps_s[:, 2048:2560]
    bcB = ps_s[:, 2560:3072]
    sw_slots = [bcA, bcB]

    ENGS = ("pe", "act", "dve", "pool", "sync")
    steps = {k: [] for k in ENGS}
    sems, ctr = {}, {}
    for name in ("sc", "sw0", "sw1", "sx", "sao", "skt", "svt",
                 "skt1", "skt2", "skt3", "svt1", "svt2", "svt3",
                 "s10", "s11", "s12", "s13", "s20", "s21", "s22", "s23",
                 "kr0", "kr1", "vl0", "vl1", "sg0", "sg1", "swc0", "swc1",
                 "scc", "pe", "act", "dve", "sat", "sout"):
        sems[name] = ctx.enter_context(nc.semaphore(name))
        ctr[name] = 0

    def wait(eng, sem, val):
        if val is None or val <= 0:
            return
        steps[eng].append(lambda e, s=sems[sem], v=val: e.wait_ge(s, v))

    def op(eng, fn, inc=None, n=1):
        if inc is None:
            steps[eng].append(fn)
            return None
        ctr[inc] += n
        steps[eng].append(lambda e, fn=fn, s=sems[inc], n=n: fn(e).then_inc(s, n))
        return ctr[inc]

    def mm(out, lhsT, rhs, start, stop, tp=None, inc=None, sgc=False):
        return op("pe", lambda e, o=out, l=lhsT, r=rhs, s=start, t=stop, p=tp,
                  g=sgc: e.matmul(o, l, r, start=s, stop=t, tile_position=p,
                                  skip_group_check=g), inc=inc)

    def dma(eng, dst, src, inc):
        return op(eng, lambda e, d=dst, s=src: e.dma_start(out=d, in_=s), inc=inc, n=16)

    dma("sync", CCs[:, :], CCd[:, :], "sc")
    dma("sync", SSs[:, :], SSd[:, :], "sc")
    dma("sync", Pms[:, :], Pmd[:, :], "sc")
    dma("sync", b1s[:, 0:NHC], b1c[:, :], "sc")
    v_consts = dma("sync", b2s[:, 0:NOC], b2c[:, :], "sc")
    for _eng in ("pe", "act", "dve"):
        wait(_eng, "sc", v_consts)

    op("dve", lambda e: e.memset(a1c[:, :], 0.0))
    op("dve", lambda e: e.memset(a2c[:, :], 0.0))
    op("dve", lambda e: e.memset(onesD[:, :], 1.0 / D))
    op("dve", lambda e: e.memset(onesb[:, :], 1.0))
    op("dve", lambda e: e.memset(onesr[:, :], 1.0))
    op("dve", lambda e: e.memset(ps_d[:, :], 0.0))
    op("dve", lambda e: e.memset(ps_s[:, :], 0.0))
    v_init = op("dve", lambda e: e.memset(ps_o[:, :], 0.0), inc="dve")

    def layer_norm_stats(src, bc1, bc2, pre_pe=None, pre_dve=None, pre_act=None,
                         pre_sd=None):
        if pre_sd:
            wait("pe", pre_sd[0], pre_sd[1])
        wait("pe", "pe", pre_pe)
        wait("pe", "act", pre_act)
        wait("pe", "dve", pre_dve)
        for fc in range(NFC):
            mm(ps_d[0:1, :], onesD[:, :], src[:, fc * 512:(fc + 1) * 512],
               fc == 0, fc == NFC - 1, tp=(0, 0), sgc=True)
        msqv = {}
        if pre_sd:
            wait("dve", pre_sd[0], pre_sd[1])
        wait("dve", "act", pre_act)
        wait("dve", "dve", pre_dve)
        for fc in range(NFC):
            tscr = t1s if fc % 2 == 0 else t2s
            if fc >= 2:
                wait("dve", "pe", msqv[fc - 2])
            sqv = op("dve", lambda e, t=tscr, s=src, f=fc:
                     e.tensor_tensor(t[:, :], s[:, f * 512:(f + 1) * 512],
                                     s[:, f * 512:(f + 1) * 512], ALU.mult),
                     inc="dve")
            wait("pe", "dve", sqv)
            msqv[fc] = mm(ps_d[64:65, :], onesD[:, :], tscr[:, :],
                          fc == 0, fc == NFC - 1, tp=(0, 64), inc="pe", sgc=True)
        return ln_tail(bc1, bc2, msqv[NFC - 1])

    def ln_tail(bc1, bc2, last_msq):
        wait("dve", "pe", last_msq)
        op("dve", lambda e: e.tensor_copy(rows[0:1, 0:512], ps_d[0:1, :]))
        op("dve", lambda e: e.tensor_copy(rows[0:1, 1024:1536], ps_d[64:65, :]))
        op("dve", lambda e: e.tensor_tensor(rows[0:1, 1536:2048], rows[0:1, 0:512],
                                            rows[0:1, 0:512], ALU.mult))
        op("dve", lambda e: e.tensor_tensor(
            rows[0:1, 1024:1536], rows[0:1, 1024:1536], rows[0:1, 1536:2048],
            ALU.subtract))
        dv = op("dve", lambda e: e.tensor_scalar_add(
            rows[0:1, 1024:1536], rows[0:1, 1024:1536], 1e-5), inc="dve")
        wait("act", "dve", dv)
        op("act", lambda e: e.activation(rows[0:1, 1024:1536], rows[0:1, 1024:1536],
                                         AF.Ln))
        av = op("act", lambda e: e.activation(rows[0:1, 512:1024],
                                              rows[0:1, 1024:1536],
                                              AF.Exp, scale=-0.5), inc="act")
        wait("pe", "act", av)
        mm(bc1[0:64, :], onesr[:, :], rows[0:1, 0:512], True, True, tp=(0, 0))
        mm(bc1[64:128, :], onesr[:, :], rows[0:1, 0:512], True, True, tp=(0, 64))
        mm(bc2[0:64, :], onesr[:, :], rows[0:1, 512:1024], True, True, tp=(0, 0))
        return mm(bc2[64:128, :], onesr[:, :], rows[0:1, 512:1024], True, True,
                  tp=(0, 64), inc="pe")

    wst_war = [None, None]
    wbuf_war = [None, None]

    def load_weight(wdram, par):
        wait("pool", "pe", wbuf_war[par])
        return dma("pool",
                   wbuf[par][:, :].rearrange("p (c n) -> p c n", n=D),
                   wdram[:, :].rearrange("(c p) n -> p c n", p=128),
                   "swc%d" % par)

    widx = 0
    kv_stores = 0
    evac_war3 = [None, None, None]
    evac_war4 = [None, None, None, None]
    slot_t2 = [None, None]
    qp_rd_pe = [None, None, None]
    qp_rd_dve = [None, None, None]
    kst = [None, None]
    vst = [None, None]
    acc_war = None

    prev_bcv = None
    for tb in range(NTB):
        if tb > 0:
            wait("sync", "pe", prev_bcv)
            wait("sync", "dve", lns_ready)
        x_sd = dma("sync", xq[:, :].rearrange("p (c t) -> p c t", t=512),
                   xT[:, tb * 512:(tb + 1) * 512]
                   .rearrange("(c p) t -> p c t", p=128), "sx")
        bcv = layer_norm_stats(xq, bcA, bcB, pre_sd=("sx", x_sd),
                               pre_dve=max(v_init if tb == 0 else 0,
                                           acc_war or 0) or None)
        lastap = None
        wait("dve", "pe", bcv)
        for fc in range(NFC):
            op("dve", lambda e, f=fc: e.tensor_tensor(
                t1s[:, :], xq[:, f * 512:(f + 1) * 512], bcA, ALU.subtract))
            lastap = op("dve", lambda e, f=fc: e.tensor_tensor(
                lns[:, f * 512:(f + 1) * 512], t1s[:, :], bcB, ALU.mult),
                inc="dve")
        lns_ready = lastap
        prev_bcv = bcv

        for (wdram, is_q) in ((wqT, True), (wkT, False)):
            par = widx % 2
            widx += 1
            castv = load_weight(wdram, par)
            wbuf_war[par] = None
            prjs = {}

            def emit_swap(oc, tb=tb):
                qp = qpre[oc % 3]
                slot = sw_slots[oc % 2]
                wait("pe", "act", evac_war3[oc % 3])
                wait("pe", "dve", slot_t2[oc % 2])
                swv = mm(slot, Pms[:, :], qp[:, :], True, True, inc="pe")
                qp_rd_pe[oc % 3] = swv
                wait("dve", "act", evac_war3[oc % 3])
                dv1 = op("dve", lambda e, q=qp, t=tb: e.tensor_tensor(
                    t1s[:, :], q[:, :], CCs[:, t * 512:(t + 1) * 512], ALU.mult),
                    inc="dve")
                qp_rd_dve[oc % 3] = dv1
                wait("dve", "pe", swv)
                t2v = op("dve", lambda e, s=slot, t=tb: e.tensor_tensor(
                    t2s[:, :], s, SSs[:, t * 512:(t + 1) * 512], ALU.mult),
                    inc="dve")
                slot_t2[oc % 2] = t2v
                if is_q:
                    dst = qrope[:, oc * LPC + tb * 512: oc * LPC + (tb + 1) * 512]
                    op("dve", lambda e, d=dst: e.tensor_tensor(
                        d, t1s[:, :], t2s[:, :], ALU.add))
                else:
                    kb = krt[oc % 2]
                    wait("dve", "kr%d" % (oc % 2), kst[oc % 2])
                    rv = op("dve", lambda e, d=kb: e.tensor_tensor(
                        d[:, :], t1s[:, :], t2s[:, :], ALU.add), inc="dve")
                    wait("pool", "dve", rv)
                    kst[oc % 2] = dma(
                        "pool",
                        agk_i[oc * 128:(oc + 1) * 128, tb * 512:(tb + 1) * 512],
                        kb[:, :], "kr%d" % (oc % 2))

            for oc in range(NHP):
                acc = ps_s[:, (oc % 4) * 512:(oc % 4) * 512 + 512]
                if oc == 0:
                    wait("pe", "swc%d" % par, castv)
                    wait("pe", "dve", lns_ready)
                wait("pe", "act", evac_war4[oc % 4])
                prj = None
                for fc in range(NFC):
                    prj = mm(acc,
                             wbuf[par][:, fc * D + oc * 128: fc * D + (oc + 1) * 128],
                             lns[:, fc * 512:(fc + 1) * 512],
                             fc == 0, fc == NFC - 1,
                             inc="pe" if fc == NFC - 1 else None)
                prjs[oc] = prj
                wbuf_war[par] = prj
                qp = qpre[oc % 3]
                wait("act", "pe", prj)
                wait("act", "pe", qp_rd_pe[oc % 3])
                wait("act", "dve", qp_rd_dve[oc % 3])
                ev = op("act", lambda e, q=qp, a=acc: e.copy(q[:, :], a[:, :]),
                        inc="act")
                evac_war3[oc % 3] = ev
                evac_war4[oc % 4] = ev
                if oc >= 1:
                    emit_swap(oc - 1)
            emit_swap(NHP - 1)

        par = widx % 2
        widx += 1
        castv = load_weight(wvT, par)
        wbuf_war[par] = None
        for tc in range(4):
            vb = vlc[tc % 2]
            prj = None
            if tc == 0:
                wait("pe", "swc%d" % par, castv)
                wait("pe", "dve", lns_ready)
            wait("pe", "dve", acc_war)
            vw = min(512, D)
            if tc == 0:
                wait("pe", "act", max(v for v in evac_war4 if v) if any(evac_war4) else None)
            for oh in range(max(1, D // 512)):
                accv = ps_g1 if oh % 2 == 0 else ps_g2
                for fc in range(NFC):
                    prj = mm(accv[:, 0:vw],
                             lns[:, fc * 512 + tc * 128: fc * 512 + (tc + 1) * 128],
                             wbuf[par][:, fc * D + oh * vw: fc * D + (oh + 1) * vw],
                             fc == 0, fc == NFC - 1,
                             inc="pe" if fc == NFC - 1 else None)
            wbuf_war[par] = prj
            wait("dve", "pe", prj)
            wait("dve", "vl%d" % (tc % 2), vst[tc % 2])
            for oh in range(max(1, D // 512)):
                accv = ps_g1 if oh % 2 == 0 else ps_g2
                vv = op("dve", lambda e, v=vb, a=accv, o=oh, vw=vw: e.tensor_copy(
                    v[:, o * vw:(o + 1) * vw], a[:, 0:vw]), inc="dve")
            acc_war = vv
            wait("pool", "dve", vv)
            vst[tc % 2] = dma(
                "pool", agv_i[(tb * 4 + tc) * 128:(tb * 4 + tc + 1) * 128, :],
                vb[:, :], "vl%d" % (tc % 2))

    wo_par = widx % 2
    widx += 1
    wo_cast = load_weight(woT, wo_par)
    wbuf_war[wo_par] = None

    r1_marks = dict(ctr)
    sgstage = [xq[:, 0:1024], xq[:, 1024:2048]]
    sgout = [s2b[2][:, 0:512], s2b[3][:, 0:512]]
    wait("sync", "pe", ctr["pe"])
    wait("sync", "dve", ctr["dve"])
    sg_war = [None, None]
    i = 0
    sg_hist = {}
    for (src_d, dst_d, RP, CT) in ((w1T, sgn1, NFC, HID), (w2T, sgn2, NHC, D)):
        CJ, CW = max(1, CT // 512), min(512, CT)
        for rp in range(RP):
            for j in range(CJ):
                k = i % 2
                wait("sync", "act", sg_hist.get(i - 2))
                ld = dma("sync", sgstage[k][:, 0:CW],
                         src_d[rp * 128:(rp + 1) * 128, j * CW:(j + 1) * CW],
                         "sw%d" % k)
                wait("act", "sw%d" % k, ld)
                wait("act", "sg%d" % k, sg_war[k])
                sv = op("act", lambda e, k=k, CW=CW: e.sign(sgout[k][:, 0:CW],
                                                            sgstage[k][:, 0:CW]),
                        inc="act")
                sg_hist[i] = sv
                wait("pool", "act", sv)
                sg_war[k] = dma("pool",
                                dst_d[rp * 128:(rp + 1) * 128, j * CW:(j + 1) * CW],
                                sgout[k][:, 0:CW], "sg%d" % k)
                i += 1
    wait("sync", "act", ctr["act"])
    astage = [xq[:, 2048:3072], xq[:, 3072:4096]]
    ab_hist = {}
    ai = 0
    prev_abs = None
    for (src_d, col_t, NCH, DEN) in ((w1n, a1c, NHC, D), (w2n, a2c, NOC, HID)):
        pieces = max(1, DEN // 1024)
        pw = min(DEN, 1024)
        for ch in range(NCH):
            for pz in range(pieces):
                k = ai % 2
                wait("sync", "act", ab_hist.get(ai - 2))
                ldv = dma("sync", astage[k][:, 0:pw],
                          src_d[ch * 128:(ch + 1) * 128, pz * pw:(pz + 1) * pw],
                          "sw%d" % k)
                wait("act", "sw%d" % k, ldv)
                dst_col = col_t[:, NCH:NCH + 1] if pieces > 1 else col_t[:, ch:ch + 1]
                prev_abs = op("act", lambda e, pw=pw, dc=dst_col, k=k: e.activation(
                    s1b[k][:, 0:pw], astage[k][:, 0:pw], AF.Abs, accum_out=dc),
                    inc="act")
                ab_hist[ai] = prev_abs
                ai += 1
                if pieces > 1:
                    wait("dve", "act", prev_abs)
                    if pz == 0:
                        op("dve", lambda e, ct=col_t, c=ch, n=NCH: e.tensor_copy(
                            ct[:, c:c + 1], ct[:, n:n + 1]))
                    else:
                        op("dve", lambda e, ct=col_t, c=ch, n=NCH: e.tensor_tensor(
                            ct[:, c:c + 1], ct[:, c:c + 1], ct[:, n:n + 1], ALU.add))
                    prev_dve = op("dve", lambda e, ct=col_t, c=ch: e.tensor_copy(
                        t2s[:, 0:1], ct[:, c:c + 1]), inc="dve")
                    wait("act", "dve", prev_dve)
        wait("dve", "act", prev_abs)
        op("dve", lambda e, ct=col_t, n=NCH, d=DEN: e.tensor_scalar_mul(
            ct[:, 0:n], ct[:, 0:n], 1.0 / d))
    alpha_act_mark = ctr["act"]
    alpha_done = op("dve", lambda e: e.tensor_copy(t1s[:, 0:1], a1c[:, 0:1]),
                    inc="dve")

    if dbg:
        for _s in ("kr0", "kr1", "vl0", "vl1"):
            wait("pool", _s, ctr[_s])
        dma("pool", dbgK[:, :], agk_i[:, :], "sg0")
        dma("pool", dbgV[:, :], agv_i[:, :], "sg0")
    rgs = [[0, 1, 2, 3], [4, 5, 6, 7]]
    for _s in ("kr0", "kr1"):
        wait("pool", _s, ctr[_s])
    op("pool", lambda e: e.collective_compute(
        "AllGather", ALU.bypass, replica_groups=rgs,
        ins=[agk_i.ap().opt()], outs=[agk_o.ap().opt()]), inc="scc", n=1)
    for _s in ("vl0", "vl1"):
        wait("pool", _s, ctr[_s])
    op("pool", lambda e: e.collective_compute(
        "AllGather", ALU.bypass, replica_groups=rgs,
        ins=[agv_i.ap().opt()], outs=[agv_o.ap().opt()]), inc="scc", n=1)

    if dbg:
        wait("sync", "dve", alpha_done)
        dma("sync", dbgW[:, 0:NHC], a1c[:, 0:NHC], "sat")
        dma("sync", dbgW[:, NHC:2 * NHC], b1s[:, 0:NHC], "sat")
        dma("sync", dbgW[:, 2 * NHC:2 * NHC + NOC], a2c[:, 0:NOC], "sat")
        dma("sync", dbgW[:, 2 * NHC + NOC:], b2s[:, 0:NOC], "sat")
    r1.close()
    ktb = sb("ktb", [128, 4 * LPC], BF16)
    vtb = sb("vtb", [128, 4 * LPC], BF16)

    exp_free = {0: None, 1: None, 2: None}
    pt_free = {0: None, 1: None, 2: None}
    attn_last_pe = None
    sat_last = None
    po_war = None
    pd_war = None

    RCH = NKC // 4
    ktsem = ["skt", "skt1", "skt2", "skt3"]
    vtsem = ["svt", "svt1", "svt2", "svt3"]
    kt_rv = [None] * 4
    vt_rv = [None] * 4

    def load_kv(hp, gate_scv=None, gate_pvs=None):
        for r in range(4):
            if gate_scv is not None:
                wait("sync", "pe", gate_scv[(r + 1) * RCH - 1])
            kt_rv[r] = dma("sync", ktb[:, r * LPC:(r + 1) * LPC],
                           agk_o[r * D + hp * 128: r * D + (hp + 1) * 128, :],
                           ktsem[r])
            if gate_pvs is not None:
                wait("sync", "pe", gate_pvs[(r + 1) * RCH - 1])
            for q4 in range(LPC // 512):
                base = (r * (LPC // 128) + q4 * 4) * 128
                vt_rv[r] = dma(
                    "sync",
                    vtb[:, base:base + 512].rearrange("p (c d) -> p c d", d=128),
                    agv_o[r * LPC + q4 * 512: r * LPC + (q4 + 1) * 512,
                          hp * 128:(hp + 1) * 128]
                    .rearrange("(c p) d -> p c d", p=128), vtsem[r])

    for hp in range(NHP):
        if hp == 0:
            wait("sync", "scc", 2)
            for _s in ("kr0", "kr1", "vl0", "vl1"):
                wait("sync", _s, r1_marks[_s])
            wait("sync", "dve", r1_marks["dve"])
            wait("sync", "act", r1_marks["act"])
            wait("sync", "pe", r1_marks["pe"])
            load_kv(0)

        for qb in range(NTB):
            qsl = slice(hp * LPC + qb * 512, hp * LPC + (qb + 1) * 512)
            denv = None
            scv = {}
            pvs = {}
            LKA = 2
            for ci in range(NKC + LKA):
                if ci < NKC:
                    c = ci
                    w = c % 3
                    wait("pe", "act", exp_free[w])
                    if qb == 0 and c % RCH == 0:
                        wait("pe", ktsem[c // RCH], kt_rv[c // RCH])
                    mm(ps_s[:, w * 1024: w * 1024 + 512],
                       ktb[0:64, c * 128:(c + 1) * 128], qrope[0:64, qsl],
                       True, True, tp=(0, 0))
                    scv[c] = mm(ps_s[:, w * 1024 + 512: w * 1024 + 1024],
                                ktb[64:128, c * 128:(c + 1) * 128],
                                qrope[64:128, qsl],
                                True, True, tp=(64, 0), inc="pe")
                if ci >= LKA:
                    c = ci - LKA
                    w = c % 3
                    wait("act", "pe", scv[c])
                    wait("act", "pe", pt_free[w])
                    expv = op("act", lambda e, w=w: e.activation(
                        pt[:, w * 1024:(w + 1) * 1024],
                        ps_s[:, w * 1024:(w + 1) * 1024],
                        AF.Exp, scale=0.125), inc="act")
                    exp_free[w] = expv
                    wait("pe", "act", expv)
                    if c == 0:
                        wait("pe", "dve", po_war)
                    if qb == 0 and c % RCH == 0:
                        wait("pe", vtsem[c // RCH], vt_rv[c // RCH])
                    mm(ps_o[0:64, :], vtb[:, c * 128: c * 128 + 64],
                       pt[:, w * 1024: w * 1024 + 512],
                       c == 0, c == NKC - 1, tp=(0, 0), sgc=True)
                    pvv = mm(ps_o[64:128, :], vtb[:, c * 128 + 64: c * 128 + 128],
                             pt[:, w * 1024 + 512: w * 1024 + 1024],
                             c == 0, c == NKC - 1, tp=(0, 64), inc="pe", sgc=True)
                    pt_free[w] = pvv
                    pvs[c] = pvv
                    if c % 2 == 1:
                        g = c // 2
                        wp = (c - 1) % 3
                        if g == 0:
                            wait("pe", "dve", pd_war)
                        st = (g == 0)
                        sp = (g == NKC // 2 - 1)
                        mm(ps_d[0:1, :], onesb[:, :],
                           pt[:, wp * 1024: wp * 1024 + 512], st, sp, tp=(0, 0),
                           sgc=True)
                        mm(ps_d[32:33, :], onesb[:, :],
                           pt[:, wp * 1024 + 512: wp * 1024 + 1024], st, sp,
                           tp=(0, 32), sgc=True)
                        mm(ps_d[64:65, :], onesb[:, :],
                           pt[:, w * 1024: w * 1024 + 512], st, sp,
                           tp=(0, 64), sgc=True)
                        denv = mm(ps_d[96:97, :], onesb[:, :],
                                  pt[:, w * 1024 + 512: w * 1024 + 1024], st, sp,
                                  tp=(0, 96), inc="pe", sgc=True)
                        pt_free[wp] = denv
                        pt_free[w] = denv
            attn_last_pe = ctr["pe"]
            if qb == NTB - 1 and hp + 1 < NHP:
                load_kv(hp + 1, gate_scv=scv, gate_pvs=pvs)
            wait("dve", "pe", denv)
            pd_war = op("dve", lambda e: e.tensor_copy(den_s[:, :], ps_d[:, :]),
                        inc="dve")
            op("dve", lambda e: e.tensor_copy(t1s[0:64, :], den_s[64:128, :]))
            op("dve", lambda e: e.tensor_tensor(
                recin[0:64, :], den_s[0:64, :], t1s[0:64, :], ALU.add))
            dnr = op("dve", lambda e: e.tensor_copy(recb[0:1, :], recin[32:33, :]),
                     inc="dve")
            wait("pe", "dve", dnr)
            mm(ps_d[0:64, :], onesr[:, :], recin[0:1, :], True, True, tp=(0, 0))
            bcv = mm(ps_d[64:128, :], onesr[:, :], recb[0:1, :], True, True,
                     tp=(0, 64), inc="pe")
            wait("dve", "pe", bcv)
            pd_war = op("dve", lambda e: e.tensor_copy(recin[:, :], ps_d[:, :]),
                        inc="dve")
            op("dve", lambda e: e.reciprocal(recb[:, :], recin[:, :]))
            wait("dve", "sat", sat_last)
            po_war = op("dve", lambda e: e.tensor_tensor(
                atu_t[:, :], ps_o[:, :], recb[:, :], ALU.mult), inc="dve")
            wait("sync", "dve", po_war)
            sat_last = dma("sync",
                           atu_d[hp * 128:(hp + 1) * 128, qb * 512:(qb + 1) * 512],
                           atu_t[:, :], "sat")

    awo_b = [vtb[:, 0:NFC * 512], vtb[:, NFC * 512:2 * NFC * 512]]
    xq_b = [xq[:, :], ktb[:, :].bitcast(F32)]
    aw_sds = [None, None]
    x_sds = [None, None]
    s1_war = [None, None, None, None]
    s2_war = [None, None, None, None]
    ffn2_last = None
    y_dve = None
    prev_fev = None
    wo_prj_last = None
    y_fcs_prev = None

    aw_sem = ["sao", "svt"]
    x_sem = ["sx", "skt"]

    def load_qb(j):
        aw_sds[j % 2] = dma("sync", awo_b[j % 2].rearrange("p (c t) -> p c t",
                                                           t=512),
                            atu_d[:, j * 512:(j + 1) * 512]
                            .rearrange("(c p) t -> p c t", p=128), aw_sem[j % 2])
        x_sds[j % 2] = dma("sync", xq_b[j % 2].rearrange("p (c t) -> p c t",
                                                         t=512),
                           xT[:, j * 512:(j + 1) * 512]
                           .rearrange("(c p) t -> p c t", p=128), x_sem[j % 2])

    for qb in range(NTB):
        awo = awo_b[qb % 2]
        xqs = xq_b[qb % 2]
        aw_sd = aw_sds[qb % 2]
        x_sd = x_sds[qb % 2]
        if qb == 0:
            wait("sync", "sat", ctr["sat"])
            load_qb(0)
            aw_sd, x_sd = aw_sds[0], x_sds[0]
            if NTB > 1:
                load_qb(1)
        elif qb + 1 < NTB:
            wait("sync", "pe", wo_prj_last)
            wait("sync", "sout", 16 * qb)
            load_qb(qb + 1)
        if dbg:
            wait("act", "sat", ctr["sat"])
            wait("dve", "sat", ctr["sat"])
        a2vs = []
        prj = None
        for oc in range(NOC):
            acc = ps_s[:, (oc % 2) * 512:(oc % 2) * 512 + 512]
            if oc == 0:
                wait("pe", aw_sem[qb % 2], aw_sd)
                wait("pe", "swc%d" % wo_par, wo_cast)
                wait("pe", "dve", max(po_war or 0, prev_fev or 0) or None)
                if qb == 0:
                    wait("pe", "act", ctr["act"])
            if oc >= 2:
                wait("pe", "act", a2vs[oc - 2])
            for fc in range(NFC):
                prj = mm(acc[:, :],
                         wbuf[wo_par][:, fc * D + oc * 128: fc * D + (oc + 1) * 128],
                         awo[:, fc * 512:(fc + 1) * 512], fc == 0, fc == NFC - 1,
                         inc="pe" if fc == NFC - 1 else None)
            wo_prj_last = prj
            wait("act", "pe", prj)
            if y_fcs_prev is not None:
                wait("act", "dve", y_fcs_prev[oc])
            a2vs.append(op("act", lambda e, o=oc, a=acc: e.copy(
                a2f[:, o * 512:(o + 1) * 512], a[:, :]), inc="act"))
        a2v = a2vs[-1]
        if dbg:
            wait("sync", "act", a2v)
            dma("sync", dbgA[:, qb * 512:(qb + 1) * 512]
                .rearrange("(c p) t -> p c t", p=128),
                a2f[:, :].rearrange("p (c t) -> p c t", t=512), "sat")
        bc2m = ps_s[:, 0:512]
        bc2r = ps_s[:, 512:1024]
        msqv2 = {}
        wait("pe", "dve", prev_fev)
        for fc in range(NFC):
            wait("pe", "act", a2vs[fc])
            mm(ps_d[0:1, :], onesD[:, :], a2f[:, fc * 512:(fc + 1) * 512],
               fc == 0, fc == NFC - 1, tp=(0, 0), sgc=True)
            tscr = t1s if fc % 2 == 0 else t2s
            if fc >= 2:
                wait("dve", "pe", msqv2[fc - 2])
            wait("dve", "act", a2vs[fc])
            sqv = op("dve", lambda e, t=tscr, f=fc: e.tensor_tensor(
                t[:, :], a2f[:, f * 512:(f + 1) * 512],
                a2f[:, f * 512:(f + 1) * 512], ALU.mult), inc="dve")
            wait("pe", "dve", sqv)
            msqv2[fc] = mm(ps_d[64:65, :], onesD[:, :], tscr[:, :],
                           fc == 0, fc == NFC - 1, tp=(0, 64), inc="pe", sgc=True)
        bcv2 = ln_tail(bc2m, bc2r, msqv2[NFC - 1])
        wait("dve", "pe", bcv2)
        wait("dve", x_sem[qb % 2], x_sd)
        bc3m = ps_s[:, 1024:1536]
        bc3r = ps_s[:, 1536:2048]
        rv = None
        msqv3 = {}
        for fc in range(NFC):
            op("dve", lambda e, f=fc, b=bc2m: e.tensor_tensor(
                den_s[:, :], a2f[:, f * 512:(f + 1) * 512], b, ALU.subtract))
            op("dve", lambda e, b=bc2r: e.tensor_tensor(den_s[:, :], den_s[:, :],
                                                        b, ALU.mult))
            rv = op("dve", lambda e, f=fc, x=xqs: e.tensor_tensor(
                x[:, f * 512:(f + 1) * 512], x[:, f * 512:(f + 1) * 512],
                den_s[:, :], ALU.add), inc="dve")
            wait("pe", "dve", rv)
            mm(ps_d[0:1, :], onesD[:, :], xqs[:, fc * 512:(fc + 1) * 512],
               fc == 0, fc == NFC - 1, tp=(0, 0), sgc=True)
            tscr = t1s if fc % 2 == 0 else t2s
            if fc >= 2:
                wait("dve", "pe", msqv3[fc - 2])
            sqv = op("dve", lambda e, t=tscr, f=fc, x=xqs: e.tensor_tensor(
                t[:, :], x[:, f * 512:(f + 1) * 512],
                x[:, f * 512:(f + 1) * 512], ALU.mult), inc="dve")
            wait("pe", "dve", sqv)
            msqv3[fc] = mm(ps_d[64:65, :], onesD[:, :], tscr[:, :],
                           fc == 0, fc == NFC - 1, tp=(0, 64), inc="pe", sgc=True)
        if dbg:
            wait("sync", "dve", rv)
            dma("sync", dbgB[:, qb * 512:(qb + 1) * 512]
                .rearrange("(c p) t -> p c t", p=128),
                xqs.rearrange("p (c t) -> p c t", t=512), "sat")
        bcv3 = ln_tail(bc3m, bc3r, msqv3[NFC - 1])
        wait("dve", "pe", bcv3)
        ln3v = None
        lnv = {}
        for fc in range(NFC):
            op("dve", lambda e, f=fc, b=bc3m, x=xqs: e.tensor_tensor(
                t1s[:, :], x[:, f * 512:(f + 1) * 512], b, ALU.subtract))
            ln3v = op("dve", lambda e, f=fc, b=bc3r: e.tensor_tensor(
                lns[:, f * 512:(f + 1) * 512], t1s[:, :], b, ALU.mult),
                inc="dve")
            lnv[fc] = ln3v
        if dbg:
            wait("sync", "dve", ln3v)
            dma("sync", dbgC[:, qb * 512:(qb + 1) * 512]
                .rearrange("(c p) t -> p c t", p=128),
                lns[:, :].rearrange("p (c t) -> p c t", t=512), "sat")
        fev = None
        msqv4 = {}
        for half in range(2):
            if dbg:
                wait("act", "sat", ctr["sat"])
            gel = {}
            for idx, hc in enumerate(range(NHC)):
                hci = hc
                k = idx % 4
                wait("sync", "pe", s1_war[k])
                s1_sd = dma("sync",
                            s1b[k][:, :].rearrange("p (c h) -> p c h", h=128),
                            sgn1[:, hc * 128:(hc + 1) * 128]
                            .rearrange("(c p) h -> p c h", p=128), "s1%d" % k)
                acc = ps_s[:, (idx % 4) * 512:(idx % 4) * 512 + 512]
                wait("pe", "s1%d" % k, s1_sd)
                if idx < 4:
                    wait("pe", "dve", max(alpha_done, fev or 0))
                    if half == 0:
                        wait("pe", "act", a2v)
                wait("pe", "act", gel.get(idx - 4))
                prj = None
                for fc in range(NFC):
                    if idx == 0 and half == 0:
                        wait("pe", "dve", lnv[fc])
                    prj = mm(acc[:, 0:256],
                             s1b[k][:, fc * 128:(fc + 1) * 128],
                             lns[:, fc * 512 + half * 256: fc * 512 + half * 256 + 256],
                             fc == 0, fc == NFC - 1,
                             inc="pe" if fc == NFC - 1 else None)
                s1_war[k] = prj
                wait("act", "pe", prj)
                if dbg and qb == 0 and half == 0 and hc in (29, 31):
                    pcp = op("act", lambda e, a=acc, h=hc: e.copy(
                        atu_t[:, (0 if h == 31 else 256):(256 if h == 31 else 512)],
                        a[:, 0:256]), inc="act")
                    if hc == 29 or hc == 31:
                        dbg_pcp = pcp
                dst = phb[:, hc * 256:(hc + 1) * 256]
                gfn = AF.Identity if gelu_copy else AF.Gelu
                gel[hci] = op("act", lambda e, h=hc, a=acc, d=dst, gfn=gfn:
                    e.activation(d, a[:, 0:256], gfn,
                                 bias=b1s[:, h:h + 1], scale=a1c[:, h:h + 1]),
                    inc="act")
            if dbg and qb == 0 and half == 0:
                wait("sync", "act", dbg_pcp)
                dma("sync", dbgP[:, :], atu_t[:, :], "sat")
            if dbg:
                wait("sync", "act", max(gel.values()))
                dma("sync", dbgG[:, qb * 512 + half * 256: qb * 512 + half * 256 + 256]
                    .rearrange("(c p) t -> p c t", p=128),
                    phb[:, 0:NHC * 256].rearrange("p (c t) -> p c t", t=256), "sat")
            f2_tiles = [ps_d[:, 0:256], ps_s[:, 0:256], ps_s[:, 512:768],
                        ps_s[:, 1024:1280], ps_s[:, 1536:1792],
                        ps_s[:, 2048:2304], ps_s[:, 2560:2816], ps_o[:, 0:256]]
            f2 = None
            for hc in range(NHC):
                k = hc % 4
                wait("sync", "pe", s2_war[k])
                s2_sd = dma("sync", s2b[k][:, :],
                            sgn2[hc * 128:(hc + 1) * 128, :], "s2%d" % k)
                wait("pe", "s2%d" % k, s2_sd)
                wait("pe", "act", gel[hc])
                if hc == 0:
                    wait("pe", "act", max(gel.values()))
                    wait("pe", "dve", fev)
                for oc in range(NOC):
                    f2 = mm(f2_tiles[oc],
                            s2b[k][:, oc * 128:(oc + 1) * 128],
                            phb[:, hc * 256:(hc + 1) * 256],
                            hc == 0, hc == NHC - 1, sgc=True,
                            inc="pe" if oc == NOC - 1 else None)
                s2_war[k] = f2
            ffn2_last = f2
            wait("dve", "pe", f2)
            if dbg:
                wait("dve", "sat", ctr["sat"])
            for oc in range(NOC):
                fev = op("dve", lambda e, o=oc, h=half: e.tensor_scalar(
                    a2f[:, o * 512 + h * 256: o * 512 + h * 256 + 256],
                    f2_tiles[o],
                    a2c[:, o:o + 1], b2s[:, o:o + 1], ALU.mult, ALU.add),
                    inc="dve")
                if half == 1:
                    wait("pe", "dve", fev)
                    mm(ps_d[0:1, :], onesD[:, :],
                       a2f[:, oc * 512:(oc + 1) * 512],
                       oc == 0, oc == NOC - 1, tp=(0, 0), sgc=True)
                    tscr = t1s if oc % 2 == 0 else t2s
                    if oc >= 2:
                        wait("dve", "pe", msqv4[oc - 2])
                    sqv = op("dve", lambda e, t=tscr, o=oc: e.tensor_tensor(
                        t[:, :], a2f[:, o * 512:(o + 1) * 512],
                        a2f[:, o * 512:(o + 1) * 512], ALU.mult), inc="dve")
                    wait("pe", "dve", sqv)
                    msqv4[oc] = mm(ps_d[64:65, :], onesD[:, :], tscr[:, :],
                                   oc == 0, oc == NOC - 1, tp=(0, 64),
                                   inc="pe", sgc=True)
        prev_fev = fev
        if dbg:
            wait("sync", "dve", fev)
            dma("sync", dbgE[:, qb * 512:(qb + 1) * 512]
                .rearrange("(c p) t -> p c t", p=128),
                a2f[:, :].rearrange("p (c t) -> p c t", t=512), "sat")
        if dbg:
            wait("dve", "sat", ctr["sat"])
        bc4m = ps_s[:, 1024:1536]
        bc4r = ps_s[:, 1536:2048]
        bcv4 = ln_tail(bc4m, bc4r, msqv4[NOC - 1])
        wait("dve", "pe", bcv4)
        y_fcs = []
        for fc in range(NFC):
            op("dve", lambda e, f=fc, b=bc4m: e.tensor_tensor(
                t1s[:, :], a2f[:, f * 512:(f + 1) * 512], b, ALU.subtract))
            op("dve", lambda e, b=bc4r: e.tensor_tensor(t2s[:, :], t1s[:, :], b,
                                                        ALU.mult))
            y_dve = op("dve", lambda e, f=fc, x=xqs: e.tensor_tensor(
                x[:, f * 512:(f + 1) * 512], x[:, f * 512:(f + 1) * 512],
                t2s[:, :], ALU.add), inc="dve")
            y_fcs.append(y_dve)
        y_fcs_prev = y_fcs
        wait("sync", "dve", y_dve)
        dma("sync", yT[:, qb * 512:(qb + 1) * 512]
            .rearrange("(c p) t -> p c t", p=128),
            xqs.rearrange("p (c t) -> p c t", t=512), "sout")

    if dbg:
        wait("sync", "dve", ctr["dve"])
        wait("sync", "act", ctr["act"])
        dma("sync", dbgW2[:, 0:NHC], a1c[:, 0:NHC], "sat")
        dma("sync", dbgW2[:, NHC:2 * NHC], b1s[:, 0:NHC], "sat")
        dma("sync", dbgW2[:, 2 * NHC:2 * NHC + NOC], a2c[:, 0:NOC], "sat")
        dma("sync", dbgW2[:, 2 * NHC + NOC:], b2s[:, 0:NOC], "sat")
    wait("sync", "sout", 16 * NTB)

    with nc.Block() as block:
        @block.tensor
        def _(e):
            for f in steps["pe"]:
                f(e)

        @block.scalar
        def _(e):
            for f in steps["act"]:
                f(e)

        @block.vector
        def _(e):
            for f in steps["dve"]:
                f(e)

        @block.gpsimd
        def _(e):
            for f in steps["pool"]:
                f(e)

        @block.sync
        def _(e):
            for f in steps["sync"]:
                f(e)

    ctx.close()
    return nc


def _stage_inputs(cfg, x, W_q, W_k, W_v, W_o, w1, b1, w2, b2):
    D, LPC, HID = cfg["D"], cfg["LPC"], cfg["HID"]
    NH, NHC, NOC = cfg["NH"], cfg["NHC"], cfg["NOC"]
    f32 = np.float32
    perm = np.concatenate([
        np.concatenate([np.arange(h * 64, (h + 1) * 64, 2),
                        np.arange(h * 64 + 1, (h + 1) * 64, 2)])
        for h in range(NH)])
    com = {
        "wqT": np.ascontiguousarray(np.asarray(W_q, f32).T[:, perm]),
        "wkT": np.ascontiguousarray(np.asarray(W_k, f32).T[:, perm]),
        "wvT": np.ascontiguousarray(np.asarray(W_v, f32).T),
        "woT": np.ascontiguousarray(np.asarray(W_o, f32).T),
        "w1T": np.ascontiguousarray(np.asarray(w1, f32).T),
        "w1n": np.ascontiguousarray(np.asarray(w1, f32)),
        "w2T": np.ascontiguousarray(np.asarray(w2, f32).T),
        "w2n": np.ascontiguousarray(np.asarray(w2, f32)),
        "b1c": np.ascontiguousarray(np.asarray(b1, f32).reshape(NHC, 128).T),
        "b2c": np.ascontiguousarray(np.asarray(b2, f32).reshape(NOC, 128).T),
    }
    Pm = np.zeros((128, 128), f32)
    for r in range(128):
        Pm[r, r ^ 32] = 1.0
    com["Pm"] = Pm
    inv = (BASE ** (-(np.arange(32, dtype=np.float64)) / 32.0))
    xnp = np.asarray(x, f32)
    in_maps = []
    for c in range(8):
        b, q0 = c // 4, (c % 4) * LPC
        pos = (q0 + np.arange(LPC, dtype=np.float64))
        ang = np.outer(inv, pos)
        cc = np.cos(ang).astype(f32)
        ss = np.sin(ang).astype(f32)
        CC = np.tile(cc, (4, 1))
        SS = np.concatenate([-ss, ss, -ss, ss], axis=0)
        m = dict(com)
        m["CC"] = np.ascontiguousarray(CC)
        m["SS"] = np.ascontiguousarray(SS)
        m["xT"] = np.ascontiguousarray(xnp[b, q0:q0 + LPC, :].T)
        in_maps.append(m)
    return in_maps


def kernel(L0_attn_x, L0_attn_W_q, L0_attn_W_k, L0_attn_W_v, L0_attn_W_o,
           L0_ffn_w1, L0_ffn_b1, L0_ffn_w2, L0_ffn_b2):
    cfg = make_cfg()
    in_maps = _stage_inputs(cfg, L0_attn_x, L0_attn_W_q, L0_attn_W_k, L0_attn_W_v,
                            L0_attn_W_o, L0_ffn_w1, L0_ffn_b1, L0_ffn_w2, L0_ffn_b2)
    nc = build_program(cfg, detect_races=False)
    res = run_bass_kernel_spmd(nc, in_maps, core_ids=list(range(8)))
    LPC = cfg["LPC"]
    out = np.empty((2, 4 * LPC, cfg["D"]), np.float32)
    for c in range(8):
        b, q0 = c // 4, (c % 4) * LPC
        out[b, q0:q0 + LPC, :] = res.results[c]["yT"].T
    return out
